# version 05
# baseline (speedup 1.0000x reference)
import numpy as np
import ml_dtypes
from contextlib import ExitStack

import concourse.bass as bass
import concourse.mybir as mybir
from concourse.bass_utils import run_bass_kernel_spmd

F32 = mybir.dt.float32
BF16 = mybir.dt.bfloat16
U8 = mybir.dt.uint8
ALU = mybir.AluOpType
ACTF = mybir.ActivationFunctionType
BF = ml_dtypes.bfloat16
EngT = mybir.EngineType

B, S, D = 2, 4096, 2048
NCC = 8
TOK = 1024
NCH = 16
H1, DH1, BS1 = 32, 64, 256
HID1 = 5632
OCH1 = HID1 // 128
SCALE0 = float(1.0 / np.sqrt(128.0))

_TRUNC = 0


def _decay():
    base = 1.0 / 2.0 ** (8.0 / H1)
    rate = base ** (np.arange(H1, dtype=np.float64) + 1.0)
    factor = 1.0 - 0.0 / (32 - 1 + 1e-5) + 1e-5
    slope = rate * factor
    r = np.arange(BS1, dtype=np.float64) + 1.0
    qd = np.exp(-slope[:, None] * r[None, :])
    kd = np.exp(-slope[:, None] * (BS1 - r[None, :]))
    ddm = r[:, None] - r[None, :]
    dd = np.where(ddm >= 0, np.exp(-slope[:, None, None] * ddm[None]), 0.0)
    bd = np.exp(-slope * BS1)
    return qd, kd, dd, bd


_QD, _KD, _DD, _BD = _decay()


class Sem:
    def __init__(self, h):
        self.h = h
        self.n = 0

    def inc(self, ins, k=1):
        ins.then_inc(self.h, k)
        self.n += k
        return self.n


class KB:

    def __init__(self):
        nc = bass.Bass("TRN2", target_bir_lowering=False,
                       detect_race_conditions=False)
        self.nc = nc
        self.core = list(range(NCC))
        ein = lambda n, s, dt=F32: nc.dram_tensor(n, s, dt, kind="ExternalInput")
        self.inp = {}
        for n, s, dt in [
            ("x_fm", [D, S], F32), ("wq_own", [D, 512], F32),
            ("bq_pp", [128, 4], F32), ("wk_own", [D, 128], F32),
            ("bk_pp", [128, 1], F32), ("wv_own", [D, 128], F32),
            ("bv_row", [1, 128], F32), ("wo_w", [D, D], F32),
            ("bo_pp", [128, 16], F32), ("masks", [4, 128, 512], F32),
            ("ing_pp", [128, 16], F32), ("inb_pp", [128, 16], F32),
            ("agg_w", [D, D], F32), ("png2", [2, D], F32),
            ("pnb2", [2, D], F32), ("d2b2", [2, D], F32),
            ("aggb2", [2, D], F32), ("bcb2", [2, D], F32),
            ("d1s", [D, 1024], F32),
            ("d1b_own", [1, 1024], F32), ("d2s", [1024, D], F32),
            ("bc_w", [D, D], F32), ("gw_l0", [D, D], F32),
            ("gb_pp", [128, 16], F32), ("outg_row", [1, D], F32),
            ("outb_row", [1, D], F32), ("own_mask", [1, 1], F32),
            ("qwT", [D, D], F32), ("kwT", [D, D], F32), ("vwT", [D, D], F32),
            ("gwT1", [D, D], F32), ("owT1", [D, D], F32),
            ("nw_row", [1, D], F32), ("nw_pp", [128, 16], F32),
            ("qd_rows", [1, H1 * BS1], F32),
            ("kd_rows", [1, H1 * BS1], F32), ("ddT", [H1, BS1, BS1], F32),
            ("coeff_rows", [1, 3 * D], F32),
            ("wg_u8", [D, HID1], U8), ("wu_u8", [D, HID1], U8),
            ("wd_u8", [HID1, D], U8), ("sg_pp", [128, OCH1], F32),
            ("su_pp", [128, OCH1], F32), ("sd_pp", [128, 16], F32),
        ]:
            self.inp[n] = ein(n, s, dt)
        self.y = nc.dram_tensor("y", [D, TOK], F32, kind="ExternalOutput")
        dr = lambda n, s, dt=F32, sp="Local": nc.dram_tensor(n, s, dt, addr_space=sp)
        self.ag_att_in = dr("ag_att_in", [512, S], BF16)
        self.ag_att_out = dr("ag_att_out", [NCC, 512, S], BF16, "Shared")
        self.ar_seed_in = dr("ar_seed_in", [2, D])
        self.ar_seed_out = dr("ar_seed_out", [2, D], F32, "Shared")
        self.ar_th_in = dr("ar_th_in", [2, D])
        self.ar_th_out = dr("ar_th_out", [2, D], F32, "Shared")
        self.ag_c_in = dr("ag_c_in", [64, H1 * DH1])
        self.ag_c_out = dr("ag_c_out", [NCC, 64, H1 * DH1], F32, "Shared")
        self.bc_dram = dr("bc_dram", [2, D])
        self.xr1 = dr("xr1", [D, TOK])
        self.xr2 = dr("xr2", [D, TOK])
        self.xr3 = dr("xr3", [D, TOK])
        self.dbg = nc.dram_tensor("dbg", [128, NCH * TOK], BF16, kind="ExternalOutput")
        self.dbg2 = nc.dram_tensor("dbg2", [1, 64], F32, kind="ExternalOutput")
        self.il = {
            "onesbfr": nc.inline_tensor(np.ones((1, 128), dtype=BF), name="c_obfr"),
            "onesfr": nc.inline_tensor(np.ones((1, 128), dtype=np.float32), name="c_ofr"),
            "onesbfc": nc.inline_tensor(np.ones((128, 1), dtype=BF), name="c_obfc"),
            "onesfc": nc.inline_tensor(np.ones((128, 1), dtype=np.float32), name="c_ofc"),
            "n128": nc.inline_tensor(np.full((1, 128), -128.0, dtype=np.float32), name="c_n128"),
            "ident": nc.inline_tensor(np.eye(128, dtype=BF), name="c_id"),
        }

    def on(self, e):
        return getattr(self.blk, e)

    def barrier(self, engines=None):
        engines = ("tensor", "scalar", "vector", "gpsimd", "sync")
        if not hasattr(self, "_brt_v"):
            return self._barrier0(engines)
        snap = {}
        P = self.P
        for e in engines:
            def emit(eng, e=e):
                if e == "gpsimd":
                    eng.wait_ge(self.GD.h, self.GD.n)
                    for k in (0, 1, 4):
                        eng.wait_ge(self.DS[k].h, self.DS[k].n)
                    ins = eng.nop()
                elif e == "sync":
                    eng.wait_ge(self.LD.h, self.LD.n)
                    for k in (2, 3, 5):
                        eng.wait_ge(self.DS[k].h, self.DS[k].n)
                    ins = eng.nop()
                elif e == "tensor":
                    eng.wait_ge(self.LD.h, self._const_ld)
                    ins = eng.ldweights(P["onesbfr"][0:1, 0:1])
                elif e == "scalar":
                    eng.wait_ge(self.AS.h, self._brt_a)
                    ins = eng.copy(P["brt"][0:1, 1:2], P["brt"][0:1, 1:2])
                else:
                    eng.wait_ge(self.VS.h, self._brt_v)
                    ins = eng.tensor_copy(P["brt"][0:1, 2:3], P["brt"][0:1, 2:3])
                self.BARS[e].inc(ins)
            self.on(e)(emit)
            snap[e] = self.BARS[e].n
        for e in engines:
            def emit2(eng, e=e):
                for o in engines:
                    if o != e:
                        eng.wait_ge(self.BARS[o].h, snap[o])
            self.on(e)(emit2)

    def _barrier0(self, engines):
        snap = {}
        for e in engines:
            def emit(eng, e=e):
                if e == "gpsimd":
                    eng.wait_ge(self.GD.h, self.GD.n)
                if e == "sync":
                    eng.wait_ge(self.LD.h, self.LD.n)
                self.BARS[e].inc(eng.nop())
            self.on(e)(emit)
            snap[e] = self.BARS[e].n
        for e in engines:
            def emit2(eng, e=e):
                for o in engines:
                    if o != e:
                        eng.wait_ge(self.BARS[o].h, snap[o])
            self.on(e)(emit2)

    def build(self):
        nc = self.nc
        es = ExitStack()
        with es:
            self.es = es
            for nm in ["LD", "GD", "CC", "TS", "AS", "VS",
                       "BT", "BA", "BV", "BG", "BS"]:
                setattr(self, nm if len(nm) == 2 else nm, Sem(es.enter_context(nc.semaphore(nm))))
            self.DS = [Sem(es.enter_context(nc.semaphore(f"DS{i}"))) for i in range(6)]
            self.CCs = self.CC
            self.BARS = {"tensor": self.BT, "scalar": self.BA, "vector": self.BV,
                         "gpsimd": self.BG, "sync": self.BS}
            sb = lambda n, s, d: es.enter_context(nc.sbuf_tensor(n, s, d))
            self.sb = sb

            P = {}
            for n, s, d in [
                ("onesbfr", [1, 128], BF16), ("onesfr", [1, 128], F32),
                ("onesbfc", [128, 1], BF16), ("onesfc", [128, 1], F32),
                ("n128", [1, 128], F32), ("ident", [128, 128], BF16),
                ("bq", [128, 4], F32), ("bk", [128, 1], F32),
                ("bv", [1, 128], F32), ("bo", [128, 16], F32),
                ("ing", [128, 16], F32), ("inb", [128, 16], F32),
                ("gb", [128, 16], F32),
                ("sg", [128, OCH1], F32), ("su", [128, OCH1], F32),
                ("sd", [128, 16], F32),
                ("r_sum", [1, TOK], F32), ("r_sq", [1, TOK], F32),
                ("r_A", [1, TOK], F32), ("r_B", [1, TOK], F32),
                ("r_t0", [1, TOK], F32), ("xsum", [1, TOK], F32),
                ("ones512", [1, 512], F32),
                ("bcpp", [128, 16], F32), ("epsc", [128, 2], F32),
                ("om", [1, 1], F32),
                ("x16", [128, 16], F32), ("x16b", [128, 16], BF16),
                ("brt", [1, 8], F32),
            ]:
                P[n] = sb("P_" + n, s, d)
            self.P = P

            self.blk = es.enter_context(nc.Block())

            def loads(sync):
                pairs = [
                    (P["onesbfr"][:], self.il["onesbfr"].ap()),
                    (P["onesfr"][:], self.il["onesfr"].ap()),
                    (P["onesbfc"][:], self.il["onesbfc"].ap()),
                    (P["onesfc"][:], self.il["onesfc"].ap()),
                    (P["n128"][:], self.il["n128"].ap()),
                    (P["ident"][:], self.il["ident"].ap()),
                    (P["bq"][:], self.inp["bq_pp"].ap()),
                    (P["bk"][:], self.inp["bk_pp"].ap()),
                    (P["bv"][:], self.inp["bv_row"].ap()),
                    (P["bo"][:], self.inp["bo_pp"].ap()),
                    (P["ing"][:], self.inp["ing_pp"].ap()),
                    (P["inb"][:], self.inp["inb_pp"].ap()),
                    (P["gb"][:], self.inp["gb_pp"].ap()),
                    (P["sg"][:], self.inp["sg_pp"].ap()),
                    (P["su"][:], self.inp["su_pp"].ap()),
                    (P["sd"][:], self.inp["sd_pp"].ap()),
                    (P["om"][:], self.inp["own_mask"].ap()),
                ]
                for dst, src_ in pairs:
                    self.LD.inc(sync.dma_start(out=dst, in_=src_), 16)
            self.on("sync")(loads)

            def fill1(v):
                v.memset(P["ones512"][:], 1.0)
                v.memset(P["epsc"][:, 0:1], 1e-5)
                v.memset(P["epsc"][:, 1:2], 1e-6)
                self.VS.inc(v.memset(P["brt"][0:1, 2:3], 0.0))
            self.on("vector")(fill1)
            self._brt_v = self.VS.n

            def fill2(a):
                self.AS.inc(a.memzero(P["brt"][0:1, 1:2]))
            self.on("scalar")(fill2)
            self._brt_a = self.AS.n
            self._const_ld = self.LD.n
            self.barrier()

            if _TRUNC == 10:
                self.emit_out_from_dram(self.xr1)
                return nc
            self.phase_a()
            if _TRUNC in (12, 13, 14):
                self.emit_out_from_dram(self.xr1)
                return nc
            with ExitStack() as mid:
                M = {}
                for n, s, d in [
                    ("th", [2, D], F32), ("tht", [2, D], F32),
                    ("thb", [2, D], BF16), ("arld", [2, D], F32),
                    ("rowtmp", [1, D], F32), ("yth", [2, 1024], F32),
                    ("ythb", [2, 1024], BF16), ("thfm", [128, 2 * NCH], BF16),
                    ("ythfm", [128, 8 * 2], BF16), ("d1b", [1, 1024], F32),
                ]:
                    M[n] = mid.enter_context(nc.sbuf_tensor("M_" + n, s, d))
                self.M = M

                def loadmid(sync):
                    for dst, src_ in [
                        (M["d1b"][:], self.inp["d1b_own"].ap()),
                    ]:
                        self.LD.inc(sync.dma_start(out=dst, in_=src_), 16)
                self.on("sync")(loadmid)
                self.barrier(["sync", "vector", "tensor", "scalar", "gpsimd"])
                self.phase_b()
                if _TRUNC in (1, 11, 15, 16, 17):
                    self.emit_out_from_dram(self.xr1)
                    return nc
                self.phase_c1()
            if _TRUNC == 21:
                self.emit_out_from_dram(self.xr1)
                return nc
            self.phase_c2()
            if _TRUNC == 2:
                self.emit_out_from_dram(self.xr2)
                return nc
            self.phase_d()
            if _TRUNC in (31, 32, 33, 41, 42):
                self.emit_out_from_dram(self.xr2)
                if _TRUNC == 31:
                    self.emit_dbg_from_dram(self.xr1)
                return nc
            if _TRUNC == 3:
                self.emit_out_from_dram(self.xr3)
                return nc
            self.phase_e()
        return nc

    def emit_out_from_dram(self, src):
        nc = self.nc
        with nc.sbuf_tensor(f"outld{self.uid()}", [128, NCH * TOK // 2], F32) as t0:
            def go(sync):
                for hf in range(2):
                    self.LD.inc(sync.dma_start(
                        out=t0[:].rearrange("p (c t) -> p c t", c=NCH),
                        in_=bass.AP(src, hf * 512,
                                    [[TOK, 128], [128 * TOK, NCH], [1, 512]])), 16)
                    sync.wait_ge(self.LD.h, self.LD.n)
                    self.LD.inc(sync.dma_start(
                        out=bass.AP(self.y, hf * 512,
                                    [[TOK, 128], [128 * TOK, NCH], [1, 512]]),
                        in_=t0[:].rearrange("p (c t) -> p c t", c=NCH)), 16)
                    sync.wait_ge(self.LD.h, self.LD.n)
            self.on("sync")(go)
            self.barrier()

    def emit_dbg_from_dram(self, src):
        nc = self.nc
        with nc.sbuf_tensor(f"dbgld{self.uid()}", [128, NCH * TOK // 2], F32) as t0:
            for hf in range(2):
                def go(sync, hf=hf):
                    self.LD.inc(sync.dma_start(
                        out=t0[:].rearrange("p (c t) -> p c t", c=NCH // 2),
                        in_=bass.AP(src, hf * (NCH // 2) * 128 * TOK,
                                    [[TOK, 128], [128 * TOK, NCH // 2], [1, TOK]])), 16)

                def go2(gp, hf=hf):
                    self.GD.inc(gp.dma_start(
                        out=self.dbg.ap()[:, hf * 8192:(hf + 1) * 8192],
                        in_=t0[:]), 16)
                self.on("sync")(go)
                self.barrier()
                self.on("gpsimd")(go2)
                self.barrier()

    def emit_resid_add(self, ph, x_sb, dram, T=TOK, col_off=0):
        nc, P = self.nc, self.P
        LD, VS = self.LD, self.VS
        rt = [ph.enter_context(nc.sbuf_tensor(f"rres{self.uid()}_{i}", [128, 512], F32))
              for i in range(2)]
        vs0 = VS.n
        rs_ = [self.DS[2], self.DS[3]]
        rsb = [rs_[0].n, rs_[1].n]
        steps = [(dc, tb) for dc in range(NCH) for tb in range(T // 512)]

        def r_sync(sync):
            for i, (dc, tb) in enumerate(steps):
                if i >= 2:
                    sync.wait_ge(VS.h, vs0 + i - 1)
                rs_[i % 2].inc(sync.dma_start(
                    out=rt[i % 2][:],
                    in_=bass.AP(dram, dc * 128 * TOK + col_off + tb * 512,
                                [[TOK, 128], [1, 512]])), 16)

        def r_v(v):
            for i, (dc, tb) in enumerate(steps):
                v.wait_ge(rs_[i % 2].h, rsb[i % 2] + 16 * (i // 2 + 1))
                dst = x_sb[:, dc * T + tb * 512: dc * T + tb * 512 + 512]
                VS.inc(v.tensor_tensor(dst, dst, rt[i % 2][:], ALU.add))
        self.on("sync")(r_sync)
        self.on("vector")(r_v)
        self.barrier(["vector", "sync", "tensor", "scalar"])

    _uid = [0]

    def uid(self):
        KB._uid[0] += 1
        return KB._uid[0]

    def emit_ln(self, ph, x_sb, T, g_row, b_row, mirror_bf=None, spill=None,
                eps=1e-5, out_dram=None):
        nc, P = self.nc, self.P
        TS, AS, VS, LD = self.TS, self.AS, self.VS, self.LD
        ntb = T // 512
        u = self.uid()
        with nc.psum_tensor(f"ln_st{u}", [128, 2 * 512], F32) as pst, \
             nc.psum_tensor(f"ln_ab{u}", [128, 4 * 512], F32) as pab, \
             nc.sbuf_tensor(f"ln_sq0{u}", [128, 512], BF16) as sq0, \
             nc.sbuf_tensor(f"ln_sq1{u}", [128, 512], BF16) as sq1:
            sq = [sq0, sq1]
            ts0, as0, vs0 = TS.n, AS.n, VS.n

            def xt(dc, tb):
                return x_sb[:, dc * T + tb * 512: dc * T + tb * 512 + 512]

            def l_a(a):
                for tb in range(ntb):
                    for dc in range(NCH):
                        i = tb * NCH + dc
                        if i >= 2:
                            a.wait_ge(TS.h, ts0 + i - 1)
                        AS.inc(a.activation(sq[i % 2][:], xt(dc, tb), ACTF.Square))

            def l_t(t):
                for tb in range(ntb):
                    if tb >= 1:
                        t.wait_ge(VS.h, vs0 + tb)
                    for dc in range(NCH):
                        i = tb * NCH + dc
                        t.matmul(pst[0:1, 0:512], P["onesfc"][:], xt(dc, tb),
                                 start=(dc == 0), stop=(dc == NCH - 1))
                        t.wait_ge(AS.h, as0 + i + 1)
                        ins = t.matmul(pst[0:1, 512:1024], P["onesbfc"][:],
                                       sq[i % 2][:], start=(dc == 0),
                                       stop=(dc == NCH - 1))
                        TS.inc(ins)
            self.on("scalar")(l_a)
            self.on("tensor")(l_t)
            TS.n = ts0 + ntb * NCH
            AS.n = as0 + ntb * NCH

            def l_v(v):
                for tb in range(ntb):
                    v.wait_ge(TS.h, ts0 + (tb + 1) * NCH)
                    sl = slice(tb * 512, tb * 512 + 512)
                    v.tensor_scalar_mul(P["r_sum"][0:1, sl], pst[0:1, 0:512], 1.0 / D)
                    v.tensor_scalar_mul(P["r_sq"][0:1, sl], pst[0:1, 512:1024], 1.0 / D)
                    v.tensor_tensor(P["r_t0"][0:1, sl], P["r_sum"][0:1, sl],
                                    P["r_sum"][0:1, sl], ALU.mult)
                    VS.inc(v.tensor_tensor(P["r_t0"][0:1, sl], P["r_sq"][0:1, sl],
                                           P["r_t0"][0:1, sl], ALU.subtract))
            self.on("vector")(l_v)
            vs1 = vs0 + ntb
            VS.n = vs1

            eci = 0 if eps == 1e-5 else 1

            def l_a2(a):
                for tb in range(ntb):
                    a.wait_ge(VS.h, vs0 + tb + 1)
                    sl = slice(tb * 512, tb * 512 + 512)
                    AS.inc(a.activation(P["r_t0"][0:1, sl], P["r_t0"][0:1, sl],
                                        ACTF.Sqrt, bias=P["epsc"][0:1, eci:eci + 1]))
            self.on("scalar")(l_a2)
            asn = AS.n

            def l_v2(v):
                for tb in range(ntb):
                    v.wait_ge(AS.h, asn - ntb + tb + 1)
                    sl = slice(tb * 512, tb * 512 + 512)
                    v.reciprocal(P["r_A"][0:1, sl], P["r_t0"][0:1, sl])
                    v.tensor_tensor(P["r_B"][0:1, sl], P["r_sum"][0:1, sl],
                                    P["r_A"][0:1, sl], ALU.mult)
                    VS.inc(v.tensor_scalar_mul(P["r_B"][0:1, sl],
                                               P["r_B"][0:1, sl], -1.0))
            self.on("vector")(l_v2)
            vs2 = VS.n

            steps = [(dc, tb) for tb in range(ntb) for dc in range(NCH)]
            ts_ap0, vs_ap0 = TS.n, VS.n
            k_per = 2 if mirror_bf is not None else 1

            def l_t2(t):
                for i, (dc, tb) in enumerate(steps):
                    t.wait_ge(VS.h, vs1 + tb + 1)
                    if i >= 2:
                        t.wait_ge(VS.h, vs_ap0 + (i - 1) * k_per)
                    sl = slice(tb * 512, tb * 512 + 512)
                    s = (i % 2) * 1024
                    if g_row is None:
                        t.matmul(pab[:, s:s + 512], P["onesfr"][:],
                                 P["r_A"][0:1, sl], start=True, stop=True)
                        ins = t.matmul(pab[:, s + 512:s + 1024], P["onesfr"][:],
                                       P["r_B"][0:1, sl], start=True, stop=True)
                    else:
                        t.matmul(pab[:, s:s + 512], g_row[0:1, dc * 128:(dc + 1) * 128],
                                 P["r_A"][0:1, sl], start=True, stop=True)
                        t.matmul(pab[:, s + 512:s + 1024],
                                 g_row[0:1, dc * 128:(dc + 1) * 128],
                                 P["r_B"][0:1, sl], start=True, stop=False)
                        ins = t.matmul(pab[:, s + 512:s + 1024],
                                       b_row[0:1, dc * 128:(dc + 1) * 128],
                                       P["ones512"][0:1, :], start=False, stop=True)
                    TS.inc(ins)

            def l_v3(v):
                for i, (dc, tb) in enumerate(steps):
                    v.wait_ge(TS.h, ts_ap0 + i + 1)
                    s = (i % 2) * 1024
                    x_ = xt(dc, tb)
                    v.tensor_tensor(x_, x_, pab[:, s:s + 512], ALU.mult)
                    ins = v.tensor_tensor(x_, x_, pab[:, s + 512:s + 1024], ALU.add)
                    if mirror_bf is not None:
                        VS.inc(ins)
                        mt = mirror_bf[:, dc * T + tb * 512: dc * T + tb * 512 + 512]
                        ins = v.tensor_copy(mt, x_)
                    VS.inc(ins)
            self.on("tensor")(l_t2)
            self.on("vector")(l_v3)
            TS.n = ts_ap0 + len(steps)
            VS.n = vs_ap0 + len(steps) * k_per
            self.barrier()
            if spill is not None:
                def l_sp(sync):
                    LD.inc(sync.dma_start(
                        out=bass.AP(spill, 0, [[T, 128], [128 * T, NCH], [1, T]]),
                        in_=x_sb[:, 0:NCH * T].rearrange("p (c t) -> p c t", c=NCH)), 16)
                self.on("sync")(l_sp)

    def proj_fm(self, tag, w_dram, w_rows, w_cols, noc, rhs_fn, ntb, tbw,
                act_epi, vec_epi=None, corr_row=None):
        nc, P = self.nc, self.P
        TS, AS, VS, GD = self.TS, self.AS, self.VS, self.GD
        ndc = w_rows // 128
        oc_pass = max(1, 8 // ntb) if tbw == 512 else 8
        passes = [(i, min(oc_pass, noc - i)) for i in range(0, noc, oc_pass)]
        with nc.psum_tensor(f"{tag}_ps", [128, 8 * 512], F32) as psum, \
             nc.sbuf_tensor(f"{tag}_w0", [128, oc_pass * 128], BF16) as w0, \
             nc.sbuf_tensor(f"{tag}_w1", [128, oc_pass * 128], BF16) as w1:
            wslab = [w0, w1]
            ws = [self.DS[0], self.DS[1]]
            wsb = [ws[0].n, ws[1].n]
            ts0, as0, vs0 = TS.n, AS.n, VS.n
            epp = [ocn * ntb for (_, ocn) in passes]
            mpp = ndc + (1 if corr_row is not None else 0)

            def markval(i):
                pi, dc = i // ndc, i % ndc
                return ts0 + pi * mpp + dc + 1

            def p_dma(gp):
                i = 0
                for pi, (oc0, ocn) in enumerate(passes):
                    for dc in range(ndc):
                        if i >= 2:
                            gp.wait_ge(TS.h, markval(i - 2))
                        src = bass.AP(w_dram, dc * 128 * w_cols + oc0 * 128,
                                      [[w_cols, 128], [1, ocn * 128]])
                        ws[i % 2].inc(gp.dma_start(out=wslab[i % 2][:, 0:ocn * 128],
                                                   in_=src), 16)
                        i += 1

            def p_t(t):
                i = 0
                for pi, (oc0, ocn) in enumerate(passes):
                    if pi >= 1:
                        t.wait_ge(AS.h, as0 + sum(epp[:pi]))
                        if vec_epi is not None:
                            t.wait_ge(VS.h, vs0 + sum(epp[:pi]))
                    for dc in range(ndc):
                        t.wait_ge(ws[i % 2].h, wsb[i % 2] + 16 * (i // 2 + 1))
                        ins = None
                        for tb in range(ntb):
                            rhs = rhs_fn(dc, tb)
                            for oci in range(ocn):
                                ins = t.matmul(psum[:, (oci * ntb + tb) * tbw:(oci * ntb + tb + 1) * tbw],
                                               wslab[i % 2][:, oci * 128:(oci + 1) * 128],
                                               rhs, start=(dc == 0),
                                               stop=(dc == ndc - 1 and corr_row is None))
                        TS.inc(ins)
                        i += 1
                    if corr_row is not None:
                        ins = None
                        for tb in range(ntb):
                            cr = corr_row(tb)
                            for oci in range(ocn):
                                ins = t.matmul(psum[:, (oci * ntb + tb) * tbw:(oci * ntb + tb + 1) * tbw],
                                               P["n128"][:], cr, start=False, stop=True)
                        TS.inc(ins)

            def p_a(a):
                k = 0
                for pi, (oc0, ocn) in enumerate(passes):
                    a.wait_ge(TS.h, ts0 + (pi + 1) * mpp)
                    for oci in range(ocn):
                        for tb in range(ntb):
                            if vec_epi is not None and k >= 2:
                                a.wait_ge(VS.h, vs0 + k - 1)
                            psl = psum[:, (oci * ntb + tb) * tbw:(oci * ntb + tb + 1) * tbw]
                            AS.inc(act_epi(a, oc0 + oci, tb, psl))
                            k += 1
            self.on("gpsimd")(p_dma)
            self.on("tensor")(p_t)
            self.on("scalar")(p_a)
            TS.n = ts0 + len(passes) * mpp
            AS.n = as0 + sum(epp)
            if vec_epi is not None:
                def p_v(v):
                    k = 0
                    for pi, (oc0, ocn) in enumerate(passes):
                        for oci in range(ocn):
                            for tb in range(ntb):
                                k += 1
                                v.wait_ge(AS.h, as0 + k)
                                VS.inc(vec_epi(v, oc0 + oci, tb, k - 1))
                self.on("vector")(p_v)
                VS.n = vs0 + sum(epp)
            self.barrier(["tensor", "scalar", "vector", "gpsimd"])

    def phase_a(self):
        nc, P = self.nc, self.P
        TS, AS, VS, GD, LD = self.TS, self.AS, self.VS, self.GD, self.LD
        with ExitStack() as ph:
            sbA = lambda n, s, d: ph.enter_context(nc.sbuf_tensor(n, s, d))
            xblk = [sbA(f"xblk{i}", [128, NCH * 512], BF16) for i in range(2)]
            wq_sb = sbA("wq_sb", [128, NCH * 512], BF16)
            wk_sb = sbA("wk_sb", [128, NCH * 128], BF16)
            wv_sb = sbA("wv_sb", [128, NCH * 128], BF16)
            k_fm = sbA("k_fm", [128, S], BF16)
            v_tm = sbA("v_tm", [128, 32 * 128], BF16)
            q_fmb = sbA("q_fmb", [128, 4 * S], BF16)
            attnout = sbA("attnout", [128, 4 * S], BF16)
            P_sb = [sbA(f"Psb{i}", [128, 1024], BF16) for i in range(2)]
            rrow = sbA("rrow", [1, 1024], F32)
            masks_l = sbA("masks_l", [128, 4 * 512], BF16)
            rbc_a = sbA("rbc_a", [128, 1024], F32)

            def a_mload(gp):
                GD.inc(gp.dma_start(
                    out=masks_l[:].rearrange("p (m t) -> p m t", m=4),
                    in_=self.inp["masks"].ap().rearrange("m p t -> p m t")), 16)
            self.on("gpsimd")(a_mload)

            def a_wload(gp):
                GD.inc(gp.dma_start(out=wq_sb[:].rearrange("p (c t) -> p c t", c=NCH),
                                    in_=bass.AP(self.inp["wq_own"], 0,
                                                [[512, 128], [128 * 512, NCH], [1, 512]])), 16)
                GD.inc(gp.dma_start(out=wk_sb[:].rearrange("p (c t) -> p c t", c=NCH),
                                    in_=bass.AP(self.inp["wk_own"], 0,
                                                [[128, 128], [128 * 128, NCH], [1, 128]])), 16)
                GD.inc(gp.dma_start(out=wv_sb[:].rearrange("p (c t) -> p c t", c=NCH),
                                    in_=bass.AP(self.inp["wv_own"], 0,
                                                [[128, 128], [128 * 128, NCH], [1, 128]])), 16)
            self.on("gpsimd")(a_wload)
            self.barrier(["gpsimd", "tensor", "scalar", "vector", "sync"])

            with nc.psum_tensor("pqkv", [128, 4 * 512], F32) as pqkv, \
                 nc.psum_tensor("pk1", [128, 512], F32) as pk1, \
                 nc.psum_tensor("pv", [128, 2 * 512], F32) as pv:
                ts0, as0 = TS.n, AS.n
                xs_ = [self.DS[0], self.DS[1]]
                xsb = [xs_[0].n, xs_[1].n]

                def a1_dma(gp):
                    for i in range(8):
                        if i >= 2:
                            gp.wait_ge(TS.h, ts0 + (i - 1) * 6)
                        src = bass.AP(self.inp["x_fm"], i * 512,
                                      [[S, 128], [128 * S, NCH], [1, 512]])
                        xs_[i % 2].inc(gp.dma_start(
                            out=xblk[i % 2][:].rearrange("p (c t) -> p c t", c=NCH),
                            in_=src), 16)

                def a1_t(t):
                    for i in range(8):
                        t.wait_ge(xs_[i % 2].h, xsb[i % 2] + 16 * (i // 2 + 1))
                        if i >= 1:
                            t.wait_ge(AS.h, as0 + 6 * i)
                        xb_ = xblk[i % 2]
                        ins = None
                        for dc in range(NCH):
                            ins = t.matmul(pk1[:, :], wk_sb[:, dc * 128:(dc + 1) * 128],
                                           xb_[:, dc * 512:(dc + 1) * 512],
                                           start=(dc == 0), stop=(dc == NCH - 1))
                        TS.inc(ins)
                        for tt in range(4):
                            if tt >= 2:
                                t.wait_ge(AS.h, as0 + 6 * i + tt)
                            for dc in range(NCH):
                                t.matmul(pv[:, (tt % 2) * 512:(tt % 2) * 512 + 128],
                                         xb_[:, dc * 512 + tt * 128: dc * 512 + tt * 128 + 128],
                                         wv_sb[:, dc * 128:(dc + 1) * 128],
                                         start=(dc == 0), stop=False)
                            ins = t.matmul(pv[:, (tt % 2) * 512:(tt % 2) * 512 + 128],
                                           P["onesfr"][:], P["bv"][:], start=False, stop=True)
                            TS.inc(ins)
                        ins = None
                        for oc in range(4):
                            for dc in range(NCH):
                                ins = t.matmul(pqkv[:, oc * 512:(oc + 1) * 512],
                                               wq_sb[:, dc * 512 + oc * 128: dc * 512 + oc * 128 + 128],
                                               xb_[:, dc * 512:(dc + 1) * 512],
                                               start=(dc == 0), stop=(dc == NCH - 1))
                        TS.inc(ins)

                def a1_a(a):
                    for i in range(8):
                        base = ts0 + i * 6
                        a.wait_ge(TS.h, base + 1)
                        AS.inc(a.add(k_fm[:, i * 512:(i + 1) * 512], pk1[:, :],
                                     P["bk"][:, 0:1]))
                        for tt in range(4):
                            a.wait_ge(TS.h, base + 2 + tt)
                            AS.inc(a.activation(
                                v_tm[:, (i * 4 + tt) * 128:(i * 4 + tt + 1) * 128],
                                pv[:, (tt % 2) * 512:(tt % 2) * 512 + 128],
                                ACTF.Copy))
                        a.wait_ge(TS.h, base + 6)
                        ins = None
                        for oc in range(4):
                            ins = a.add(
                                q_fmb[:, oc * S + i * 512: oc * S + (i + 1) * 512],
                                pqkv[:, oc * 512:(oc + 1) * 512],
                                P["bq"][:, oc:oc + 1])
                        AS.inc(ins)
                self.on("gpsimd")(a1_dma)
                self.on("tensor")(a1_t)
                self.on("scalar")(a1_a)
                TS.n = ts0 + 48
                AS.n = as0 + 48
                self.barrier()
            if _TRUNC == 12:
                return

            with nc.psum_tensor("sc0", [128, 1024], F32) as sc0, \
                 nc.psum_tensor("sc1", [128, 1024], F32) as sc1, \
                 nc.psum_tensor("avp", [128, 1024], F32) as avp, \
                 nc.psum_tensor("sm0", [128, 512], F32) as sm0, \
                 nc.psum_tensor("sm1", [128, 512], F32) as sm1:
                sc = [sc0, sc1]
                plan = []
                tsn, asn, vsn = TS.n, AS.n, VS.n
                for qb in range(8):
                    for hg in range(2):
                        nkt = 4 * qb + 4
                        plan.append({"qb": qb, "hg": hg, "nkt": nkt,
                                     "ts0": tsn, "as0": asn, "vs0": vsn})
                        tsn += nkt + 2
                        asn += nkt + 1
                        vsn += 6

                def a2_t(t):
                    for st in plan:
                        qb, hg, nkt = st["qb"], st["hg"], st["nkt"]
                        qh0 = 2 * hg
                        ins = None
                        t.wait_ge(VS.h, st["vs0"])
                        for kt in range(nkt + 1):
                            if kt < nkt:
                                if kt >= 2:
                                    t.wait_ge(AS.h, st["as0"] + kt - 1)
                                s = sc[kt % 2]
                                for j in range(2):
                                    qh = qh0 + j
                                    ins2 = t.matmul(
                                        s[:, j * 512:(j + 1) * 512],
                                        k_fm[:, kt * 128:(kt + 1) * 128],
                                        q_fmb[:, qh * S + qb * 512: qh * S + (qb + 1) * 512],
                                        start=True, stop=True)
                                TS.inc(ins2)
                            if kt > 0:
                                kk = kt - 1
                                diag = (kk >= nkt - 4)
                                if diag:
                                    t.wait_ge(VS.h, st["vs0"] + (kk - (nkt - 4)) + 1)
                                else:
                                    t.wait_ge(AS.h, st["as0"] + kk + 1)
                                Pb = P_sb[kk % 2]
                                first, last = (kk == 0), (kk == nkt - 1)
                                t.matmul(sm0[0:1, :], P["onesbfc"][:], Pb[:, 0:512],
                                         start=first, stop=last)
                                t.matmul(sm1[0:1, :], P["onesbfc"][:], Pb[:, 512:1024],
                                         start=first, stop=last)
                                t.matmul(avp[:, 0:512], v_tm[:, kk * 128:(kk + 1) * 128],
                                         Pb[:, 0:512], start=first, stop=last)
                                ins = t.matmul(avp[:, 512:1024],
                                               v_tm[:, kk * 128:(kk + 1) * 128],
                                               Pb[:, 512:1024], start=first, stop=last)
                        TS.inc(ins)
                        t.wait_ge(VS.h, st["vs0"] + 5)
                        t.matmul(sc0[:, 0:512], P["onesfr"][:], rrow[0:1, 0:512],
                                 start=True, stop=True)
                        ins = t.matmul(sc0[:, 512:1024], P["onesfr"][:],
                                       rrow[0:1, 512:1024], start=True, stop=True)
                        TS.inc(ins)

                def a2_a(a):
                    for st in plan:
                        for kt in range(st["nkt"]):
                            a.wait_ge(TS.h, st["ts0"] + kt + 1)
                            AS.inc(a.activation(P_sb[kt % 2][:, :], sc[kt % 2][:, :],
                                                ACTF.Exp, scale=SCALE0))
                        a.wait_ge(TS.h, st["ts0"] + st["nkt"] + 2)
                        a.wait_ge(VS.h, st["vs0"])
                        AS.inc(a.activation(rbc_a[:, :], sc0[:, :], ACTF.Copy))

                def a2_v(v):
                    for st in plan:
                        qb, hg, nkt = st["qb"], st["hg"], st["nkt"]
                        qh0 = 2 * hg
                        for di in range(4):
                            kt = nkt - 4 + di
                            v.wait_ge(AS.h, st["as0"] + kt + 1)
                            Pb = P_sb[kt % 2]
                            mk = masks_l[:, di * 512:(di + 1) * 512]
                            v.tensor_tensor(Pb[:, 0:512], Pb[:, 0:512], mk, ALU.mult)
                            VS.inc(v.tensor_tensor(Pb[:, 512:1024], Pb[:, 512:1024],
                                                   mk, ALU.mult))
                        v.wait_ge(TS.h, st["ts0"] + nkt + 1)
                        v.reciprocal(rrow[0:1, 0:512], sm0[0:1, :])
                        VS.inc(v.reciprocal(rrow[0:1, 512:1024], sm1[0:1, :]))
                        v.wait_ge(AS.h, st["as0"] + nkt + 1)
                        ins = None
                        for j in range(2):
                            qh = qh0 + j
                            ins = v.tensor_tensor(
                                attnout[:, qh * S + qb * 512: qh * S + (qb + 1) * 512],
                                avp[:, j * 512:(j + 1) * 512],
                                rbc_a[:, j * 512:(j + 1) * 512], ALU.mult)
                        VS.inc(ins)
                self.on("tensor")(a2_t)
                self.on("scalar")(a2_a)
                self.on("vector")(a2_v)
                TS.n, AS.n, VS.n = tsn, asn, vsn
                self.barrier()
            if _TRUNC == 13:
                return

            def a3(gp):
                GD.inc(gp.dma_start(
                    out=bass.AP(self.ag_att_in, 0, [[S, 128], [128 * S, 4], [1, S]]),
                    in_=attnout[:].rearrange("p (h t) -> p h t", h=4)), 16)
                gp.wait_ge(GD.h, GD.n)
                self.CCs.inc(gp.collective_compute(
                    "AllGather", ALU.bypass, replica_groups=[self.core],
                    ins=[self.ag_att_in.ap()], outs=[self.ag_att_out.ap()]))
                gp.wait_ge(self.CCs.h, self.CCs.n)
            self.on("gpsimd")(a3)
            self.barrier()

    def phase_b(self):
        nc, P = self.nc, self.P
        TS, AS, VS, GD, LD = self.TS, self.AS, self.VS, self.GD, self.LD
        with ExitStack() as ph:
            sbB = lambda n, s, d: ph.enter_context(nc.sbuf_tensor(n, s, d))
            ao_own = sbB("ao_own", [128, NCH * TOK], BF16)
            x1_sb = sbB("x1_sb", [128, NCH * TOK], F32)
            tmpB = [sbB(f"tmpB{i}", [128, 512], F32) for i in range(2)]

            def b_dma(gp):
                pid = nc.partition_id(engines=[EngT.Pool])
                off = (pid // 4) * (4 * 512 * S) + (pid % 4) * TOK
                GD.inc(gp.dma_start(
                    out=ao_own[:].rearrange("p (c t) -> p c t", c=NCH),
                    in_=bass.AP(self.ag_att_out, off,
                                [[S, 128], [512 * S, 4], [128 * S, 4], [1, TOK]])), 16)

            def b_sync(sync):
                pid2 = nc.partition_id(engines=[EngT.SP])
                offx = (pid2 % 4) * TOK
                LD.inc(sync.dma_start(
                    out=x1_sb[:].rearrange("p (c t) -> p c t", c=NCH),
                    in_=bass.AP(self.inp["x_fm"], offx,
                                [[S, 128], [128 * S, NCH], [1, TOK]])), 16)
            self.on("gpsimd")(b_dma)
            self.on("sync")(b_sync)
            self.barrier()
            if _TRUNC == 15:
                def spdbg15(sync):
                    self.LD.inc(sync.dma_start(out=self.dbg.ap()[:, 0:4096],
                                               in_=ao_own[:, 0:4096]), 16)
                self.on("sync")(spdbg15)
                self.barrier()
                return

            def b_epi_a(a, oc, tb, psl):
                return a.activation(tmpB[(oc * 2 + tb) % 2][:], psl, ACTF.Copy)

            if _TRUNC == 16:
                self.proj_fm("bo", self.inp["wo_w"], D, D, 16,
                             lambda dc, tb: ao_own[:, dc * TOK + tb * 512: dc * TOK + tb * 512 + 512],
                             2, 512, b_epi_a)
                return

            def b_epi_v(v, oc, tb, i):
                dst = x1_sb[:, oc * TOK + tb * 512: oc * TOK + tb * 512 + 512]
                return v.scalar_tensor_tensor(dst, tmpB[i % 2][:],
                                              P["bo"][:, oc:oc + 1], dst,
                                              ALU.add, ALU.add)

            self.proj_fm("bo", self.inp["wo_w"], D, D, 16,
                         lambda dc, tb: ao_own[:, dc * TOK + tb * 512: dc * TOK + tb * 512 + 512],
                         2, 512, b_epi_a, b_epi_v)

            if _TRUNC == 11:
                self.barrier()

                def spdbg(sync):
                    self.LD.inc(sync.dma_start(out=self.dbg.ap()[:, 0:4096],
                                               in_=ao_own[:, 0:4096]), 16)
                self.on("sync")(spdbg)

                def sp11(sync):
                    self.LD.inc(sync.dma_start(
                        out=bass.AP(self.xr1, 0, [[TOK, 128], [128 * TOK, NCH], [1, TOK]]),
                        in_=x1_sb[:].rearrange("p (c t) -> p c t", c=NCH)), 16)
                self.on("sync")(sp11)
                self.barrier()
                return
            self.emit_ln(ph, x1_sb, TOK, None, None, spill=self.xr1)
            self.barrier(["sync", "vector", "tensor", "scalar", "gpsimd"])

            if _TRUNC == 1:
                return

            def s_v0(v):
                lastcol = x1_sb[:].rearrange("p (c t) -> p c t", c=NCH)[:, :, 1023:1024]
                VS.inc(v.tensor_copy(P["x16"][:], lastcol.squeeze(2)))
            self.on("vector")(s_v0)
            self.barrier(["vector", "tensor", "scalar"])

            with nc.psum_tensor("psd", [128, 4 * 512], F32) as psd, \
                 nc.psum_tensor("psd2", [16, 16], F32) as psd2:
                def s_a0(a):
                    AS.inc(a.activation(P["x16b"][:], P["x16"][:], ACTF.Square))
                self.on("scalar")(s_a0)

                def s_t0(t):
                    t.wait_ge(AS.h, AS.n)
                    t.matmul(psd2[0:16, 0:1], P["x16"][:], P["onesfc"][:],
                             start=True, stop=True)
                    TS.inc(t.matmul(psd2[0:16, 1:2], P["x16b"][:], P["onesbfc"][:],
                                    start=True, stop=True))
                self.on("tensor")(s_t0)

                s16 = ph.enter_context(nc.sbuf_tensor("s16", [16, 2], F32))
                mu1 = ph.enter_context(nc.sbuf_tensor("mu1", [1, 4], F32))

                def s_v1(v):
                    v.wait_ge(TS.h, TS.n)
                    VS.inc(v.tensor_copy(s16[:], psd2[0:16, 0:2]))
                self.on("vector")(s_v1)
                self.barrier(["vector", "tensor", "scalar"])

                def s_t1(t):
                    TS.inc(t.matmul(psd2[0:1, 2:4], P["onesfc"][0:16, 0:1], s16[:],
                                    start=True, stop=True))
                self.on("tensor")(s_t1)

                def s_v2(v):
                    v.wait_ge(TS.h, TS.n)
                    VS.inc(v.tensor_scalar_mul(mu1[0:1, 0:2], psd2[0:1, 2:4], 1.0 / D))
                    v.wait_ge(VS.h, VS.n)
                    VS.inc(v.tensor_tensor(mu1[0:1, 2:3], mu1[0:1, 0:1],
                                           mu1[0:1, 0:1], ALU.mult))
                    v.wait_ge(VS.h, VS.n)
                    VS.inc(v.tensor_tensor(mu1[0:1, 2:3], mu1[0:1, 1:2],
                                           mu1[0:1, 2:3], ALU.subtract))
                self.on("vector")(s_v2)

                def s_a1(a):
                    a.wait_ge(VS.h, VS.n)
                    AS.inc(a.activation(mu1[0:1, 2:3], mu1[0:1, 2:3], ACTF.Sqrt,
                                        bias=P["epsc"][0:1, 1:2]))
                self.on("scalar")(s_a1)

                def s_v3(v):
                    v.wait_ge(AS.h, AS.n)
                    VS.inc(v.reciprocal(mu1[0:1, 3:4], mu1[0:1, 2:3]))
                self.on("vector")(s_v3)
                self.barrier(["vector", "tensor", "scalar"])

                psb_s = ph.enter_context(nc.sbuf_tensor("psb_s", [128, 2], F32))
                with nc.psum_tensor("psb", [128, 2], F32) as psb:
                    def s_t2(t):
                        t.matmul(psb[:, 0:1], P["onesfr"][:], mu1[0:1, 0:1],
                                 start=True, stop=True)
                        TS.inc(t.matmul(psb[:, 1:2], P["onesfr"][:], mu1[0:1, 3:4],
                                        start=True, stop=True))
                    self.on("tensor")(s_t2)

                    def s_v4(v):
                        v.wait_ge(TS.h, TS.n)
                        VS.inc(v.tensor_copy(psb_s[:], psb[:, :]))
                        v.wait_ge(VS.h, VS.n)
                        VS.inc(v.tensor_scalar(P["x16"][:], P["x16"][:], psb_s[:, 0:1],
                                               psb_s[:, 1:2], ALU.subtract, ALU.mult))
                        v.wait_ge(VS.h, VS.n)
                        VS.inc(v.tensor_tensor(P["x16"][:], P["x16"][:],
                                               P["ing"][:], ALU.mult))
                        v.wait_ge(VS.h, VS.n)
                        VS.inc(v.tensor_tensor(P["x16"][:], P["x16"][:],
                                               P["inb"][:], ALU.add))
                        v.wait_ge(VS.h, VS.n)
                        VS.inc(v.tensor_copy(P["x16b"][:], P["x16"][:]))
                    self.on("vector")(s_v4)
                    self.barrier(["vector", "tensor", "scalar"])
                    if _TRUNC == 17:
                        def s_v4b(v):
                            VS.inc(v.tensor_copy(P["bcpp"][:], P["x16"][:]))
                        self.on("vector")(s_v4b)
                        self.barrier(["vector", "tensor", "scalar"])

                agsl = [ph.enter_context(nc.sbuf_tensor(f"agsl{i}", [128, 512], BF16))
                        for i in range(2)]
                ts0 = TS.n
                ag_ = [self.DS[0], self.DS[1]]
                agb = [ag_[0].n, ag_[1].n]

                def sg_dma(gp):
                    i = 0
                    for dc in range(NCH):
                        for nb4 in range(4):
                            if i >= 2:
                                gp.wait_ge(TS.h, ts0 + i - 1)
                            src = bass.AP(self.inp["agg_w"], dc * 128 * D + nb4 * 512,
                                          [[D, 128], [1, 512]])
                            ag_[i % 2].inc(gp.dma_start(out=agsl[i % 2][:], in_=src), 16)
                            i += 1
                self.on("gpsimd")(sg_dma)

                def sg_t(t):
                    i = 0
                    for dc in range(NCH):
                        for nb4 in range(4):
                            t.wait_ge(ag_[i % 2].h, agb[i % 2] + 16 * (i // 2 + 1))
                            TS.inc(t.matmul(psd[0:1, nb4 * 512:(nb4 + 1) * 512],
                                            P["x16b"][:, dc:dc + 1], agsl[i % 2][:],
                                            start=(dc == 0), stop=(dc == NCH - 1)))
                            i += 1
                self.on("tensor")(sg_t)
                TS.n = ts0 + 64

                def sg_v(v):
                    v.wait_ge(TS.h, TS.n)
                    v.memset(self.M["arld"][:], 0.0)
                    VS.inc(v.tensor_single_scalar(self.M["rowtmp"][0:1, :], psd[0:1, 0:D],
                                                  P["om"][0:1, 0:1], ALU.mult))
                self.on("vector")(sg_v)
                self.barrier(["vector", "tensor", "scalar", "gpsimd"])

                def sg_g2(gp):
                    pid = nc.partition_id(engines=[EngT.Pool])
                    GD.inc(gp.dma_start(out=self.ar_seed_in.ap(), in_=self.M["arld"][:]), 16)
                    gp.wait_ge(GD.h, GD.n)
                    off = (pid // 4) * D
                    GD.inc(gp.dma_start(out=bass.AP(self.ar_seed_in, off, [[D, 1], [1, D]]),
                                        in_=self.M["rowtmp"][0:1, :]), 16)
                    gp.wait_ge(GD.h, GD.n)
                    self.CCs.inc(gp.collective_compute(
                        "AllReduce", ALU.add, replica_groups=[self.core],
                        ins=[self.ar_seed_in.ap()], outs=[self.ar_seed_out.ap()]))
                    gp.wait_ge(self.CCs.h, self.CCs.n)
                    GD.inc(gp.dma_start(out=self.M["arld"][:], in_=self.ar_seed_out.ap()), 16)
                self.on("gpsimd")(sg_g2)
                self.barrier()
                if _TRUNC == 17:
                    def spdbg17(gp):
                        self.GD.inc(gp.dma_start(out=self.dbg.ap()[0:2, 0:D],
                                                 in_=self.M["arld"][:]), 16)
                        self.GD.inc(gp.dma_start(out=self.dbg.ap()[0:1, 2 * D:3 * D],
                                                 in_=self.M["rowtmp"][0:1, :]), 16)
                        self.GD.inc(gp.dma_start(out=self.dbg.ap()[0:1, 3 * D:3 * D + 16],
                                                 in_=P["x16"][0:1, 0:16]), 16)
                        self.GD.inc(gp.dma_start(out=self.dbg.ap()[0:1, 3 * D + 16:3 * D + 32],
                                                 in_=P["x16b"][0:1, 0:16]), 16)
                        self.GD.inc(gp.dma_start(out=self.dbg.ap()[0:1, 3 * D + 32:3 * D + 36],
                                                 in_=mu1[0:1, 0:4]), 16)
                        self.GD.inc(gp.dma_start(out=self.dbg.ap()[0:1, 3 * D + 64:3 * D + 80],
                                                 in_=P["bcpp"][0:1, 0:16]), 16)
                        self.GD.inc(gp.dma_start(out=self.dbg2.ap()[0:1, 0:16],
                                                 in_=P["x16"][0:1, 0:16]), 16)
                        self.GD.inc(gp.dma_start(out=self.dbg2.ap()[0:1, 16:20],
                                                 in_=mu1[0:1, 0:4]), 16)
                        self.GD.inc(gp.dma_start(out=self.dbg2.ap()[0:1, 20:22],
                                                 in_=psb_s[0:1, 0:2]), 16)
                        self.GD.inc(gp.dma_start(out=self.dbg2.ap()[0:1, 32:48],
                                                 in_=P["bcpp"][0:1, 0:16]), 16)
                    self.on("gpsimd")(spdbg17)
                    self.barrier()
                    return



    def phase_c1(self):
        nc, P = self.nc, self.P
        TS, AS, VS, GD, LD = self.TS, self.AS, self.VS, self.GD, self.LD
        with ExitStack() as ph:
            sbC = lambda n, s, d: ph.enter_context(nc.sbuf_tensor(n, s, d))
            d1_sb = sbC("d1_sb", [128, NCH * 1024], BF16)
            d2_sb = sbC("d2_sb", [128, 8 * D], BF16)
            prs = {}
            for nm in ["png2", "pnb2", "d2b2", "aggb2", "bcb2"]:
                prs[nm] = sbC("c_" + nm, [2, D], F32)
            self.c_prs = prs

            def c_ld0(sync):
                for nm in ["png2", "pnb2", "d2b2", "aggb2", "bcb2"]:
                    LD.inc(sync.dma_start(out=prs[nm][:], in_=self.inp[nm].ap()), 16)
            self.on("sync")(c_ld0)

            def c_init(v):
                VS.inc(v.tensor_tensor(self.M["th"][:], self.M["arld"][:],
                                       prs["aggb2"][:], ALU.add))

            def c_ld(gp):
                GD.inc(gp.dma_start(
                    out=d1_sb[:].rearrange("p (c t) -> p c t", c=NCH),
                    in_=bass.AP(self.inp["d1s"], 0,
                                [[1024, 128], [128 * 1024, NCH], [1, 1024]])), 16)
                GD.inc(gp.dma_start(
                    out=d2_sb[:].rearrange("p (c t) -> p c t", c=8),
                    in_=bass.AP(self.inp["d2s"], 0,
                                [[D, 128], [128 * D, 8], [1, D]])), 16)
            self.on("gpsimd")(c_ld)
            self.barrier()
            self.on("vector")(c_init)
            self.barrier(["vector", "tensor", "scalar"])

            with nc.psum_tensor("pth", [128, 4 * 512], F32) as pth, \
                 nc.psum_tensor("ptt", [128, 2 * 128], BF16) as ptt:
                for step in range(4):
                    self.thought_step(ph, pth, ptt, self.M["th"], d1_sb, d2_sb, step)

                self.rowpair_matmul(ph, pth, ptt, self.M["th"], self.inp["bc_w"],
                                    self.M["arld"], bias_rows=prs["bcb2"],
                                    bias_base=0)
                def cb_g(gp):
                    pid = nc.partition_id(engines=[EngT.Pool])
                    GD.inc(gp.dma_start(out=self.bc_dram.ap(), in_=self.M["arld"][:]), 16)
                    gp.wait_ge(GD.h, GD.n)
                    off = (pid // 4) * D
                    GD.inc(gp.dma_start(out=self.M["rowtmp"][0:1, :],
                                        in_=bass.AP(self.bc_dram, off, [[D, 1], [1, D]])), 16)
                self.on("gpsimd")(cb_g)
                self.barrier()

                def cb_t(t):
                    ins = None
                    for c in range(NCH):
                        ins = t.matmul(pth[:, c:c + 1],
                                       self.M["rowtmp"][0:1, c * 128:(c + 1) * 128],
                                       P["onesfr"][0:1, 0:1], start=True, stop=True)
                    TS.inc(ins)
                self.on("tensor")(cb_t)

                def cb_v(v):
                    v.wait_ge(TS.h, TS.n)
                    VS.inc(v.tensor_copy(P["bcpp"][:], pth[:, 0:16]))
                self.on("vector")(cb_v)
                self.barrier(["vector", "tensor", "scalar"])
                if _TRUNC == 21:
                    def spdbg21(gp):
                        self.GD.inc(gp.dma_start(out=self.dbg.ap()[0:2, 0:D],
                                                 in_=self.M["th"][:]), 16)
                        self.GD.inc(gp.dma_start(out=self.dbg.ap()[0:2, D:2 * D],
                                                 in_=self.M["arld"][:]), 16)
                        self.GD.inc(gp.dma_start(out=self.dbg.ap()[0:1, 2 * D:3 * D],
                                                 in_=self.M["rowtmp"][0:1, :]), 16)
                        self.GD.inc(gp.dma_start(out=self.dbg.ap()[0:1, 3 * D:3 * D + 16],
                                                 in_=P["bcpp"][0:1, 0:16]), 16)
                    self.on("gpsimd")(spdbg21)
                    self.barrier()

    def phase_c2(self):
        nc, P = self.nc, self.P
        TS, AS, VS, GD, LD = self.TS, self.AS, self.VS, self.GD, self.LD
        with ExitStack() as ph:
            sbC = lambda n, s, d: ph.enter_context(nc.sbuf_tensor(n, s, d))
            outg_l = sbC("outg_l", [1, D], F32)
            outb_l = sbC("outb_l", [1, D], F32)
            x2_sb = sbC("x2_sb", [128, NCH * TOK], F32)
            x1b_sb = sbC("x1cb_sb", [128, NCH * TOK], BF16)
            gtile = [sbC(f"gt{i}", [128, 512], F32) for i in range(2)]
            xrt = [sbC(f"xrt{i}", [128, 512], F32) for i in range(2)]

            def c2_ld(gp):
                GD.inc(gp.dma_start(
                    out=x1b_sb[:].rearrange("p (c t) -> p c t", c=NCH),
                    in_=bass.AP(self.xr1, 0, [[TOK, 128], [128 * TOK, NCH], [1, TOK]])), 16)

            def c2_ld2(sync):
                LD.inc(sync.dma_start(out=outg_l[:], in_=self.inp["outg_row"].ap()), 16)
                LD.inc(sync.dma_start(out=outb_l[:], in_=self.inp["outb_row"].ap()), 16)
            self.on("gpsimd")(c2_ld)
            self.on("sync")(c2_ld2)
            self.barrier()

            vs_c0 = VS.n
            xr_ = [self.DS[2], self.DS[3]]
            xrb = [xr_[0].n, xr_[1].n]

            def c2_sync(sync):
                k = 0
                for pi in range(4):
                    for oci in range(4):
                        for tb in range(2):
                            oc = pi * 4 + oci
                            if k >= 2:
                                sync.wait_ge(VS.h, vs_c0 + k - 1)
                            xr_[k % 2].inc(sync.dma_start(
                                out=xrt[k % 2][:],
                                in_=bass.AP(self.xr1, oc * 128 * TOK + tb * 512,
                                            [[TOK, 128], [1, 512]])), 16)
                            k += 1

            def g_epi_a(a, oc, tb, psl):
                return a.activation(gtile[(oc * 2 + tb) % 2][:], psl, ACTF.Sigmoid,
                                    bias=P["gb"][:, oc:oc + 1])

            def g_epi_v(v, oc, tb, i):
                v.wait_ge(xr_[i % 2].h, xrb[i % 2] + 16 * (i // 2 + 1))
                dst = x2_sb[:, oc * TOK + tb * 512: oc * TOK + tb * 512 + 512]
                return v.scalar_tensor_tensor(dst, gtile[i % 2][:],
                                              P["bcpp"][:, oc:oc + 1],
                                              xrt[i % 2][:], ALU.mult, ALU.add)

            self.on("sync")(c2_sync)
            self.proj_fm("g0", self.inp["gw_l0"], D, D, 16,
                         lambda dc, tb: x1b_sb[:, dc * TOK + tb * 512: dc * TOK + tb * 512 + 512],
                         2, 512, g_epi_a, g_epi_v)

            self.emit_ln(ph, x2_sb, TOK, outg_l, outb_l, eps=1e-6)
            self.emit_resid_add(ph, x2_sb, self.xr1)
            self.emit_ln(ph, x2_sb, TOK, None, None, spill=self.xr2)
            self.barrier()

    def thought_step(self, ph, pth, ptt, th, d1_sb, d2_sb, step):
        nc, P = self.nc, self.P
        TS, AS, VS, GD = self.TS, self.AS, self.VS, self.GD

        def t_v(v):
            VS.inc(v.reduce_sum(self.M["arld"][0:2, 0:1], th[:], mybir.AxisListType.X))
            v.wait_ge(VS.h, VS.n)
            VS.inc(v.tensor_scalar_mul(self.M["arld"][0:2, 0:1],
                                       self.M["arld"][0:2, 0:1], 1.0 / D))
        self.on("vector")(t_v)

        def t_a(a):
            a.wait_ge(VS.h, VS.n)
            AS.inc(a.activation(self.M["tht"][:], th[:], ACTF.Square))
        self.on("scalar")(t_a)

        def t_v2(v):
            v.wait_ge(AS.h, AS.n)
            VS.inc(v.reduce_sum(self.M["arld"][0:2, 1:2], self.M["tht"][:],
                                mybir.AxisListType.X))
            v.wait_ge(VS.h, VS.n)
            VS.inc(v.tensor_scalar_mul(self.M["arld"][0:2, 1:2],
                                       self.M["arld"][0:2, 1:2], 1.0 / D))
            v.wait_ge(VS.h, VS.n)
            VS.inc(v.tensor_tensor(self.M["arld"][0:2, 2:3], self.M["arld"][0:2, 0:1],
                                   self.M["arld"][0:2, 0:1], ALU.mult))
            v.wait_ge(VS.h, VS.n)
            VS.inc(v.tensor_tensor(self.M["arld"][0:2, 2:3], self.M["arld"][0:2, 1:2],
                                   self.M["arld"][0:2, 2:3], ALU.subtract))
        self.on("vector")(t_v2)

        def t_a2(a):
            a.wait_ge(VS.h, VS.n)
            AS.inc(a.activation(self.M["arld"][0:2, 2:3], self.M["arld"][0:2, 2:3],
                                ACTF.Sqrt, bias=P["epsc"][0:2, 1:2]))
        self.on("scalar")(t_a2)

        def t_v3(v):
            v.wait_ge(AS.h, AS.n)
            VS.inc(v.reciprocal(self.M["arld"][0:2, 3:4], self.M["arld"][0:2, 2:3]))
            v.wait_ge(VS.h, VS.n)
            v.tensor_scalar(self.M["tht"][:], th[:], self.M["arld"][0:2, 0:1],
                            self.M["arld"][0:2, 3:4], ALU.subtract, ALU.mult)
            v.tensor_tensor(self.M["tht"][:], self.M["tht"][:], self.c_prs["png2"][:], ALU.mult)
            v.tensor_tensor(self.M["tht"][:], self.M["tht"][:], self.c_prs["pnb2"][:], ALU.add)
            VS.inc(v.tensor_copy(self.M["thb"][:], self.M["tht"][:]))
        self.on("vector")(t_v3)
        self.barrier(["vector", "tensor", "scalar"])

        ts0, as0 = TS.n, AS.n

        def t_tr(t):
            for c in range(NCH):
                if c >= 2:
                    t.wait_ge(AS.h, as0 + c - 1)
                TS.inc(t.transpose(ptt[:, (c % 2) * 128:(c % 2) * 128 + 2],
                                   self.M["thb"][0:2, c * 128:(c + 1) * 128],
                                   P["ident"][0:2, 0:2]))

        def t_te(a):
            for c in range(NCH):
                a.wait_ge(TS.h, ts0 + c + 1)
                AS.inc(a.activation(self.M["thfm"][:, c * 2:(c + 1) * 2],
                                    ptt[:, (c % 2) * 128:(c % 2) * 128 + 2],
                                    ACTF.Copy))
        self.on("tensor")(t_tr)
        self.on("scalar")(t_te)
        TS.n = ts0 + NCH
        AS.n = as0 + NCH
        self.barrier(["tensor", "scalar", "vector"])

        ts1 = TS.n

        def d1_t(t):
            for nb in range(2):
                for dc in range(NCH):
                    t.matmul(pth[0:2, nb * 512:(nb + 1) * 512],
                             self.M["thfm"][:, dc * 2:(dc + 1) * 2],
                             d1_sb[:, dc * 1024 + nb * 512: dc * 1024 + nb * 512 + 512],
                             start=(dc == 0), stop=False)
                TS.inc(t.matmul(pth[0:2, nb * 512:(nb + 1) * 512],
                                P["onesfr"][0:1, 0:2],
                                self.M["d1b"][0:1, nb * 512:(nb + 1) * 512],
                                start=False, stop=True))
        self.on("tensor")(d1_t)
        TS.n = ts1 + 2

        def d1_a(a):
            a.wait_ge(TS.h, TS.n)
            AS.inc(a.activation(self.M["yth"][0:2, :], pth[0:2, 0:1024], ACTF.Gelu))
        self.on("scalar")(d1_a)

        def d1_v(v):
            v.wait_ge(AS.h, AS.n)
            VS.inc(v.tensor_copy(self.M["ythb"][:], self.M["yth"][:]))
        self.on("vector")(d1_v)
        self.barrier(["vector", "tensor", "scalar"])

        ts2, as2 = TS.n, AS.n

        def d2_tr(t):
            for c in range(8):
                if c >= 2:
                    t.wait_ge(AS.h, as2 + c - 1)
                TS.inc(t.transpose(ptt[:, (c % 2) * 128:(c % 2) * 128 + 2],
                                   self.M["ythb"][0:2, c * 128:(c + 1) * 128],
                                   P["ident"][0:2, 0:2]))

        def d2_te(a):
            for c in range(8):
                a.wait_ge(TS.h, ts2 + c + 1)
                AS.inc(a.activation(self.M["ythfm"][:, c * 2:(c + 1) * 2],
                                    ptt[:, (c % 2) * 128:(c % 2) * 128 + 2],
                                    ACTF.Copy))
        self.on("tensor")(d2_tr)
        self.on("scalar")(d2_te)
        TS.n = ts2 + 8
        AS.n = as2 + 8
        self.barrier(["tensor", "scalar"])

        ts3 = TS.n

        def d2_t(t):
            for nb in range(4):
                ins = None
                for hc in range(8):
                    ins = t.matmul(pth[0:2, nb * 512:(nb + 1) * 512],
                                   self.M["ythfm"][:, hc * 2:(hc + 1) * 2],
                                   d2_sb[:, hc * D + nb * 512: hc * D + nb * 512 + 512],
                                   start=(hc == 0), stop=(hc == 7))
                TS.inc(ins)
        self.on("tensor")(d2_t)
        TS.n = ts3 + 4

        def d2_v2(v):
            v.wait_ge(TS.h, TS.n)
            VS.inc(v.tensor_copy(self.M["arld"][:], pth[0:2, 0:2048]))
        self.on("vector")(d2_v2)
        self.barrier(["vector", "tensor", "scalar", "gpsimd"])

        def d2_g(gp):
            GD.inc(gp.dma_start(out=self.ar_th_in.ap(), in_=self.M["arld"][:]), 16)
            gp.wait_ge(GD.h, GD.n)
            self.CCs.inc(gp.collective_compute(
                "AllReduce", ALU.add, replica_groups=[self.core],
                ins=[self.ar_th_in.ap()], outs=[self.ar_th_out.ap()]))
            gp.wait_ge(self.CCs.h, self.CCs.n)
            GD.inc(gp.dma_start(out=self.M["arld"][:], in_=self.ar_th_out.ap()), 16)
        self.on("gpsimd")(d2_g)
        self.barrier()

        def d2_v3(v):
            v.tensor_tensor(th[:], th[:], self.M["arld"][:], ALU.add)
            VS.inc(v.tensor_tensor(th[:], th[:], self.c_prs["d2b2"][:], ALU.add))
        self.on("vector")(d2_v3)
        self.barrier(["vector", "tensor", "scalar", "gpsimd"])

    def rowpair_matmul(self, ph, pth, ptt, th, w_dram, out_row2, bias_rows,
                       bias_base):
        nc, P = self.nc, self.P
        TS, AS, VS, GD = self.TS, self.AS, self.VS, self.GD
        thb = self.M["thb"]

        def r_v(v):
            VS.inc(v.tensor_copy(thb[:], th[:]))
        self.on("vector")(r_v)
        self.barrier(["vector", "tensor", "scalar"])
        ts0, as0 = TS.n, AS.n

        def r_tr(t):
            for c in range(NCH):
                if c >= 2:
                    t.wait_ge(AS.h, as0 + c - 1)
                TS.inc(t.transpose(ptt[:, (c % 2) * 128:(c % 2) * 128 + 2],
                                   thb[0:2, c * 128:(c + 1) * 128],
                                   P["ident"][0:2, 0:2]))

        def r_te(a):
            for c in range(NCH):
                a.wait_ge(TS.h, ts0 + c + 1)
                AS.inc(a.activation(self.M["thfm"][:, c * 2:(c + 1) * 2],
                                    ptt[:, (c % 2) * 128:(c % 2) * 128 + 2],
                                    ACTF.Copy))
        self.on("tensor")(r_tr)
        self.on("scalar")(r_te)
        TS.n = ts0 + NCH
        AS.n = as0 + NCH
        self.barrier(["tensor", "scalar", "gpsimd"])

        wsl = [ph.enter_context(nc.sbuf_tensor(f"rw{i}_{bias_base}", [128, 512], BF16))
               for i in range(2)]
        ts1 = TS.n
        ws_ = [self.DS[0], self.DS[1]]
        wsb = [ws_[0].n, ws_[1].n]

        def r_dma(gp):
            i = 0
            for dc in range(NCH):
                for nb in range(4):
                    if i >= 2:
                        gp.wait_ge(TS.h, ts1 + i - 1)
                    src = bass.AP(w_dram, dc * 128 * D + nb * 512, [[D, 128], [1, 512]])
                    ws_[i % 2].inc(gp.dma_start(out=wsl[i % 2][:], in_=src), 16)
                    i += 1
        self.on("gpsimd")(r_dma)

        def r_t(t):
            i = 0
            for dc in range(NCH):
                for nb in range(4):
                    t.wait_ge(ws_[i % 2].h, wsb[i % 2] + 16 * (i // 2 + 1))
                    TS.inc(t.matmul(pth[0:2, nb * 512:(nb + 1) * 512],
                                    self.M["thfm"][:, dc * 2:(dc + 1) * 2],
                                    wsl[i % 2][:], start=(dc == 0),
                                    stop=(dc == NCH - 1)))
                    i += 1
        self.on("tensor")(r_t)
        TS.n = ts1 + 64

        def r_v2(v):
            v.wait_ge(TS.h, TS.n)
            v.tensor_copy(out_row2[:], pth[0:2, 0:2048])
            VS.inc(v.tensor_tensor(out_row2[:], out_row2[:],
                                   bias_rows[:], ALU.add))
        self.on("vector")(r_v2)
        self.barrier(["vector", "tensor", "scalar", "gpsimd"])

    def phase_d(self):
        nc, P = self.nc, self.P
        TS, AS, VS, GD, LD = self.TS, self.AS, self.VS, self.GD, self.LD
        with ExitStack() as ph:
            attnr = ph.enter_context(nc.sbuf_tensor("attnr", [128, NCH * TOK], BF16))
            with ExitStack() as ph1:
                sbD = lambda n, s, d: ph1.enter_context(nc.sbuf_tensor(n, s, d))
                qf = sbD("qf", [128, NCH * TOK], BF16)
                kf = sbD("kf", [128, NCH * TOK], BF16)
                vt = sbD("vt", [128, 8 * D], BF16)

                with ExitStack() as ph1a:
                    x2b = ph1a.enter_context(
                        nc.sbuf_tensor("x2b", [128, NCH * TOK], BF16))

                    def d_ld(gp):
                        GD.inc(gp.dma_start(
                            out=x2b[:].rearrange("p (c t) -> p c t", c=NCH),
                            in_=bass.AP(self.xr2, 0,
                                        [[TOK, 128], [128 * TOK, NCH], [1, TOK]])), 16)
                    self.on("gpsimd")(d_ld)
                    self.barrier()

                    def mk_epi(dst):
                        def epi(a, oc, tb, psl):
                            return a.activation(
                                dst[:, oc * TOK + tb * 512: oc * TOK + tb * 512 + 512],
                                psl, ACTF.Silu)
                        return epi

                    rhs2 = lambda dc, tb: x2b[:, dc * TOK + tb * 512: dc * TOK + tb * 512 + 512]
                    self.proj_fm("qw", self.inp["qwT"], D, D, 16, rhs2, 2, 512,
                                 mk_epi(qf))
                    self.proj_fm("kw", self.inp["kwT"], D, D, 16, rhs2, 2, 512,
                                 mk_epi(kf))

                    with nc.psum_tensor("pvd2", [128, 4 * 512], F32) as pvd2:
                        vsl2 = [ph1a.enter_context(
                            nc.sbuf_tensor(f"vsl2{i}", [128, D], BF16))
                            for i in range(2)]
                        ts0, as0 = TS.n, AS.n
                        vs_ = [self.DS[0], self.DS[1]]
                        vsb = [vs_[0].n, vs_[1].n]

                        def v2_dma(gp):
                            i = 0
                            for tt in range(8):
                                for dc in range(NCH):
                                    if i >= 2:
                                        gp.wait_ge(TS.h, ts0 + i - 1)
                                    src = bass.AP(self.inp["vwT"], dc * 128 * D,
                                                  [[D, 128], [1, D]])
                                    vs_[i % 2].inc(gp.dma_start(out=vsl2[i % 2][:], in_=src), 16)
                                    i += 1

                        def v2_t(t):
                            i = 0
                            for tt in range(8):
                                if tt >= 1:
                                    t.wait_ge(AS.h, as0 + tt)
                                for dc in range(NCH):
                                    t.wait_ge(vs_[i % 2].h, vsb[i % 2] + 16 * (i // 2 + 1))
                                    ins = None
                                    for nb in range(4):
                                        ins = t.matmul(pvd2[:, nb * 512:(nb + 1) * 512],
                                                       x2b[:, dc * TOK + tt * 128: dc * TOK + tt * 128 + 128],
                                                       vsl2[i % 2][:, nb * 512:(nb + 1) * 512],
                                                       start=(dc == 0), stop=(dc == NCH - 1))
                                    TS.inc(ins)
                                    i += 1

                        def v2_a(a):
                            for tt in range(8):
                                a.wait_ge(TS.h, ts0 + (tt + 1) * NCH)
                                ins = None
                                for nb in range(4):
                                    ins = a.activation(
                                        vt[:, tt * D + nb * 512: tt * D + nb * 512 + 512],
                                        pvd2[:, nb * 512:(nb + 1) * 512], ACTF.Silu)
                                AS.inc(ins)
                        self.on("gpsimd")(v2_dma)
                        self.on("tensor")(v2_t)
                        self.on("scalar")(v2_a)
                        TS.n = ts0 + 8 * NCH
                        AS.n = as0 + 8
                        self.barrier()
                    if _TRUNC == 31:
                        return

                with ExitStack() as ph1b:
                    sb2 = lambda n, s, d: ph1b.enter_context(nc.sbuf_tensor(n, s, d))
                    KVb = sb2("KVb", [128, 4 * 1024], BF16)
                    Scur = sb2("Scur", [128, 1024], F32)
                    Cld = sb2("Cld", [128, 1024], F32)
                    ktb = sb2("ktb", [128, 256], BF16)
                    kts = sb2("kts", [128, 256], BF16)
                    dds = [sb2(f"dds{i}", [128, 512], F32) for i in range(2)]
                    qtl = sb2("qtl", [128, TOK], BF16)
                    S4 = sb2("S4", [128, 4 * 64], F32)
                    S4z = sb2("S4z", [128, 2 * 4 * 64], BF16)
                    awbuf = sb2("awbuf", [128, 2 * 512], BF16)
                    kdrow = [sb2(f"kdrow{i}", [1, 512], F32) for i in range(2)]
                    qdrow = [sb2(f"qdrow{i}", [1, 512], F32) for i in range(2)]
                    cfrow = [sb2(f"cfrow{i}", [1, 1024], F32) for i in range(2)]

                    with nc.psum_tensor("pkd", [128, 256], F32) as pkd, \
                         nc.psum_tensor("pktr", [128, 256], F32) as pktr, \
                         nc.psum_tensor("pKV", [128, 64], F32) as pKV:
                        plan = []
                        tsn, asn, vsn = TS.n, AS.n, VS.n
                        kd_ = [self.DS[2], self.DS[3]]
                        kdb = [kd_[0].n, kd_[1].n]
                        for hp in range(16):
                            e = {"hp": hp}
                            e["ld_kd"] = kdb[hp % 2] + 16 * (hp // 2 + 1)
                            tsn += 1
                            e["ts_kd"] = tsn
                            e["vs_kt"] = []
                            e["ts_tr"] = []
                            e["as_ev"] = []
                            e["ts_kv"] = []
                            e["vs_kv"] = []
                            for bl in range(4):
                                vsn += 1
                                e["vs_kt"].append(vsn)
                                tsn += 4
                                e["ts_tr"].append(tsn)
                                asn += 1
                                e["as_ev"].append(asn)
                                tsn += 2
                                e["ts_kv"].append(tsn)
                                vsn += 2
                                e["vs_kv"].append(vsn)
                            plan.append(e)

                        def d1_s(sync):
                            for hi, e in enumerate(plan):
                                if hi >= 2:
                                    sync.wait_ge(TS.h, plan[hi - 2]["ts_kd"])
                                kd_[hi % 2].inc(sync.dma_start(
                                    out=kdrow[hi % 2][:],
                                    in_=bass.AP(self.inp["kd_rows"], e["hp"] * 512,
                                                [[512, 1], [1, 512]])), 16)

                        def d1_t(t):
                            for hi, e in enumerate(plan):
                                hp = e["hp"]
                                t.wait_ge(kd_[hi % 2].h, e["ld_kd"])
                                if hi >= 1:
                                    t.wait_ge(VS.h, plan[hi - 1]["vs_kt"][3])
                                t.matmul(pkd[0:64, :], P["onesfr"][0:1, 0:64],
                                         kdrow[hi % 2][0:1, 0:256],
                                         start=True, stop=True)
                                TS.inc(t.matmul(pkd[64:128, :], P["onesfr"][0:1, 0:64],
                                                kdrow[hi % 2][0:1, 256:512],
                                                start=True, stop=True,
                                                tile_position=(0, 64)))
                                for bl in range(4):
                                    t.wait_ge(VS.h, e["vs_kt"][bl])
                                    for par in range(2):
                                        pr = par * 64
                                        for j in range(2):
                                            TS.inc(t.matmul(
                                                pktr[:, par * 128 + j * 64: par * 128 + (j + 1) * 64],
                                                ktb[:, j * 128:(j + 1) * 128],
                                                P["ident"][:, pr:pr + 64],
                                                start=True, stop=True))
                                    t.wait_ge(AS.h, e["as_ev"][bl])
                                    if hi + bl >= 1:
                                        prevkv = e["vs_kv"][bl - 1] if bl >= 1 \
                                            else plan[hi - 1]["vs_kv"][3]
                                        t.wait_ge(VS.h, prevkv)
                                    for par in range(2):
                                        pr = par * 64
                                        h = 2 * hp + par
                                        ins = None
                                        for j in range(2):
                                            ins = t.matmul(
                                                pKV[pr:pr + 64, 0:64],
                                                kts[:, par * 128 + j * 64: par * 128 + (j + 1) * 64],
                                                vt[:, (bl * 2 + j) * D + h * 64: (bl * 2 + j) * D + h * 64 + 64],
                                                start=(j == 0), stop=(j == 1),
                                                tile_position=(0, pr))
                                        TS.inc(ins)

                        def d1_v(v):
                            for hi, e in enumerate(plan):
                                hp = e["hp"]
                                for bl in range(4):
                                    v.wait_ge(TS.h, e["ts_kd"])
                                    if hi + bl >= 1:
                                        prev = e["ts_tr"][bl - 1] if bl >= 1 \
                                            else plan[hi - 1]["ts_tr"][3]
                                        v.wait_ge(TS.h, prev)
                                    kf_sl = kf[:, hp * TOK + bl * 256: hp * TOK + bl * 256 + 256]
                                    VS.inc(v.tensor_tensor(ktb[:, :], kf_sl,
                                                           pkd[:, :], ALU.mult))
                                    v.wait_ge(TS.h, e["ts_kv"][bl])
                                    for par in range(2):
                                        pr = par * 64
                                        ins = v.tensor_copy(
                                            KVb[pr:pr + 64, bl * 1024 + hp * 64: bl * 1024 + hp * 64 + 64],
                                            pKV[pr:pr + 64, 0:64])
                                        VS.inc(ins)

                        def d1_a(a):
                            for hi, e in enumerate(plan):
                                for bl in range(4):
                                    a.wait_ge(TS.h, e["ts_tr"][bl])
                                    if hi + bl >= 1:
                                        prev = e["ts_kv"][bl - 1] if bl >= 1 \
                                            else plan[hi - 1]["ts_kv"][3]
                                        a.wait_ge(TS.h, prev)
                                    AS.inc(a.activation(kts[:, :], pktr[:, :],
                                                        ACTF.Copy))
                        self.on("sync")(d1_s)
                        self.on("tensor")(d1_t)
                        self.on("vector")(d1_v)
                        self.on("scalar")(d1_a)
                        TS.n, AS.n, VS.n = tsn, asn, vsn
                        self.barrier(["tensor", "scalar", "vector", "sync"])
                        if _TRUNC == 41:
                            return

                        def c_v(v):
                            for h in range(H1):
                                hp, pr = h // 2, (h % 2) * 64
                                sl = slice(hp * 64, hp * 64 + 64)
                                for bl in range(4):
                                    w = float(_BD[h] ** (3 - bl))
                                    srcv = KVb[pr:pr + 64, bl * 1024 + hp * 64: bl * 1024 + hp * 64 + 64]
                                    if bl > 0:
                                        v.wait_ge(VS.h, VS.n)
                                    if bl == 0:
                                        VS.inc(v.tensor_scalar_mul(
                                            Cld[pr:pr + 64, sl], srcv, w))
                                    else:
                                        VS.inc(v.scalar_tensor_tensor(
                                            Cld[pr:pr + 64, sl], srcv, w,
                                            Cld[pr:pr + 64, sl], ALU.mult, ALU.add))
                        self.on("vector")(c_v)
                        self.barrier(["vector", "gpsimd"])

                        def c_g(gp):
                            for par in range(2):
                                GD.inc(gp.dma_start(
                                    out=bass.AP(self.ag_c_in, par * 64,
                                                [[H1 * DH1, 64], [128, 16], [1, 64]]),
                                    in_=Cld[par * 64:par * 64 + 64, :].rearrange(
                                        "p (c e) -> p c e", c=16)), 16)
                            gp.wait_ge(GD.h, GD.n)
                            self.CCs.inc(gp.collective_compute(
                                "AllGather", ALU.bypass, replica_groups=[self.core],
                                ins=[self.ag_c_in.ap()], outs=[self.ag_c_out.ap()]))
                            gp.wait_ge(self.CCs.h, self.CCs.n)
                        self.on("gpsimd")(c_g)
                        self.barrier()
                        if _TRUNC == 42:
                            return

                        vs1 = VS.n
                        cl_ = self.DS[4]
                        clb = cl_.n
                        cf_ = [self.DS[2], self.DS[3]]
                        cfb = [cf_[0].n, cf_[1].n]

                        def s_g2(gp):
                            pid = nc.partition_id(engines=[EngT.Pool])
                            for j in range(3):
                                if j >= 1:
                                    gp.wait_ge(VS.h, vs1 + j)
                                base = ((pid // 4) * 4 + j) * 64 * H1 * DH1
                                for par in range(2):
                                    cl_.inc(gp.dma_start(
                                        out=Cld[par * 64:par * 64 + 64, :].rearrange(
                                            "p (c e) -> p c e", c=16),
                                        in_=bass.AP(self.ag_c_out, base + par * 64,
                                                    [[H1 * DH1, 64], [128, 16], [1, 64]])), 16)

                        with nc.psum_tensor("pcf", [128, 2 * 512], F32) as pcf:
                            ts_cf = [TS.n]
                            cf_marks = [TS.n + k + 1 for k in range(6)]

                            def s_s(sync):
                                k = 0
                                for j in range(3):
                                    for par in range(2):
                                        if k >= 2:
                                            sync.wait_ge(TS.h, cf_marks[(k - 2)])
                                        cf_[k % 2].inc(sync.dma_start(
                                            out=cfrow[k % 2][:],
                                            in_=bass.AP(self.inp["coeff_rows"],
                                                        j * D + par * 1024,
                                                        [[1024, 1], [1, 1024]])), 16)
                                        k += 1

                            def s_t(t):
                                k = 0
                                for j in range(3):
                                    t.wait_ge(cl_.h, clb + 32 * (j + 1))
                                    if j >= 1:
                                        t.wait_ge(VS.h, vs1 + j)
                                    for par in range(2):
                                        pr = par * 64
                                        t.wait_ge(cf_[k % 2].h, cfb[k % 2] + 16 * (k // 2 + 1))
                                        t.matmul(pcf[pr:pr + 64, 0:512],
                                                 P["onesfr"][0:1, 0:64],
                                                 cfrow[k % 2][0:1, 0:512],
                                                 start=True, stop=True,
                                                 tile_position=(0, pr))
                                        TS.inc(t.matmul(pcf[pr:pr + 64, 512:1024],
                                                        P["onesfr"][0:1, 0:64],
                                                        cfrow[k % 2][0:1, 512:1024],
                                                        start=True, stop=True,
                                                        tile_position=(0, pr)))
                                        k += 1

                            def s_v2(v):
                                v.memset(Scur[:, :], 0.0)
                                for j in range(3):
                                    v.wait_ge(TS.h, ts_cf[0] + (j + 1) * 2)
                                    v.tensor_tensor(Cld[:, :], Cld[:, :],
                                                    pcf[:, 0:1024], ALU.mult)
                                    VS.inc(v.tensor_tensor(Scur[:, :], Scur[:, :],
                                                           Cld[:, :], ALU.add))
                            self.on("gpsimd")(s_g2)
                            self.on("sync")(s_s)
                            self.on("tensor")(s_t)
                            self.on("vector")(s_v2)
                            TS.n = ts_cf[0] + 6
                            VS.n = vs1 + 3
                            self.barrier(["tensor", "scalar", "vector", "gpsimd", "sync"])

                    if _TRUNC == 32:
                        return
                    with nc.psum_tensor("pqd2", [128, 256], F32) as pqd2, \
                         nc.psum_tensor("paw0", [128, 512], F32) as paw0, \
                         nc.psum_tensor("paw1", [128, 512], F32) as paw1, \
                         nc.psum_tensor("pip", [128, 2 * 512], F32) as pip:
                        paw = [paw0, paw1]
                        plan = []
                        tsn, asn, vsn = TS.n, AS.n, VS.n
                        qd_ = [self.DS[2], self.DS[3]]
                        qdb = [qd_[0].n, qd_[1].n]
                        dd_ = self.DS[4]
                        ddb = dd_.n
                        for hp in range(16):
                            e = {"hp": hp}
                            e["gd_dd"] = ddb + 32 * (hp + 1)
                            e["ld_qd"] = qdb[hp % 2] + 16 * (hp // 2 + 1)
                            tsn += 1
                            e["ts_qd"] = tsn
                            e["aw"] = [0] * 8
                            e["ip"] = [0] * 8
                            for s in range(8):
                                tsn += 1
                                e["aw"][s] = tsn
                                if s >= 1:
                                    tsn += 1
                                    e["ip"][s - 1] = tsn
                            tsn += 1
                            e["ip"][7] = tsn
                            e["vs0"] = vsn
                            e["qt"] = [vsn + 1 + b for b in range(4)]
                            e["st"] = [vsn + 4 + 4 * (b + 1) for b in range(4)]
                            e["dd"] = [vsn + 20 + 1 + s for s in range(8)]
                            vsn += 28
                            e["ev"] = [asn + 1 + s for s in range(8)]
                            asn += 8
                            plan.append(e)

                        def d2_s(sync):
                            for hi, e in enumerate(plan):
                                if hi >= 2:
                                    sync.wait_ge(TS.h, plan[hi - 2]["ts_qd"])
                                qd_[hi % 2].inc(sync.dma_start(
                                    out=qdrow[hi % 2][:],
                                    in_=bass.AP(self.inp["qd_rows"], e["hp"] * 512,
                                                [[512, 1], [1, 512]])), 16)

                        def d2_g2(gp):
                            for hi, e in enumerate(plan):
                                hp = e["hp"]
                                if hi >= 1:
                                    gp.wait_ge(VS.h, plan[hi - 1]["dd"][7])
                                for par in range(2):
                                    src = bass.AP(self.inp["ddT"],
                                                  (2 * hp + par) * BS1 * BS1,
                                                  [[BS1, 128], [128 * BS1, 2], [1, 256]])
                                    dd_.inc(gp.dma_start(
                                        out=dds[par][:].rearrange("p (j t) -> p j t", j=2),
                                        in_=src), 16)

                        def d2_t2(t):
                            for hi, e in enumerate(plan):
                                hp = e["hp"]
                                t.wait_ge(qd_[hi % 2].h, e["ld_qd"])
                                if hi >= 1:
                                    t.wait_ge(VS.h, plan[hi - 1]["qt"][3])
                                t.matmul(pqd2[0:64, :], P["onesfr"][0:1, 0:64],
                                         qdrow[hi % 2][0:1, 0:256],
                                         start=True, stop=True)
                                TS.inc(t.matmul(pqd2[64:128, :], P["onesfr"][0:1, 0:64],
                                                qdrow[hi % 2][0:1, 256:512],
                                                start=True, stop=True,
                                                tile_position=(0, 64)))
                                def emit_aw(s):
                                    bl, par = s // 2, s % 2
                                    pr = par * 64
                                    if s >= 2:
                                        t.wait_ge(VS.h, e["dd"][s - 2])
                                    ins = None
                                    for j in range(2):
                                        kf_sl = kf[pr:pr + 64,
                                                   hp * TOK + bl * 256 + j * 128:
                                                   hp * TOK + bl * 256 + j * 128 + 128]
                                        ins = t.matmul(
                                            paw[s % 2][:, j * 256:(j + 1) * 256],
                                            kf_sl,
                                            qf[pr:pr + 64,
                                               hp * TOK + bl * 256: hp * TOK + (bl + 1) * 256],
                                            start=True, stop=True)
                                    TS.inc(ins)

                                def emit_ip(s):
                                    bl, par = s // 2, s % 2
                                    pr = par * 64
                                    h = 2 * hp + par
                                    t.wait_ge(VS.h, e["dd"][s])
                                    t.wait_ge(VS.h, e["qt"][bl])
                                    t.wait_ge(VS.h, e["st"][bl])
                                    if s >= 2:
                                        t.wait_ge(AS.h, e["ev"][s - 2])
                                    for j in range(2):
                                        t.matmul(
                                            pip[pr:pr + 64, (s % 2) * 512:(s % 2) * 512 + 256],
                                            vt[:, (bl * 2 + j) * D + h * 64: (bl * 2 + j) * D + h * 64 + 64],
                                            awbuf[:, (s % 2) * 512 + j * 256:(s % 2) * 512 + (j + 1) * 256],
                                            start=(j == 0), stop=False,
                                            tile_position=(0, pr))
                                    ins = t.matmul(
                                        pip[pr:pr + 64, (s % 2) * 512:(s % 2) * 512 + 256],
                                        S4z[:, par * 256 + bl * 64: par * 256 + (bl + 1) * 64],
                                        qtl[:, bl * 256:(bl + 1) * 256],
                                        start=False, stop=True,
                                        tile_position=(0, pr))
                                    TS.inc(ins)

                                for s in range(8):
                                    emit_aw(s)
                                    if s >= 1:
                                        emit_ip(s - 1)
                                emit_ip(7)

                        def d2_v(v):
                            v.memset(S4z[:], 0.0)
                            for hi, e in enumerate(plan):
                                hp = e["hp"]
                                v.wait_ge(TS.h, e["ts_qd"])
                                if hi >= 1:
                                    v.wait_ge(TS.h, plan[hi - 1]["ip"][7])
                                for bl in range(4):
                                    qsrc = qf[:, hp * TOK + bl * 256: hp * TOK + bl * 256 + 256]
                                    VS.inc(v.tensor_tensor(
                                        qtl[:, bl * 256:(bl + 1) * 256], qsrc,
                                        pqd2[:, :], ALU.mult))
                                stb = e["vs0"] + 4
                                sk = 0
                                for bl in range(4):
                                    for par in range(2):
                                        pr = par * 64
                                        h = 2 * hp + par
                                        if bl == 0:
                                            ins = v.tensor_copy(
                                                S4[pr:pr + 64, 0:64],
                                                Scur[pr:pr + 64, hp * 64:(hp + 1) * 64])
                                        else:
                                            ins = v.scalar_tensor_tensor(
                                                S4[pr:pr + 64, bl * 64:(bl + 1) * 64],
                                                S4[pr:pr + 64, (bl - 1) * 64: bl * 64],
                                                float(_BD[h]),
                                                KVb[pr:pr + 64, (bl - 1) * 1024 + hp * 64: (bl - 1) * 1024 + hp * 64 + 64],
                                                ALU.mult, ALU.add)
                                        VS.inc(ins)
                                        v.wait_ge(VS.h, stb + 2 * sk + 1)
                                        VS.inc(v.tensor_copy(
                                            S4z[pr:pr + 64, par * 256 + bl * 64: par * 256 + (bl + 1) * 64],
                                            S4[pr:pr + 64, bl * 64:(bl + 1) * 64]))
                                        sk += 1
                                for s in range(8):
                                    bl, par = s // 2, s % 2
                                    v.wait_ge(TS.h, e["aw"][s])
                                    v.wait_ge(dd_.h, e["gd_dd"])
                                    if s >= 2:
                                        v.wait_ge(TS.h, e["ip"][s - 2])
                                    VS.inc(v.tensor_tensor(
                                        awbuf[:, (s % 2) * 512:(s % 2) * 512 + 512],
                                        paw[s % 2][:, :], dds[par][:, :],
                                        ALU.mult))

                        def d2_a(a):
                            for e in plan:
                                hp = e["hp"]
                                for s in range(8):
                                    bl, par = s // 2, s % 2
                                    pr = par * 64
                                    a.wait_ge(TS.h, e["ip"][s])
                                    AS.inc(a.activation(
                                        attnr[pr:pr + 64,
                                              hp * TOK + bl * 256: hp * TOK + bl * 256 + 256],
                                        pip[pr:pr + 64, (s % 2) * 512:(s % 2) * 512 + 256],
                                        ACTF.Copy))
                        self.on("sync")(d2_s)
                        self.on("gpsimd")(d2_g2)
                        self.on("tensor")(d2_t2)
                        self.on("vector")(d2_v)
                        self.on("scalar")(d2_a)
                        TS.n, AS.n, VS.n = tsn, asn, vsn
                        self.barrier()
                    if _TRUNC == 33:
                        def spdbg3(sync):
                            self.LD.inc(sync.dma_start(out=self.dbg.ap(),
                                                       in_=attnr[:, :]), 16)
                        self.on("sync")(spdbg3)
                        self.barrier()
                        return

            with ExitStack() as ph3:
                sb3 = lambda n, s, d: ph3.enter_context(nc.sbuf_tensor(n, s, d))
                nwpp = sb3("nwpp", [128, 16], F32)
                x3_sb = sb3("x3_sb", [128, NCH * TOK], F32)
                ga_sb = sb3("ga_sb", [128, NCH * TOK], BF16)
                x2b2 = sb3("x2b2", [128, NCH * TOK], BF16)
                gt3 = [sb3(f"gt3{i}", [128, 512], F32) for i in range(2)]

                def d3_ld(gp):
                    GD.inc(gp.dma_start(
                        out=x2b2[:].rearrange("p (c t) -> p c t", c=NCH),
                        in_=bass.AP(self.xr2, 0,
                                    [[TOK, 128], [128 * TOK, NCH], [1, TOK]])), 16)

                def d3_ld2(sync):
                    LD.inc(sync.dma_start(out=nwpp[:], in_=self.inp["nw_pp"].ap()), 16)
                self.on("gpsimd")(d3_ld)
                self.on("sync")(d3_ld2)
                self.barrier()

                with nc.psum_tensor("prs", [128, 512], F32) as prs, \
                     nc.sbuf_tensor("sqd0", [128, 512], BF16) as sqd0, \
                     nc.sbuf_tensor("sqd1", [128, 512], BF16) as sqd1:
                    sqd = [sqd0, sqd1]
                    ts0, as0, vs0 = TS.n, AS.n, VS.n

                    def r_a(a):
                        for tb in range(2):
                            for dc in range(NCH):
                                i = tb * NCH + dc
                                if i >= 2:
                                    a.wait_ge(TS.h, ts0 + i - 1)
                                AS.inc(a.activation(
                                    sqd[i % 2][:],
                                    attnr[:, dc * TOK + tb * 512: dc * TOK + tb * 512 + 512],
                                    ACTF.Square))

                    def r_t(t):
                        for tb in range(2):
                            if tb >= 1:
                                t.wait_ge(VS.h, vs0 + tb)
                            for dc in range(NCH):
                                i = tb * NCH + dc
                                t.wait_ge(AS.h, as0 + i + 1)
                                TS.inc(t.matmul(prs[0:1, :], P["onesbfc"][:],
                                                sqd[i % 2][:], start=(dc == 0),
                                                stop=(dc == NCH - 1)))

                    def r_v(v):
                        for tb in range(2):
                            v.wait_ge(TS.h, ts0 + (tb + 1) * NCH)
                            sl = slice(tb * 512, tb * 512 + 512)
                            VS.inc(v.tensor_scalar_mul(P["r_sq"][0:1, sl],
                                                       prs[0:1, :], 1.0 / D))
                    self.on("scalar")(r_a)
                    self.on("tensor")(r_t)
                    self.on("vector")(r_v)
                    TS.n = ts0 + 2 * NCH
                    AS.n = as0 + 2 * NCH
                    VS.n = vs0 + 2
                    self.barrier(["tensor", "scalar", "vector"])

                def r_a2(a):
                    for tb in range(2):
                        sl = slice(tb * 512, tb * 512 + 512)
                        AS.inc(a.activation(P["r_t0"][0:1, sl], P["r_sq"][0:1, sl],
                                            ACTF.Sqrt, bias=P["epsc"][0:1, 0:1]))
                self.on("scalar")(r_a2)

                def r_v2(v):
                    v.wait_ge(AS.h, AS.n)
                    for tb in range(2):
                        sl = slice(tb * 512, tb * 512 + 512)
                        VS.inc(v.reciprocal(P["r_A"][0:1, sl], P["r_t0"][0:1, sl]))
                self.on("vector")(r_v2)
                self.barrier(["tensor", "scalar", "vector"])

                rbc_sb = ph3.enter_context(nc.sbuf_tensor("rbc_sb", [128, 1024], F32))
                with nc.psum_tensor("prb", [128, 2 * 512], F32) as prb:
                    def nb_t(t):
                        ins = None
                        for tb in range(2):
                            ins = t.matmul(prb[:, tb * 512:(tb + 1) * 512],
                                           P["onesfr"][:],
                                           P["r_A"][0:1, tb * 512:(tb + 1) * 512],
                                           start=True, stop=True)
                        TS.inc(ins)
                    self.on("tensor")(nb_t)

                    def nb_v(v):
                        v.wait_ge(TS.h, TS.n)
                        VS.inc(v.tensor_copy(rbc_sb[:], prb[:, :]))
                    self.on("vector")(nb_v)
                    self.barrier(["tensor", "scalar", "vector"])

                rhsga = lambda dc, tb: x2b2[:, dc * TOK + tb * 512: dc * TOK + tb * 512 + 512]

                def ga_epi_a2(a, oc, tb, psl):
                    return a.activation(gt3[(oc * 2 + tb) % 2][:], psl,
                                        ACTF.Sigmoid)

                def ga_epi_v2(v, oc, tb, i):
                    dst = ga_sb[:, oc * TOK + tb * 512: oc * TOK + tb * 512 + 512]
                    at = attnr[:, oc * TOK + tb * 512: oc * TOK + tb * 512 + 512]
                    v.tensor_tensor(dst, at, rbc_sb[:, tb * 512:(tb + 1) * 512],
                                    ALU.mult)
                    v.tensor_scalar_mul(dst, dst, nwpp[:, oc:oc + 1])
                    return v.tensor_tensor(dst, dst, gt3[i % 2][:], ALU.mult)

                self.proj_fm("g1", self.inp["gwT1"], D, D, 16, rhsga, 2, 512,
                             ga_epi_a2, ga_epi_v2)

                xres = [ph3.enter_context(
                    nc.sbuf_tensor(f"xres{i}", [128, 512], F32))
                    for i in range(2)]
                xo_ = [self.DS[2], self.DS[3]]
                xob = [xo_[0].n, xo_[1].n]
                vs_o0 = self.VS.n

                def o_sync(sync):
                    k = 0
                    for pi in range(4):
                        for oci in range(4):
                            for tb in range(2):
                                oc = pi * 4 + oci
                                if k >= 2:
                                    sync.wait_ge(VS.h, vs_o0 + k - 1)
                                xo_[k % 2].inc(sync.dma_start(
                                    out=xres[k % 2][:],
                                    in_=bass.AP(self.xr2,
                                                oc * 128 * TOK + tb * 512,
                                                [[TOK, 128], [1, 512]])), 16)
                                k += 1

                def o_epi_a(a, oc, tb, psl):
                    return a.activation(gt3[(oc * 2 + tb) % 2][:], psl, ACTF.Copy)

                def o_epi_v(v, oc, tb, i):
                    v.wait_ge(xo_[i % 2].h, xob[i % 2] + 16 * (i // 2 + 1))
                    dst = x3_sb[:, oc * TOK + tb * 512: oc * TOK + tb * 512 + 512]
                    return v.tensor_tensor(dst, gt3[i % 2][:], xres[i % 2][:],
                                           ALU.add)

                self.on("sync")(o_sync)
                self.proj_fm("o1", self.inp["owT1"], D, D, 16,
                             lambda dc, tb: ga_sb[:, dc * TOK + tb * 512: dc * TOK + tb * 512 + 512],
                             2, 512, o_epi_a, o_epi_v)

                self.emit_ln(ph3, x3_sb, TOK, None, None, spill=self.xr3)
                self.barrier()

    def phase_e(self):
        nc, P = self.nc, self.P
        TS, AS, VS, GD, LD = self.TS, self.AS, self.VS, self.GD, self.LD
        for hf in range(2):
            with ExitStack() as ph:
                sbE = lambda n, s, d: ph.enter_context(nc.sbuf_tensor(n, s, d))
                xb = sbE(f"xbe{hf}", [128, NCH * 512], BF16)
                gate = sbE(f"gate{hf}", [128, OCH1 * 512], BF16)
                up = sbE(f"up{hf}", [128, OCH1 * 512], BF16)
                x4 = sbE(f"x4_{hf}", [128, NCH * 512], F32)
                tmpE = [sbE(f"tmpE{hf}{i}", [128, 512], F32) for i in range(2)]

                def e_ld(gp):
                    GD.inc(gp.dma_start(
                        out=xb[:].rearrange("p (c t) -> p c t", c=NCH),
                        in_=bass.AP(self.xr3, hf * 512,
                                    [[TOK, 128], [128 * TOK, NCH], [1, 512]])), 16)
                self.on("gpsimd")(e_ld)
                self.barrier(["gpsimd", "tensor", "scalar", "vector", "sync"])

                rhsE = lambda dc, tb: xb[:, dc * 512:(dc + 1) * 512]

                with nc.psum_tensor(f"pxs{hf}", [128, 512], F32) as pxs:
                    def xs_t(t):
                        ins = None
                        for dc in range(NCH):
                            ins = t.matmul(pxs[0:1, :], P["onesbfc"][:], rhsE(dc, 0),
                                           start=(dc == 0), stop=(dc == NCH - 1))
                        TS.inc(ins)
                    self.on("tensor")(xs_t)

                    def xs_v(v):
                        v.wait_ge(TS.h, TS.n)
                        VS.inc(v.tensor_copy(P["xsum"][0:1, 0:512], pxs[0:1, :]))
                    self.on("vector")(xs_v)
                    self.barrier(["tensor", "scalar", "vector"])

                def ge_epi(a, oc, tb, psl):
                    return a.activation(gate[:, oc * 512:(oc + 1) * 512], psl,
                                        ACTF.Silu, scale=P["sg"][:, oc:oc + 1])

                def up_epi(a, oc, tb, psl):
                    return a.mul(up[:, oc * 512:(oc + 1) * 512], psl,
                                 P["su"][:, oc:oc + 1])

                corr = lambda tb: P["xsum"][0:1, 0:512]
                self.proj_fm(f"eg{hf}", self.inp["wg_u8"], D, HID1, OCH1,
                             rhsE, 1, 512, ge_epi, corr_row=corr)
                self.proj_fm(f"eu{hf}", self.inp["wu_u8"], D, HID1, OCH1,
                             rhsE, 1, 512, up_epi, corr_row=corr)

                vs0 = VS.n

                def gu_v(v):
                    for oc in range(OCH1):
                        VS.inc(v.tensor_tensor(gate[:, oc * 512:(oc + 1) * 512],
                                               gate[:, oc * 512:(oc + 1) * 512],
                                               up[:, oc * 512:(oc + 1) * 512],
                                               ALU.mult))
                self.on("vector")(gu_v)
                self.barrier(["vector", "tensor", "scalar"])

                with nc.psum_tensor(f"pgs{hf}", [128, 512], F32) as pgs:
                    def gs_t(t):
                        ins = None
                        for hc in range(OCH1):
                            ins = t.matmul(pgs[0:1, :], P["onesbfc"][:],
                                           gate[:, hc * 512:(hc + 1) * 512],
                                           start=(hc == 0), stop=(hc == OCH1 - 1))
                        TS.inc(ins)
                    self.on("tensor")(gs_t)

                    def gs_v(v):
                        v.wait_ge(TS.h, TS.n)
                        VS.inc(v.tensor_copy(P["xsum"][0:1, 512:1024], pgs[0:1, :]))
                    self.on("vector")(gs_v)
                    self.barrier(["tensor", "scalar", "vector"])

                xe_ = [self.DS[2], self.DS[3]]
                xeb = [xe_[0].n, xe_[1].n]
                xres_e = [sbE(f"xre{hf}{i}", [128, 512], F32) for i in range(2)]

                def dn_epi_a(a, oc, tb, psl):
                    return a.activation(tmpE[oc % 2][:], psl, ACTF.Copy)

                vs_e0 = self.VS.n

                def dn_epi_v(v, oc, tb, i):
                    v.wait_ge(xe_[i % 2].h, xeb[i % 2] + 16 * (i // 2 + 1))
                    dst = x4[:, oc * 512:(oc + 1) * 512]
                    return v.scalar_tensor_tensor(dst, tmpE[i % 2][:],
                                                  P["sd"][:, oc:oc + 1],
                                                  xres_e[i % 2][:],
                                                  ALU.mult, ALU.add)

                def d_sync2(sync):
                    for k in range(16):
                        if k >= 2:
                            sync.wait_ge(VS.h, vs_e0 + k - 1)
                        xe_[k % 2].inc(sync.dma_start(
                            out=xres_e[k % 2][:],
                            in_=bass.AP(self.xr3, k * 128 * TOK + hf * 512,
                                        [[TOK, 128], [1, 512]])), 16)
                self.on("sync")(d_sync2)
                corr2 = lambda tb: P["xsum"][0:1, 512:1024]
                self.proj_fm(f"ed{hf}", self.inp["wd_u8"], HID1, D, 16,
                             lambda dc, tb: gate[:, dc * 512:(dc + 1) * 512],
                             1, 512, dn_epi_a, dn_epi_v, corr_row=corr2)

                self.emit_ln(ph, x4, 512, None, None)
                def e_out(sync):
                    LD.inc(sync.dma_start(
                        out=bass.AP(self.y, hf * 512, [[TOK, 128], [128 * TOK, NCH], [1, 512]]),
                        in_=x4[:].rearrange("p (c t) -> p c t", c=NCH)), 16)
                self.on("sync")(e_out)
                self.barrier()

        def fin(sync):
            sync.wait_ge(self.LD.h, self.LD.n)
        self.on("sync")(fin)


_BUILT = {}


def _host_prep(inputs):
    x = np.asarray(inputs["L0_attn_x"], dtype=np.float32)
    asc = np.ascontiguousarray
    qd32 = _QD.astype(np.float32)
    kd32 = _KD.astype(np.float32)
    dd32 = _DD.astype(np.float32)

    mask = np.zeros((4, 128, 512), np.float32)
    for o in range(4):
        ki = 128 * o + np.arange(128)[:, None]
        qj = np.arange(512)[None, :]
        mask[o] = (ki <= qj).astype(np.float32)

    def pp(vec, n):
        return asc(np.asarray(vec, np.float32).reshape(n, 128).T)

    qkv_w = np.asarray(inputs["L1_attn_qkv_w"], np.float32).reshape(H1, 3, DH1, D)
    qwT = asc(qkv_w[:, 0].reshape(H1 * DH1, D).T)
    kwT = asc(qkv_w[:, 1].reshape(H1 * DH1, D).T)
    vwT = asc(qkv_w[:, 2].reshape(H1 * DH1, D).T)
    gwT1 = asc(np.asarray(inputs["L1_attn_gate_w"], np.float32).T)
    owT1 = asc(np.asarray(inputs["L1_attn_out_w"], np.float32).T)

    u8 = lambda a: asc(np.asarray(a).view(np.uint8)[:, ::4])
    wg8 = asc(u8(inputs["L1_ffn_wq_gate"]).T)
    wu8 = asc(u8(inputs["L1_ffn_wq_up"]).T)
    wd8 = asc(u8(inputs["L1_ffn_wq_down"]).T)

    pair = lambda k: np.ascontiguousarray(
        np.tile(np.asarray(inputs[k], np.float32).reshape(1, D), (2, 1)))
    png2 = pair("L0_ffn_pn_gamma")
    pnb2 = pair("L0_ffn_pn_beta")
    d2b2 = pair("L0_ffn_d2_b")
    aggb2 = pair("L0_ffn_agg_b")
    bcb2 = pair("L0_ffn_bc_b")

    x_fm = [asc(x[b].T) for b in range(B)]
    wq = np.asarray(inputs["L0_attn_wq"], np.float32)
    wk = np.asarray(inputs["L0_attn_wk"], np.float32)
    wv = np.asarray(inputs["L0_attn_wv"], np.float32)
    bq = np.asarray(inputs["L0_attn_bq"], np.float32)
    bk = np.asarray(inputs["L0_attn_bk"], np.float32)
    bv = np.asarray(inputs["L0_attn_bv"], np.float32)

    in_maps = []
    for c in range(NCC):
        b, p = c // 4, c % 4
        g = p
        coeff = np.zeros((3, D), np.float32)
        for j in range(3):
            if j < p:
                for h in range(H1):
                    coeff[j, h * 64:(h + 1) * 64] = (_BD[h] ** 4) ** (p - 1 - j)
        m = {
            "x_fm": x_fm[b],
            "wq_own": asc(wq[:, 512 * g:512 * g + 512]),
            "bq_pp": pp(bq[512 * g:512 * g + 512], 4),
            "wk_own": asc(wk[:, 128 * g:128 * g + 128]),
            "bk_pp": pp(bk[128 * g:128 * g + 128], 1),
            "wv_own": asc(wv[:, 128 * g:128 * g + 128]),
            "bv_row": asc(bv[128 * g:128 * g + 128].reshape(1, 128)),
            "wo_w": np.asarray(inputs["L0_attn_wo"], np.float32),
            "bo_pp": pp(inputs["L0_attn_bo"], 16),
            "masks": mask,
            "ing_pp": pp(inputs["L0_ffn_in_gamma"], 16),
            "inb_pp": pp(inputs["L0_ffn_in_beta"], 16),
            "agg_w": np.asarray(inputs["L0_ffn_agg_w"], np.float32),
            "png2": png2, "pnb2": pnb2, "d2b2": d2b2,
            "aggb2": aggb2, "bcb2": bcb2,
            "d1s": asc(np.asarray(inputs["L0_ffn_d1_w"], np.float32)[:, 1024 * c:1024 * c + 1024]),
            "d1b_own": asc(np.asarray(inputs["L0_ffn_d1_b"], np.float32)[1024 * c:1024 * c + 1024].reshape(1, 1024)),
            "d2s": asc(np.asarray(inputs["L0_ffn_d2_w"], np.float32)[1024 * c:1024 * c + 1024, :]),
            "bc_w": np.asarray(inputs["L0_ffn_bc_w"], np.float32),
            "gw_l0": np.asarray(inputs["L0_ffn_gate_w"], np.float32),
            "gb_pp": pp(inputs["L0_ffn_gate_b"], 16),
            "outg_row": np.asarray(inputs["L0_ffn_out_gamma"], np.float32).reshape(1, D),
            "outb_row": np.asarray(inputs["L0_ffn_out_beta"], np.float32).reshape(1, D),
            "own_mask": np.full((1, 1), 1.0 if p == 3 else 0.0, np.float32),
            "qwT": qwT, "kwT": kwT, "vwT": vwT, "gwT1": gwT1, "owT1": owT1,
            "nw_row": np.asarray(inputs["L1_attn_norm_w"], np.float32).reshape(1, D),
            "nw_pp": pp(inputs["L1_attn_norm_w"], 16),
            "qd_rows": qd32.reshape(1, -1),
            "kd_rows": kd32.reshape(1, -1),
            "ddT": np.ascontiguousarray(np.transpose(dd32, (0, 2, 1))),
            "coeff_rows": coeff.reshape(1, -1),
            "wg_u8": wg8, "wu_u8": wu8, "wd_u8": wd8,
            "sg_pp": pp(inputs["L1_ffn_s_gate"], OCH1),
            "su_pp": pp(inputs["L1_ffn_s_up"], OCH1),
            "sd_pp": pp(inputs["L1_ffn_s_down"], 16),
        }
        in_maps.append(m)
    return in_maps


def kernel(**inputs):
    in_maps = _host_prep(inputs)
    kb = KB()
    nc = kb.build()
    res = run_bass_kernel_spmd(nc, in_maps, core_ids=list(range(NCC)))
    out = np.zeros((B, D, S), np.float32)
    for c in range(NCC):
        b, p = c // 4, c % 4
        out[b][:, 1024 * p:1024 * p + 1024] = res.results[c]["y"]
    return np.ascontiguousarray(out.transpose(0, 2, 1))


# version 06 (final)
# speedup vs baseline: 1.0251x; 1.0251x over previous
import numpy as np
import ml_dtypes
from contextlib import ExitStack

import concourse.bass as bass
import concourse.mybir as mybir
from concourse.bass_utils import run_bass_kernel_spmd

F32 = mybir.dt.float32
BF16 = mybir.dt.bfloat16
U8 = mybir.dt.uint8
ALU = mybir.AluOpType
ACTF = mybir.ActivationFunctionType
BF = ml_dtypes.bfloat16
EngT = mybir.EngineType

B, S, D = 2, 4096, 2048
NCC = 8
TOK = 1024
NCH = 16
H1, DH1, BS1 = 32, 64, 256
HID1 = 5632
OCH1 = HID1 // 128
SCALE0 = float(1.0 / np.sqrt(128.0))

_TRUNC = 0


def _decay():
    base = 1.0 / 2.0 ** (8.0 / H1)
    rate = base ** (np.arange(H1, dtype=np.float64) + 1.0)
    factor = 1.0 - 0.0 / (32 - 1 + 1e-5) + 1e-5
    slope = rate * factor
    r = np.arange(BS1, dtype=np.float64) + 1.0
    qd = np.exp(-slope[:, None] * r[None, :])
    kd = np.exp(-slope[:, None] * (BS1 - r[None, :]))
    ddm = r[:, None] - r[None, :]
    dd = np.where(ddm >= 0, np.exp(-slope[:, None, None] * ddm[None]), 0.0)
    bd = np.exp(-slope * BS1)
    return qd, kd, dd, bd


_QD, _KD, _DD, _BD = _decay()


class Sem:
    def __init__(self, h):
        self.h = h
        self.n = 0

    def inc(self, ins, k=1):
        ins.then_inc(self.h, k)
        self.n += k
        return self.n


class KB:

    def __init__(self):
        nc = bass.Bass("TRN2", target_bir_lowering=False,
                       detect_race_conditions=False)
        self.nc = nc
        self.core = list(range(NCC))
        ein = lambda n, s, dt=F32: nc.dram_tensor(n, s, dt, kind="ExternalInput")
        self.inp = {}
        for n, s, dt in [
            ("x_fm", [D, S], F32), ("wq_own", [D, 512], F32),
            ("bq_pp", [128, 4], F32), ("wk_own", [D, 128], F32),
            ("bk_pp", [128, 1], F32), ("wv_own", [D, 128], F32),
            ("bv_row", [1, 128], F32), ("wo_w", [D, D], F32),
            ("bo_pp", [128, 16], F32), ("masks", [4, 128, 512], F32),
            ("ing_pp", [128, 16], F32), ("inb_pp", [128, 16], F32),
            ("agg_w", [D, D], F32), ("png2", [2, D], F32),
            ("pnb2", [2, D], F32), ("d2b2", [2, D], F32),
            ("aggb2", [2, D], F32), ("bcb2", [2, D], F32),
            ("d1s", [D, 1024], F32),
            ("d1b_own", [1, 1024], F32), ("d2s", [1024, D], F32),
            ("bc_w", [D, D], F32), ("gw_l0", [D, D], F32),
            ("gb_pp", [128, 16], F32), ("outg_row", [1, D], F32),
            ("outb_row", [1, D], F32), ("own_mask", [1, 1], F32),
            ("qwT", [D, D], F32), ("kwT", [D, D], F32), ("vwT", [D, D], F32),
            ("gwT1", [D, D], F32), ("owT1", [D, D], F32),
            ("nw_row", [1, D], F32), ("nw_pp", [128, 16], F32),
            ("qd_rows", [1, H1 * BS1], F32),
            ("kd_rows", [1, H1 * BS1], F32), ("ddT", [H1, BS1, BS1], F32),
            ("coeff_rows", [1, 3 * D], F32),
            ("wg_u8", [D, HID1], U8), ("wu_u8", [D, HID1], U8),
            ("wd_u8", [HID1, D], U8), ("sg_pp", [128, OCH1], F32),
            ("su_pp", [128, OCH1], F32), ("sd_pp", [128, 16], F32),
        ]:
            self.inp[n] = ein(n, s, dt)
        self.y = nc.dram_tensor("y", [D, TOK], F32, kind="ExternalOutput")
        dr = lambda n, s, dt=F32, sp="Local": nc.dram_tensor(n, s, dt, addr_space=sp)
        self.ag_att_in = dr("ag_att_in", [512, S], BF16)
        self.ag_att_out = dr("ag_att_out", [NCC, 512, S], BF16, "Shared")
        self.ar_seed_in = dr("ar_seed_in", [2, D])
        self.ar_seed_out = dr("ar_seed_out", [2, D], F32, "Shared")
        self.ar_th_in = dr("ar_th_in", [2, D])
        self.ar_th_out = dr("ar_th_out", [2, D], F32, "Shared")
        self.ag_c_in = dr("ag_c_in", [64, H1 * DH1])
        self.ag_c_out = dr("ag_c_out", [NCC, 64, H1 * DH1], F32, "Shared")
        self.bc_dram = dr("bc_dram", [2, D])
        self.xr1 = dr("xr1", [D, TOK])
        self.xr2 = dr("xr2", [D, TOK])
        self.xr3 = dr("xr3", [D, TOK])
        self.dbg = nc.dram_tensor("dbg", [128, NCH * TOK], BF16, kind="ExternalOutput")
        self.dbg2 = nc.dram_tensor("dbg2", [1, 64], F32, kind="ExternalOutput")
        self.il = {
            "onesbfr": nc.inline_tensor(np.ones((1, 128), dtype=BF), name="c_obfr"),
            "onesfr": nc.inline_tensor(np.ones((1, 128), dtype=np.float32), name="c_ofr"),
            "onesbfc": nc.inline_tensor(np.ones((128, 1), dtype=BF), name="c_obfc"),
            "onesfc": nc.inline_tensor(np.ones((128, 1), dtype=np.float32), name="c_ofc"),
            "n128": nc.inline_tensor(np.full((1, 128), -128.0, dtype=np.float32), name="c_n128"),
            "ident": nc.inline_tensor(np.eye(128, dtype=BF), name="c_id"),
        }

    def on(self, e):
        return getattr(self.blk, e)

    def barrier(self, engines=None):
        engines = ("tensor", "scalar", "vector", "gpsimd", "sync")
        if not hasattr(self, "_brt_v"):
            return self._barrier0(engines)
        snap = {}
        P = self.P
        for e in engines:
            def emit(eng, e=e):
                if e == "gpsimd":
                    eng.wait_ge(self.GD.h, self.GD.n)
                    for k in (0, 1, 4):
                        eng.wait_ge(self.DS[k].h, self.DS[k].n)
                    ins = eng.nop()
                elif e == "sync":
                    eng.wait_ge(self.LD.h, self.LD.n)
                    for k in (2, 3, 5):
                        eng.wait_ge(self.DS[k].h, self.DS[k].n)
                    ins = eng.nop()
                elif e == "tensor":
                    eng.wait_ge(self.LD.h, self._const_ld)
                    ins = eng.ldweights(P["onesbfr"][0:1, 0:1])
                elif e == "scalar":
                    eng.wait_ge(self.AS.h, self._brt_a)
                    ins = eng.copy(P["brt"][0:1, 1:2], P["brt"][0:1, 1:2])
                else:
                    eng.wait_ge(self.VS.h, self._brt_v)
                    ins = eng.tensor_copy(P["brt"][0:1, 2:3], P["brt"][0:1, 2:3])
                self.BARS[e].inc(ins)
            self.on(e)(emit)
            snap[e] = self.BARS[e].n
        for e in engines:
            def emit2(eng, e=e):
                for o in engines:
                    if o != e:
                        eng.wait_ge(self.BARS[o].h, snap[o])
            self.on(e)(emit2)

    def _barrier0(self, engines):
        snap = {}
        for e in engines:
            def emit(eng, e=e):
                if e == "gpsimd":
                    eng.wait_ge(self.GD.h, self.GD.n)
                if e == "sync":
                    eng.wait_ge(self.LD.h, self.LD.n)
                self.BARS[e].inc(eng.nop())
            self.on(e)(emit)
            snap[e] = self.BARS[e].n
        for e in engines:
            def emit2(eng, e=e):
                for o in engines:
                    if o != e:
                        eng.wait_ge(self.BARS[o].h, snap[o])
            self.on(e)(emit2)

    def build(self):
        nc = self.nc
        es = ExitStack()
        with es:
            self.es = es
            for nm in ["LD", "GD", "CC", "TS", "AS", "VS",
                       "BT", "BA", "BV", "BG", "BS"]:
                setattr(self, nm if len(nm) == 2 else nm, Sem(es.enter_context(nc.semaphore(nm))))
            self.DS = [Sem(es.enter_context(nc.semaphore(f"DS{i}"))) for i in range(6)]
            self.CCs = self.CC
            self.BARS = {"tensor": self.BT, "scalar": self.BA, "vector": self.BV,
                         "gpsimd": self.BG, "sync": self.BS}
            sb = lambda n, s, d: es.enter_context(nc.sbuf_tensor(n, s, d))
            self.sb = sb

            P = {}
            for n, s, d in [
                ("onesbfr", [1, 128], BF16), ("onesfr", [1, 128], F32),
                ("onesbfc", [128, 1], BF16), ("onesfc", [128, 1], F32),
                ("n128", [1, 128], F32), ("ident", [128, 128], BF16),
                ("bq", [128, 4], F32), ("bk", [128, 1], F32),
                ("bv", [1, 128], F32), ("bo", [128, 16], F32),
                ("ing", [128, 16], F32), ("inb", [128, 16], F32),
                ("gb", [128, 16], F32),
                ("sg", [128, OCH1], F32), ("su", [128, OCH1], F32),
                ("sd", [128, 16], F32),
                ("r_sum", [1, TOK], F32), ("r_sq", [1, TOK], F32),
                ("r_A", [1, TOK], F32), ("r_B", [1, TOK], F32),
                ("r_t0", [1, TOK], F32), ("xsum", [1, TOK], F32),
                ("ones512", [1, 512], F32),
                ("bcpp", [128, 16], F32), ("epsc", [128, 2], F32),
                ("om", [1, 1], F32),
                ("x16", [128, 16], F32), ("x16b", [128, 16], BF16),
                ("brt", [1, 8], F32),
            ]:
                P[n] = sb("P_" + n, s, d)
            self.P = P

            self.blk = es.enter_context(nc.Block())

            def loads(sync):
                pairs = [
                    (P["onesbfr"][:], self.il["onesbfr"].ap()),
                    (P["onesfr"][:], self.il["onesfr"].ap()),
                    (P["onesbfc"][:], self.il["onesbfc"].ap()),
                    (P["onesfc"][:], self.il["onesfc"].ap()),
                    (P["n128"][:], self.il["n128"].ap()),
                    (P["ident"][:], self.il["ident"].ap()),
                    (P["bq"][:], self.inp["bq_pp"].ap()),
                    (P["bk"][:], self.inp["bk_pp"].ap()),
                    (P["bv"][:], self.inp["bv_row"].ap()),
                    (P["bo"][:], self.inp["bo_pp"].ap()),
                    (P["ing"][:], self.inp["ing_pp"].ap()),
                    (P["inb"][:], self.inp["inb_pp"].ap()),
                    (P["gb"][:], self.inp["gb_pp"].ap()),
                    (P["sg"][:], self.inp["sg_pp"].ap()),
                    (P["su"][:], self.inp["su_pp"].ap()),
                    (P["sd"][:], self.inp["sd_pp"].ap()),
                    (P["om"][:], self.inp["own_mask"].ap()),
                ]
                for dst, src_ in pairs:
                    self.LD.inc(sync.dma_start(out=dst, in_=src_), 16)
            self.on("sync")(loads)

            def fill1(v):
                v.memset(P["ones512"][:], 1.0)
                v.memset(P["epsc"][:, 0:1], 1e-5)
                v.memset(P["epsc"][:, 1:2], 1e-6)
                self.VS.inc(v.memset(P["brt"][0:1, 2:3], 0.0))
            self.on("vector")(fill1)
            self._brt_v = self.VS.n

            def fill2(a):
                self.AS.inc(a.memzero(P["brt"][0:1, 1:2]))
            self.on("scalar")(fill2)
            self._brt_a = self.AS.n
            self._const_ld = self.LD.n
            self.barrier()

            if _TRUNC == 10:
                self.emit_out_from_dram(self.xr1)
                return nc
            self.phase_a()
            if _TRUNC in (12, 13, 14):
                self.emit_out_from_dram(self.xr1)
                return nc
            with ExitStack() as mid:
                M = {}
                for n, s, d in [
                    ("th", [2, D], F32), ("tht", [2, D], F32),
                    ("thb", [2, D], BF16), ("arld", [2, D], F32),
                    ("rowtmp", [1, D], F32), ("yth", [2, 1024], F32),
                    ("ythb", [2, 1024], BF16), ("thfm", [128, 2 * NCH], BF16),
                    ("ythfm", [128, 8 * 2], BF16), ("d1b", [1, 1024], F32),
                ]:
                    M[n] = mid.enter_context(nc.sbuf_tensor("M_" + n, s, d))
                self.M = M

                def loadmid(sync):
                    for dst, src_ in [
                        (M["d1b"][:], self.inp["d1b_own"].ap()),
                    ]:
                        self.LD.inc(sync.dma_start(out=dst, in_=src_), 16)
                self.on("sync")(loadmid)
                self.barrier(["sync", "vector", "tensor", "scalar", "gpsimd"])
                self.phase_b()
                if _TRUNC in (1, 11, 15, 16, 17):
                    self.emit_out_from_dram(self.xr1)
                    return nc
                self.phase_c1()
            if _TRUNC == 21:
                self.emit_out_from_dram(self.xr1)
                return nc
            self.phase_c2()
            if _TRUNC == 2:
                self.emit_out_from_dram(self.xr2)
                return nc
            self.phase_d()
            if _TRUNC in (31, 32, 33, 41, 42):
                self.emit_out_from_dram(self.xr2)
                if _TRUNC == 31:
                    self.emit_dbg_from_dram(self.xr1)
                return nc
            if _TRUNC == 3:
                self.emit_out_from_dram(self.xr3)
                return nc
            self.phase_e()
        return nc

    def emit_out_from_dram(self, src):
        nc = self.nc
        with nc.sbuf_tensor(f"outld{self.uid()}", [128, NCH * TOK // 2], F32) as t0:
            def go(sync):
                for hf in range(2):
                    self.LD.inc(sync.dma_start(
                        out=t0[:].rearrange("p (c t) -> p c t", c=NCH),
                        in_=bass.AP(src, hf * 512,
                                    [[TOK, 128], [128 * TOK, NCH], [1, 512]])), 16)
                    sync.wait_ge(self.LD.h, self.LD.n)
                    self.LD.inc(sync.dma_start(
                        out=bass.AP(self.y, hf * 512,
                                    [[TOK, 128], [128 * TOK, NCH], [1, 512]]),
                        in_=t0[:].rearrange("p (c t) -> p c t", c=NCH)), 16)
                    sync.wait_ge(self.LD.h, self.LD.n)
            self.on("sync")(go)
            self.barrier()

    def emit_dbg_from_dram(self, src):
        nc = self.nc
        with nc.sbuf_tensor(f"dbgld{self.uid()}", [128, NCH * TOK // 2], F32) as t0:
            for hf in range(2):
                def go(sync, hf=hf):
                    self.LD.inc(sync.dma_start(
                        out=t0[:].rearrange("p (c t) -> p c t", c=NCH // 2),
                        in_=bass.AP(src, hf * (NCH // 2) * 128 * TOK,
                                    [[TOK, 128], [128 * TOK, NCH // 2], [1, TOK]])), 16)

                def go2(gp, hf=hf):
                    self.GD.inc(gp.dma_start(
                        out=self.dbg.ap()[:, hf * 8192:(hf + 1) * 8192],
                        in_=t0[:]), 16)
                self.on("sync")(go)
                self.barrier()
                self.on("gpsimd")(go2)
                self.barrier()

    def emit_resid_add(self, ph, x_sb, dram, T=TOK, col_off=0):
        nc, P = self.nc, self.P
        LD, VS = self.LD, self.VS
        rt = [ph.enter_context(nc.sbuf_tensor(f"rres{self.uid()}_{i}", [128, 512], F32))
              for i in range(2)]
        vs0 = VS.n
        rs_ = [self.DS[2], self.DS[3]]
        rsb = [rs_[0].n, rs_[1].n]
        steps = [(dc, tb) for dc in range(NCH) for tb in range(T // 512)]

        def r_sync(sync):
            for i, (dc, tb) in enumerate(steps):
                if i >= 2:
                    sync.wait_ge(VS.h, vs0 + i - 1)
                rs_[i % 2].inc(sync.dma_start(
                    out=rt[i % 2][:],
                    in_=bass.AP(dram, dc * 128 * TOK + col_off + tb * 512,
                                [[TOK, 128], [1, 512]])), 16)

        def r_v(v):
            for i, (dc, tb) in enumerate(steps):
                v.wait_ge(rs_[i % 2].h, rsb[i % 2] + 16 * (i // 2 + 1))
                dst = x_sb[:, dc * T + tb * 512: dc * T + tb * 512 + 512]
                VS.inc(v.tensor_tensor(dst, dst, rt[i % 2][:], ALU.add))
        self.on("sync")(r_sync)
        self.on("vector")(r_v)
        self.barrier(["vector", "sync", "tensor", "scalar"])

    _uid = [0]

    def uid(self):
        KB._uid[0] += 1
        return KB._uid[0]

    def emit_ln(self, ph, x_sb, T, g_row, b_row, mirror_bf=None, spill=None,
                eps=1e-5, out_dram=None):
        nc, P = self.nc, self.P
        TS, AS, VS, LD = self.TS, self.AS, self.VS, self.LD
        ntb = T // 512
        u = self.uid()
        with nc.psum_tensor(f"ln_st{u}", [128, 2 * 512], F32) as pst, \
             nc.psum_tensor(f"ln_ab{u}", [128, 4 * 512], F32) as pab, \
             nc.sbuf_tensor(f"ln_sq0{u}", [128, 512], BF16) as sq0, \
             nc.sbuf_tensor(f"ln_sq1{u}", [128, 512], BF16) as sq1:
            sq = [sq0, sq1]
            ts0, as0, vs0 = TS.n, AS.n, VS.n

            def xt(dc, tb):
                return x_sb[:, dc * T + tb * 512: dc * T + tb * 512 + 512]

            def l_a(a):
                for tb in range(ntb):
                    for dc in range(NCH):
                        i = tb * NCH + dc
                        if i >= 2:
                            a.wait_ge(TS.h, ts0 + i - 1)
                        AS.inc(a.activation(sq[i % 2][:], xt(dc, tb), ACTF.Square))

            def l_t(t):
                for tb in range(ntb):
                    if tb >= 1:
                        t.wait_ge(VS.h, vs0 + tb)
                    for dc in range(NCH):
                        i = tb * NCH + dc
                        t.matmul(pst[0:1, 0:512], P["onesfc"][:], xt(dc, tb),
                                 start=(dc == 0), stop=(dc == NCH - 1))
                        t.wait_ge(AS.h, as0 + i + 1)
                        ins = t.matmul(pst[0:1, 512:1024], P["onesbfc"][:],
                                       sq[i % 2][:], start=(dc == 0),
                                       stop=(dc == NCH - 1))
                        TS.inc(ins)
            self.on("scalar")(l_a)
            self.on("tensor")(l_t)
            TS.n = ts0 + ntb * NCH
            AS.n = as0 + ntb * NCH

            def l_v(v):
                for tb in range(ntb):
                    v.wait_ge(TS.h, ts0 + (tb + 1) * NCH)
                    sl = slice(tb * 512, tb * 512 + 512)
                    v.tensor_scalar_mul(P["r_sum"][0:1, sl], pst[0:1, 0:512], 1.0 / D)
                    v.tensor_scalar_mul(P["r_sq"][0:1, sl], pst[0:1, 512:1024], 1.0 / D)
                    v.tensor_tensor(P["r_t0"][0:1, sl], P["r_sum"][0:1, sl],
                                    P["r_sum"][0:1, sl], ALU.mult)
                    VS.inc(v.tensor_tensor(P["r_t0"][0:1, sl], P["r_sq"][0:1, sl],
                                           P["r_t0"][0:1, sl], ALU.subtract))
            self.on("vector")(l_v)
            vs1 = vs0 + ntb
            VS.n = vs1

            eci = 0 if eps == 1e-5 else 1

            def l_a2(a):
                for tb in range(ntb):
                    a.wait_ge(VS.h, vs0 + tb + 1)
                    sl = slice(tb * 512, tb * 512 + 512)
                    AS.inc(a.activation(P["r_t0"][0:1, sl], P["r_t0"][0:1, sl],
                                        ACTF.Sqrt, bias=P["epsc"][0:1, eci:eci + 1]))
            self.on("scalar")(l_a2)
            asn = AS.n

            def l_v2(v):
                for tb in range(ntb):
                    v.wait_ge(AS.h, asn - ntb + tb + 1)
                    sl = slice(tb * 512, tb * 512 + 512)
                    v.reciprocal(P["r_A"][0:1, sl], P["r_t0"][0:1, sl])
                    v.tensor_tensor(P["r_B"][0:1, sl], P["r_sum"][0:1, sl],
                                    P["r_A"][0:1, sl], ALU.mult)
                    VS.inc(v.tensor_scalar_mul(P["r_B"][0:1, sl],
                                               P["r_B"][0:1, sl], -1.0))
            self.on("vector")(l_v2)
            vs2 = VS.n

            steps = [(dc, tb) for tb in range(ntb) for dc in range(NCH)]
            ts_ap0, vs_ap0 = TS.n, VS.n
            k_per = 2 if mirror_bf is not None else 1

            def l_t2(t):
                for i, (dc, tb) in enumerate(steps):
                    t.wait_ge(VS.h, vs1 + tb + 1)
                    if i >= 2:
                        t.wait_ge(VS.h, vs_ap0 + (i - 1) * k_per)
                    sl = slice(tb * 512, tb * 512 + 512)
                    s = (i % 2) * 1024
                    if g_row is None:
                        t.matmul(pab[:, s:s + 512], P["onesfr"][:],
                                 P["r_A"][0:1, sl], start=True, stop=True)
                        ins = t.matmul(pab[:, s + 512:s + 1024], P["onesfr"][:],
                                       P["r_B"][0:1, sl], start=True, stop=True)
                    else:
                        t.matmul(pab[:, s:s + 512], g_row[0:1, dc * 128:(dc + 1) * 128],
                                 P["r_A"][0:1, sl], start=True, stop=True)
                        t.matmul(pab[:, s + 512:s + 1024],
                                 g_row[0:1, dc * 128:(dc + 1) * 128],
                                 P["r_B"][0:1, sl], start=True, stop=False)
                        ins = t.matmul(pab[:, s + 512:s + 1024],
                                       b_row[0:1, dc * 128:(dc + 1) * 128],
                                       P["ones512"][0:1, :], start=False, stop=True)
                    TS.inc(ins)

            def l_v3(v):
                for i, (dc, tb) in enumerate(steps):
                    v.wait_ge(TS.h, ts_ap0 + i + 1)
                    s = (i % 2) * 1024
                    x_ = xt(dc, tb)
                    v.tensor_tensor(x_, x_, pab[:, s:s + 512], ALU.mult)
                    ins = v.tensor_tensor(x_, x_, pab[:, s + 512:s + 1024], ALU.add)
                    if mirror_bf is not None:
                        VS.inc(ins)
                        mt = mirror_bf[:, dc * T + tb * 512: dc * T + tb * 512 + 512]
                        ins = v.tensor_copy(mt, x_)
                    VS.inc(ins)
            self.on("tensor")(l_t2)
            self.on("vector")(l_v3)
            TS.n = ts_ap0 + len(steps)
            VS.n = vs_ap0 + len(steps) * k_per
            self.barrier()
            if spill is not None:
                def l_sp(sync):
                    LD.inc(sync.dma_start(
                        out=bass.AP(spill, 0, [[T, 128], [128 * T, NCH], [1, T]]),
                        in_=x_sb[:, 0:NCH * T].rearrange("p (c t) -> p c t", c=NCH)), 16)
                self.on("sync")(l_sp)

    def proj_fm(self, tag, w_dram, w_rows, w_cols, noc, rhs_fn, ntb, tbw,
                act_epi, vec_epi=None, corr_row=None):
        nc, P = self.nc, self.P
        TS, AS, VS, GD = self.TS, self.AS, self.VS, self.GD
        ndc = w_rows // 128
        oc_pass = max(1, 8 // ntb) if tbw == 512 else 8
        passes = [(i, min(oc_pass, noc - i)) for i in range(0, noc, oc_pass)]
        with nc.psum_tensor(f"{tag}_ps", [128, 8 * 512], F32) as psum, \
             nc.sbuf_tensor(f"{tag}_w0", [128, oc_pass * 128], BF16) as w0, \
             nc.sbuf_tensor(f"{tag}_w1", [128, oc_pass * 128], BF16) as w1:
            wslab = [w0, w1]
            ws = [self.DS[0], self.DS[1]]
            wsb = [ws[0].n, ws[1].n]
            ts0, as0, vs0 = TS.n, AS.n, VS.n
            epp = [ocn * ntb for (_, ocn) in passes]
            mpp = ndc + (1 if corr_row is not None else 0)

            def markval(i):
                pi, dc = i // ndc, i % ndc
                return ts0 + pi * mpp + dc + 1

            def p_dma(gp):
                i = 0
                for pi, (oc0, ocn) in enumerate(passes):
                    for dc in range(ndc):
                        if i >= 2:
                            gp.wait_ge(TS.h, markval(i - 2))
                        src = bass.AP(w_dram, dc * 128 * w_cols + oc0 * 128,
                                      [[w_cols, 128], [1, ocn * 128]])
                        ws[i % 2].inc(gp.dma_start(out=wslab[i % 2][:, 0:ocn * 128],
                                                   in_=src), 16)
                        i += 1

            def p_t(t):
                i = 0
                for pi, (oc0, ocn) in enumerate(passes):
                    if pi >= 1:
                        t.wait_ge(AS.h, as0 + sum(epp[:pi]))
                        if vec_epi is not None:
                            t.wait_ge(VS.h, vs0 + sum(epp[:pi]))
                    for dc in range(ndc):
                        t.wait_ge(ws[i % 2].h, wsb[i % 2] + 16 * (i // 2 + 1))
                        ins = None
                        for tb in range(ntb):
                            rhs = rhs_fn(dc, tb)
                            for oci in range(ocn):
                                ins = t.matmul(psum[:, (oci * ntb + tb) * tbw:(oci * ntb + tb + 1) * tbw],
                                               wslab[i % 2][:, oci * 128:(oci + 1) * 128],
                                               rhs, start=(dc == 0),
                                               stop=(dc == ndc - 1 and corr_row is None))
                        TS.inc(ins)
                        i += 1
                    if corr_row is not None:
                        ins = None
                        for tb in range(ntb):
                            cr = corr_row(tb)
                            for oci in range(ocn):
                                ins = t.matmul(psum[:, (oci * ntb + tb) * tbw:(oci * ntb + tb + 1) * tbw],
                                               P["n128"][:], cr, start=False, stop=True)
                        TS.inc(ins)

            def p_a(a):
                k = 0
                for pi, (oc0, ocn) in enumerate(passes):
                    a.wait_ge(TS.h, ts0 + (pi + 1) * mpp)
                    for oci in range(ocn):
                        for tb in range(ntb):
                            if vec_epi is not None and k >= 2:
                                a.wait_ge(VS.h, vs0 + k - 1)
                            psl = psum[:, (oci * ntb + tb) * tbw:(oci * ntb + tb + 1) * tbw]
                            AS.inc(act_epi(a, oc0 + oci, tb, psl))
                            k += 1
            self.on("gpsimd")(p_dma)
            self.on("tensor")(p_t)
            self.on("scalar")(p_a)
            TS.n = ts0 + len(passes) * mpp
            AS.n = as0 + sum(epp)
            if vec_epi is not None:
                def p_v(v):
                    k = 0
                    for pi, (oc0, ocn) in enumerate(passes):
                        for oci in range(ocn):
                            for tb in range(ntb):
                                k += 1
                                v.wait_ge(AS.h, as0 + k)
                                VS.inc(vec_epi(v, oc0 + oci, tb, k - 1))
                self.on("vector")(p_v)
                VS.n = vs0 + sum(epp)
            self.barrier(["tensor", "scalar", "vector", "gpsimd"])

    def phase_a(self):
        nc, P = self.nc, self.P
        TS, AS, VS, GD, LD = self.TS, self.AS, self.VS, self.GD, self.LD
        with ExitStack() as ph:
            sbA = lambda n, s, d: ph.enter_context(nc.sbuf_tensor(n, s, d))
            xblk = [sbA(f"xblk{i}", [128, NCH * 512], BF16) for i in range(2)]
            wq_sb = sbA("wq_sb", [128, NCH * 512], BF16)
            wk_sb = sbA("wk_sb", [128, NCH * 128], BF16)
            wv_sb = sbA("wv_sb", [128, NCH * 128], BF16)
            k_fm = sbA("k_fm", [128, S], BF16)
            v_tm = sbA("v_tm", [128, 32 * 128], BF16)
            q_fmb = sbA("q_fmb", [128, 4 * S], BF16)
            attnout = sbA("attnout", [128, 4 * S], BF16)
            P_sb = [sbA(f"Psb{i}", [128, 1024], BF16) for i in range(2)]
            rrow = sbA("rrow", [1, 1024], F32)
            masks_l = sbA("masks_l", [128, 4 * 512], BF16)
            rbc_a = sbA("rbc_a", [128, 1024], F32)

            def a_mload(gp):
                GD.inc(gp.dma_start(
                    out=masks_l[:].rearrange("p (m t) -> p m t", m=4),
                    in_=self.inp["masks"].ap().rearrange("m p t -> p m t")), 16)
            self.on("gpsimd")(a_mload)

            def a_wload(gp):
                GD.inc(gp.dma_start(out=wq_sb[:].rearrange("p (c t) -> p c t", c=NCH),
                                    in_=bass.AP(self.inp["wq_own"], 0,
                                                [[512, 128], [128 * 512, NCH], [1, 512]])), 16)
                GD.inc(gp.dma_start(out=wk_sb[:].rearrange("p (c t) -> p c t", c=NCH),
                                    in_=bass.AP(self.inp["wk_own"], 0,
                                                [[128, 128], [128 * 128, NCH], [1, 128]])), 16)
                GD.inc(gp.dma_start(out=wv_sb[:].rearrange("p (c t) -> p c t", c=NCH),
                                    in_=bass.AP(self.inp["wv_own"], 0,
                                                [[128, 128], [128 * 128, NCH], [1, 128]])), 16)
            self.on("gpsimd")(a_wload)
            self.barrier(["gpsimd", "tensor", "scalar", "vector", "sync"])

            with nc.psum_tensor("pqkv", [128, 4 * 512], F32) as pqkv, \
                 nc.psum_tensor("pk1", [128, 512], F32) as pk1, \
                 nc.psum_tensor("pv", [128, 2 * 512], F32) as pv:
                ts0, as0 = TS.n, AS.n
                xs_ = [self.DS[0], self.DS[1]]
                xsb = [xs_[0].n, xs_[1].n]

                def a1_dma(gp):
                    for i in range(8):
                        if i >= 2:
                            gp.wait_ge(TS.h, ts0 + (i - 1) * 6)
                        src = bass.AP(self.inp["x_fm"], i * 512,
                                      [[S, 128], [128 * S, NCH], [1, 512]])
                        xs_[i % 2].inc(gp.dma_start(
                            out=xblk[i % 2][:].rearrange("p (c t) -> p c t", c=NCH),
                            in_=src), 16)

                def a1_t(t):
                    for i in range(8):
                        t.wait_ge(xs_[i % 2].h, xsb[i % 2] + 16 * (i // 2 + 1))
                        if i >= 1:
                            t.wait_ge(AS.h, as0 + 6 * i)
                        xb_ = xblk[i % 2]
                        ins = None
                        for dc in range(NCH):
                            ins = t.matmul(pk1[:, :], wk_sb[:, dc * 128:(dc + 1) * 128],
                                           xb_[:, dc * 512:(dc + 1) * 512],
                                           start=(dc == 0), stop=(dc == NCH - 1))
                        TS.inc(ins)
                        for tt in range(4):
                            if tt >= 2:
                                t.wait_ge(AS.h, as0 + 6 * i + tt)
                            for dc in range(NCH):
                                t.matmul(pv[:, (tt % 2) * 512:(tt % 2) * 512 + 128],
                                         xb_[:, dc * 512 + tt * 128: dc * 512 + tt * 128 + 128],
                                         wv_sb[:, dc * 128:(dc + 1) * 128],
                                         start=(dc == 0), stop=False)
                            ins = t.matmul(pv[:, (tt % 2) * 512:(tt % 2) * 512 + 128],
                                           P["onesfr"][:], P["bv"][:], start=False, stop=True)
                            TS.inc(ins)
                        ins = None
                        for oc in range(4):
                            for dc in range(NCH):
                                ins = t.matmul(pqkv[:, oc * 512:(oc + 1) * 512],
                                               wq_sb[:, dc * 512 + oc * 128: dc * 512 + oc * 128 + 128],
                                               xb_[:, dc * 512:(dc + 1) * 512],
                                               start=(dc == 0), stop=(dc == NCH - 1))
                        TS.inc(ins)

                def a1_a(a):
                    for i in range(8):
                        base = ts0 + i * 6
                        a.wait_ge(TS.h, base + 1)
                        AS.inc(a.add(k_fm[:, i * 512:(i + 1) * 512], pk1[:, :],
                                     P["bk"][:, 0:1]))
                        for tt in range(4):
                            a.wait_ge(TS.h, base + 2 + tt)
                            AS.inc(a.activation(
                                v_tm[:, (i * 4 + tt) * 128:(i * 4 + tt + 1) * 128],
                                pv[:, (tt % 2) * 512:(tt % 2) * 512 + 128],
                                ACTF.Copy))
                        a.wait_ge(TS.h, base + 6)
                        ins = None
                        for oc in range(4):
                            ins = a.add(
                                q_fmb[:, oc * S + i * 512: oc * S + (i + 1) * 512],
                                pqkv[:, oc * 512:(oc + 1) * 512],
                                P["bq"][:, oc:oc + 1])
                        AS.inc(ins)
                self.on("gpsimd")(a1_dma)
                self.on("tensor")(a1_t)
                self.on("scalar")(a1_a)
                TS.n = ts0 + 48
                AS.n = as0 + 48
                self.barrier()
            if _TRUNC == 12:
                return

            with nc.psum_tensor("sc0", [128, 1024], F32) as sc0, \
                 nc.psum_tensor("sc1", [128, 1024], F32) as sc1, \
                 nc.psum_tensor("avp", [128, 1024], F32) as avp, \
                 nc.psum_tensor("sm0", [128, 512], F32) as sm0, \
                 nc.psum_tensor("sm1", [128, 512], F32) as sm1:
                sc = [sc0, sc1]
                plan = []
                tsn, asn, vsn = TS.n, AS.n, VS.n
                for qb in range(8):
                    for hg in range(2):
                        nkt = 4 * qb + 4
                        plan.append({"qb": qb, "hg": hg, "nkt": nkt,
                                     "ts0": tsn, "as0": asn, "vs0": vsn})
                        tsn += nkt + 2
                        asn += nkt + 1
                        vsn += 6

                def a2_t(t):
                    for st in plan:
                        qb, hg, nkt = st["qb"], st["hg"], st["nkt"]
                        qh0 = 2 * hg
                        ins = None
                        t.wait_ge(VS.h, st["vs0"])
                        for kt in range(nkt + 1):
                            if kt < nkt:
                                if kt >= 2:
                                    t.wait_ge(AS.h, st["as0"] + kt - 1)
                                s = sc[kt % 2]
                                for j in range(2):
                                    qh = qh0 + j
                                    ins2 = t.matmul(
                                        s[:, j * 512:(j + 1) * 512],
                                        k_fm[:, kt * 128:(kt + 1) * 128],
                                        q_fmb[:, qh * S + qb * 512: qh * S + (qb + 1) * 512],
                                        start=True, stop=True)
                                TS.inc(ins2)
                            if kt > 0:
                                kk = kt - 1
                                diag = (kk >= nkt - 4)
                                if diag:
                                    t.wait_ge(VS.h, st["vs0"] + (kk - (nkt - 4)) + 1)
                                else:
                                    t.wait_ge(AS.h, st["as0"] + kk + 1)
                                Pb = P_sb[kk % 2]
                                first, last = (kk == 0), (kk == nkt - 1)
                                t.matmul(sm0[0:1, :], P["onesbfc"][:], Pb[:, 0:512],
                                         start=first, stop=last)
                                t.matmul(sm1[0:1, :], P["onesbfc"][:], Pb[:, 512:1024],
                                         start=first, stop=last)
                                t.matmul(avp[:, 0:512], v_tm[:, kk * 128:(kk + 1) * 128],
                                         Pb[:, 0:512], start=first, stop=last)
                                ins = t.matmul(avp[:, 512:1024],
                                               v_tm[:, kk * 128:(kk + 1) * 128],
                                               Pb[:, 512:1024], start=first, stop=last)
                        TS.inc(ins)
                        t.wait_ge(VS.h, st["vs0"] + 5)
                        t.matmul(sc0[:, 0:512], P["onesfr"][:], rrow[0:1, 0:512],
                                 start=True, stop=True)
                        ins = t.matmul(sc0[:, 512:1024], P["onesfr"][:],
                                       rrow[0:1, 512:1024], start=True, stop=True)
                        TS.inc(ins)

                def a2_a(a):
                    for st in plan:
                        for kt in range(st["nkt"]):
                            a.wait_ge(TS.h, st["ts0"] + kt + 1)
                            AS.inc(a.activation(P_sb[kt % 2][:, :], sc[kt % 2][:, :],
                                                ACTF.Exp, scale=SCALE0))
                        a.wait_ge(TS.h, st["ts0"] + st["nkt"] + 2)
                        a.wait_ge(VS.h, st["vs0"])
                        AS.inc(a.activation(rbc_a[:, :], sc0[:, :], ACTF.Copy))

                def a2_v(v):
                    for st in plan:
                        qb, hg, nkt = st["qb"], st["hg"], st["nkt"]
                        qh0 = 2 * hg
                        for di in range(4):
                            kt = nkt - 4 + di
                            v.wait_ge(AS.h, st["as0"] + kt + 1)
                            Pb = P_sb[kt % 2]
                            mk = masks_l[:, di * 512:(di + 1) * 512]
                            v.tensor_tensor(Pb[:, 0:512], Pb[:, 0:512], mk, ALU.mult)
                            VS.inc(v.tensor_tensor(Pb[:, 512:1024], Pb[:, 512:1024],
                                                   mk, ALU.mult))
                        v.wait_ge(TS.h, st["ts0"] + nkt + 1)
                        v.reciprocal(rrow[0:1, 0:512], sm0[0:1, :])
                        VS.inc(v.reciprocal(rrow[0:1, 512:1024], sm1[0:1, :]))
                        v.wait_ge(AS.h, st["as0"] + nkt + 1)
                        ins = None
                        for j in range(2):
                            qh = qh0 + j
                            ins = v.tensor_tensor(
                                attnout[:, qh * S + qb * 512: qh * S + (qb + 1) * 512],
                                avp[:, j * 512:(j + 1) * 512],
                                rbc_a[:, j * 512:(j + 1) * 512], ALU.mult)
                        VS.inc(ins)
                self.on("tensor")(a2_t)
                self.on("scalar")(a2_a)
                self.on("vector")(a2_v)
                TS.n, AS.n, VS.n = tsn, asn, vsn
                self.barrier()
            if _TRUNC == 13:
                return

            def a3(gp):
                GD.inc(gp.dma_start(
                    out=bass.AP(self.ag_att_in, 0, [[S, 128], [128 * S, 4], [1, S]]),
                    in_=attnout[:].rearrange("p (h t) -> p h t", h=4)), 16)
                gp.wait_ge(GD.h, GD.n)
                self.CCs.inc(gp.collective_compute(
                    "AllGather", ALU.bypass, replica_groups=[self.core],
                    ins=[self.ag_att_in.ap()], outs=[self.ag_att_out.ap()]))
                gp.wait_ge(self.CCs.h, self.CCs.n)
            self.on("gpsimd")(a3)
            self.barrier()

    def phase_b(self):
        nc, P = self.nc, self.P
        TS, AS, VS, GD, LD = self.TS, self.AS, self.VS, self.GD, self.LD
        with ExitStack() as ph:
            sbB = lambda n, s, d: ph.enter_context(nc.sbuf_tensor(n, s, d))
            ao_own = sbB("ao_own", [128, NCH * TOK], BF16)
            x1_sb = sbB("x1_sb", [128, NCH * TOK], F32)
            tmpB = [sbB(f"tmpB{i}", [128, 512], F32) for i in range(2)]

            def b_dma(gp):
                pid = nc.partition_id(engines=[EngT.Pool])
                off = (pid // 4) * (4 * 512 * S) + (pid % 4) * TOK
                GD.inc(gp.dma_start(
                    out=ao_own[:].rearrange("p (c t) -> p c t", c=NCH),
                    in_=bass.AP(self.ag_att_out, off,
                                [[S, 128], [512 * S, 4], [128 * S, 4], [1, TOK]])), 16)

            def b_sync(sync):
                pid2 = nc.partition_id(engines=[EngT.SP])
                offx = (pid2 % 4) * TOK
                LD.inc(sync.dma_start(
                    out=x1_sb[:].rearrange("p (c t) -> p c t", c=NCH),
                    in_=bass.AP(self.inp["x_fm"], offx,
                                [[S, 128], [128 * S, NCH], [1, TOK]])), 16)
            self.on("gpsimd")(b_dma)
            self.on("sync")(b_sync)
            self.barrier()
            if _TRUNC == 15:
                def spdbg15(sync):
                    self.LD.inc(sync.dma_start(out=self.dbg.ap()[:, 0:4096],
                                               in_=ao_own[:, 0:4096]), 16)
                self.on("sync")(spdbg15)
                self.barrier()
                return

            def b_epi_a(a, oc, tb, psl):
                return a.activation(tmpB[(oc * 2 + tb) % 2][:], psl, ACTF.Copy)

            if _TRUNC == 16:
                self.proj_fm("bo", self.inp["wo_w"], D, D, 16,
                             lambda dc, tb: ao_own[:, dc * TOK + tb * 512: dc * TOK + tb * 512 + 512],
                             2, 512, b_epi_a)
                return

            def b_epi_v(v, oc, tb, i):
                dst = x1_sb[:, oc * TOK + tb * 512: oc * TOK + tb * 512 + 512]
                return v.scalar_tensor_tensor(dst, tmpB[i % 2][:],
                                              P["bo"][:, oc:oc + 1], dst,
                                              ALU.add, ALU.add)

            self.proj_fm("bo", self.inp["wo_w"], D, D, 16,
                         lambda dc, tb: ao_own[:, dc * TOK + tb * 512: dc * TOK + tb * 512 + 512],
                         2, 512, b_epi_a, b_epi_v)

            if _TRUNC == 11:
                self.barrier()

                def spdbg(sync):
                    self.LD.inc(sync.dma_start(out=self.dbg.ap()[:, 0:4096],
                                               in_=ao_own[:, 0:4096]), 16)
                self.on("sync")(spdbg)

                def sp11(sync):
                    self.LD.inc(sync.dma_start(
                        out=bass.AP(self.xr1, 0, [[TOK, 128], [128 * TOK, NCH], [1, TOK]]),
                        in_=x1_sb[:].rearrange("p (c t) -> p c t", c=NCH)), 16)
                self.on("sync")(sp11)
                self.barrier()
                return
            self.emit_ln(ph, x1_sb, TOK, None, None, spill=self.xr1)
            self.barrier(["sync", "vector", "tensor", "scalar", "gpsimd"])

            if _TRUNC == 1:
                return

            def s_v0(v):
                lastcol = x1_sb[:].rearrange("p (c t) -> p c t", c=NCH)[:, :, 1023:1024]
                VS.inc(v.tensor_copy(P["x16"][:], lastcol.squeeze(2)))
            self.on("vector")(s_v0)
            self.barrier(["vector", "tensor", "scalar"])

            with nc.psum_tensor("psd", [128, 4 * 512], F32) as psd, \
                 nc.psum_tensor("psd2", [16, 16], F32) as psd2:
                def s_a0(a):
                    AS.inc(a.activation(P["x16b"][:], P["x16"][:], ACTF.Square))
                self.on("scalar")(s_a0)

                def s_t0(t):
                    t.wait_ge(AS.h, AS.n)
                    t.matmul(psd2[0:16, 0:1], P["x16"][:], P["onesfc"][:],
                             start=True, stop=True)
                    TS.inc(t.matmul(psd2[0:16, 1:2], P["x16b"][:], P["onesbfc"][:],
                                    start=True, stop=True))
                self.on("tensor")(s_t0)

                s16 = ph.enter_context(nc.sbuf_tensor("s16", [16, 2], F32))
                mu1 = ph.enter_context(nc.sbuf_tensor("mu1", [1, 4], F32))

                def s_v1(v):
                    v.wait_ge(TS.h, TS.n)
                    VS.inc(v.tensor_copy(s16[:], psd2[0:16, 0:2]))
                self.on("vector")(s_v1)
                self.barrier(["vector", "tensor", "scalar"])

                def s_t1(t):
                    TS.inc(t.matmul(psd2[0:1, 2:4], P["onesfc"][0:16, 0:1], s16[:],
                                    start=True, stop=True))
                self.on("tensor")(s_t1)

                def s_v2(v):
                    v.wait_ge(TS.h, TS.n)
                    VS.inc(v.tensor_scalar_mul(mu1[0:1, 0:2], psd2[0:1, 2:4], 1.0 / D))
                    v.wait_ge(VS.h, VS.n)
                    VS.inc(v.tensor_tensor(mu1[0:1, 2:3], mu1[0:1, 0:1],
                                           mu1[0:1, 0:1], ALU.mult))
                    v.wait_ge(VS.h, VS.n)
                    VS.inc(v.tensor_tensor(mu1[0:1, 2:3], mu1[0:1, 1:2],
                                           mu1[0:1, 2:3], ALU.subtract))
                self.on("vector")(s_v2)

                def s_a1(a):
                    a.wait_ge(VS.h, VS.n)
                    AS.inc(a.activation(mu1[0:1, 2:3], mu1[0:1, 2:3], ACTF.Sqrt,
                                        bias=P["epsc"][0:1, 1:2]))
                self.on("scalar")(s_a1)

                def s_v3(v):
                    v.wait_ge(AS.h, AS.n)
                    VS.inc(v.reciprocal(mu1[0:1, 3:4], mu1[0:1, 2:3]))
                self.on("vector")(s_v3)
                self.barrier(["vector", "tensor", "scalar"])

                psb_s = ph.enter_context(nc.sbuf_tensor("psb_s", [128, 2], F32))
                with nc.psum_tensor("psb", [128, 2], F32) as psb:
                    def s_t2(t):
                        t.matmul(psb[:, 0:1], P["onesfr"][:], mu1[0:1, 0:1],
                                 start=True, stop=True)
                        TS.inc(t.matmul(psb[:, 1:2], P["onesfr"][:], mu1[0:1, 3:4],
                                        start=True, stop=True))
                    self.on("tensor")(s_t2)

                    def s_v4(v):
                        v.wait_ge(TS.h, TS.n)
                        VS.inc(v.tensor_copy(psb_s[:], psb[:, :]))
                        v.wait_ge(VS.h, VS.n)
                        VS.inc(v.tensor_scalar(P["x16"][:], P["x16"][:], psb_s[:, 0:1],
                                               psb_s[:, 1:2], ALU.subtract, ALU.mult))
                        v.wait_ge(VS.h, VS.n)
                        VS.inc(v.tensor_tensor(P["x16"][:], P["x16"][:],
                                               P["ing"][:], ALU.mult))
                        v.wait_ge(VS.h, VS.n)
                        VS.inc(v.tensor_tensor(P["x16"][:], P["x16"][:],
                                               P["inb"][:], ALU.add))
                        v.wait_ge(VS.h, VS.n)
                        VS.inc(v.tensor_copy(P["x16b"][:], P["x16"][:]))
                    self.on("vector")(s_v4)
                    self.barrier(["vector", "tensor", "scalar"])
                    if _TRUNC == 17:
                        def s_v4b(v):
                            VS.inc(v.tensor_copy(P["bcpp"][:], P["x16"][:]))
                        self.on("vector")(s_v4b)
                        self.barrier(["vector", "tensor", "scalar"])

                agsl = [ph.enter_context(nc.sbuf_tensor(f"agsl{i}", [128, 512], BF16))
                        for i in range(2)]
                ts0 = TS.n
                ag_ = [self.DS[0], self.DS[1]]
                agb = [ag_[0].n, ag_[1].n]

                def sg_dma(gp):
                    i = 0
                    for dc in range(NCH):
                        for nb4 in range(4):
                            if i >= 2:
                                gp.wait_ge(TS.h, ts0 + i - 1)
                            src = bass.AP(self.inp["agg_w"], dc * 128 * D + nb4 * 512,
                                          [[D, 128], [1, 512]])
                            ag_[i % 2].inc(gp.dma_start(out=agsl[i % 2][:], in_=src), 16)
                            i += 1
                self.on("gpsimd")(sg_dma)

                def sg_t(t):
                    i = 0
                    for dc in range(NCH):
                        for nb4 in range(4):
                            t.wait_ge(ag_[i % 2].h, agb[i % 2] + 16 * (i // 2 + 1))
                            TS.inc(t.matmul(psd[0:1, nb4 * 512:(nb4 + 1) * 512],
                                            P["x16b"][:, dc:dc + 1], agsl[i % 2][:],
                                            start=(dc == 0), stop=(dc == NCH - 1)))
                            i += 1
                self.on("tensor")(sg_t)
                TS.n = ts0 + 64

                def sg_v(v):
                    v.wait_ge(TS.h, TS.n)
                    v.memset(self.M["arld"][:], 0.0)
                    VS.inc(v.tensor_single_scalar(self.M["rowtmp"][0:1, :], psd[0:1, 0:D],
                                                  P["om"][0:1, 0:1], ALU.mult))
                self.on("vector")(sg_v)
                self.barrier(["vector", "tensor", "scalar", "gpsimd"])

                def sg_g2(gp):
                    pid = nc.partition_id(engines=[EngT.Pool])
                    GD.inc(gp.dma_start(out=self.ar_seed_in.ap(), in_=self.M["arld"][:]), 16)
                    gp.wait_ge(GD.h, GD.n)
                    off = (pid // 4) * D
                    GD.inc(gp.dma_start(out=bass.AP(self.ar_seed_in, off, [[D, 1], [1, D]]),
                                        in_=self.M["rowtmp"][0:1, :]), 16)
                    gp.wait_ge(GD.h, GD.n)
                    self.CCs.inc(gp.collective_compute(
                        "AllReduce", ALU.add, replica_groups=[self.core],
                        ins=[self.ar_seed_in.ap()], outs=[self.ar_seed_out.ap()]))
                    gp.wait_ge(self.CCs.h, self.CCs.n)
                    GD.inc(gp.dma_start(out=self.M["arld"][:], in_=self.ar_seed_out.ap()), 16)
                self.on("gpsimd")(sg_g2)
                self.barrier()
                if _TRUNC == 17:
                    def spdbg17(gp):
                        self.GD.inc(gp.dma_start(out=self.dbg.ap()[0:2, 0:D],
                                                 in_=self.M["arld"][:]), 16)
                        self.GD.inc(gp.dma_start(out=self.dbg.ap()[0:1, 2 * D:3 * D],
                                                 in_=self.M["rowtmp"][0:1, :]), 16)
                        self.GD.inc(gp.dma_start(out=self.dbg.ap()[0:1, 3 * D:3 * D + 16],
                                                 in_=P["x16"][0:1, 0:16]), 16)
                        self.GD.inc(gp.dma_start(out=self.dbg.ap()[0:1, 3 * D + 16:3 * D + 32],
                                                 in_=P["x16b"][0:1, 0:16]), 16)
                        self.GD.inc(gp.dma_start(out=self.dbg.ap()[0:1, 3 * D + 32:3 * D + 36],
                                                 in_=mu1[0:1, 0:4]), 16)
                        self.GD.inc(gp.dma_start(out=self.dbg.ap()[0:1, 3 * D + 64:3 * D + 80],
                                                 in_=P["bcpp"][0:1, 0:16]), 16)
                        self.GD.inc(gp.dma_start(out=self.dbg2.ap()[0:1, 0:16],
                                                 in_=P["x16"][0:1, 0:16]), 16)
                        self.GD.inc(gp.dma_start(out=self.dbg2.ap()[0:1, 16:20],
                                                 in_=mu1[0:1, 0:4]), 16)
                        self.GD.inc(gp.dma_start(out=self.dbg2.ap()[0:1, 20:22],
                                                 in_=psb_s[0:1, 0:2]), 16)
                        self.GD.inc(gp.dma_start(out=self.dbg2.ap()[0:1, 32:48],
                                                 in_=P["bcpp"][0:1, 0:16]), 16)
                    self.on("gpsimd")(spdbg17)
                    self.barrier()
                    return



    def phase_c1(self):
        nc, P = self.nc, self.P
        TS, AS, VS, GD, LD = self.TS, self.AS, self.VS, self.GD, self.LD
        with ExitStack() as ph:
            sbC = lambda n, s, d: ph.enter_context(nc.sbuf_tensor(n, s, d))
            d1_sb = sbC("d1_sb", [128, NCH * 1024], BF16)
            d2_sb = sbC("d2_sb", [128, 8 * D], BF16)
            prs = {}
            for nm in ["png2", "pnb2", "d2b2", "aggb2", "bcb2"]:
                prs[nm] = sbC("c_" + nm, [2, D], F32)
            self.c_prs = prs

            def c_ld0(sync):
                for nm in ["png2", "pnb2", "d2b2", "aggb2", "bcb2"]:
                    LD.inc(sync.dma_start(out=prs[nm][:], in_=self.inp[nm].ap()), 16)
            self.on("sync")(c_ld0)

            def c_init(v):
                VS.inc(v.tensor_tensor(self.M["th"][:], self.M["arld"][:],
                                       prs["aggb2"][:], ALU.add))

            def c_ld(gp):
                GD.inc(gp.dma_start(
                    out=d1_sb[:].rearrange("p (c t) -> p c t", c=NCH),
                    in_=bass.AP(self.inp["d1s"], 0,
                                [[1024, 128], [128 * 1024, NCH], [1, 1024]])), 16)
                GD.inc(gp.dma_start(
                    out=d2_sb[:].rearrange("p (c t) -> p c t", c=8),
                    in_=bass.AP(self.inp["d2s"], 0,
                                [[D, 128], [128 * D, 8], [1, D]])), 16)
            self.on("gpsimd")(c_ld)
            self.barrier()
            self.on("vector")(c_init)
            self.barrier(["vector", "tensor", "scalar"])

            with nc.psum_tensor("pth", [128, 4 * 512], F32) as pth, \
                 nc.psum_tensor("ptt", [128, 2 * 128], BF16) as ptt:
                for step in range(4):
                    self.thought_step(ph, pth, ptt, self.M["th"], d1_sb, d2_sb, step)

                self.rowpair_matmul(ph, pth, ptt, self.M["th"], self.inp["bc_w"],
                                    self.M["arld"], bias_rows=prs["bcb2"],
                                    bias_base=0)
                def cb_g(gp):
                    pid = nc.partition_id(engines=[EngT.Pool])
                    GD.inc(gp.dma_start(out=self.bc_dram.ap(), in_=self.M["arld"][:]), 16)
                    gp.wait_ge(GD.h, GD.n)
                    off = (pid // 4) * D
                    GD.inc(gp.dma_start(out=self.M["rowtmp"][0:1, :],
                                        in_=bass.AP(self.bc_dram, off, [[D, 1], [1, D]])), 16)
                self.on("gpsimd")(cb_g)
                self.barrier()

                def cb_t(t):
                    ins = None
                    for c in range(NCH):
                        ins = t.matmul(pth[:, c:c + 1],
                                       self.M["rowtmp"][0:1, c * 128:(c + 1) * 128],
                                       P["onesfr"][0:1, 0:1], start=True, stop=True)
                    TS.inc(ins)
                self.on("tensor")(cb_t)

                def cb_v(v):
                    v.wait_ge(TS.h, TS.n)
                    VS.inc(v.tensor_copy(P["bcpp"][:], pth[:, 0:16]))
                self.on("vector")(cb_v)
                self.barrier(["vector", "tensor", "scalar"])
                if _TRUNC == 21:
                    def spdbg21(gp):
                        self.GD.inc(gp.dma_start(out=self.dbg.ap()[0:2, 0:D],
                                                 in_=self.M["th"][:]), 16)
                        self.GD.inc(gp.dma_start(out=self.dbg.ap()[0:2, D:2 * D],
                                                 in_=self.M["arld"][:]), 16)
                        self.GD.inc(gp.dma_start(out=self.dbg.ap()[0:1, 2 * D:3 * D],
                                                 in_=self.M["rowtmp"][0:1, :]), 16)
                        self.GD.inc(gp.dma_start(out=self.dbg.ap()[0:1, 3 * D:3 * D + 16],
                                                 in_=P["bcpp"][0:1, 0:16]), 16)
                    self.on("gpsimd")(spdbg21)
                    self.barrier()

    def phase_c2(self):
        nc, P = self.nc, self.P
        TS, AS, VS, GD, LD = self.TS, self.AS, self.VS, self.GD, self.LD
        with ExitStack() as ph:
            sbC = lambda n, s, d: ph.enter_context(nc.sbuf_tensor(n, s, d))
            outg_l = sbC("outg_l", [1, D], F32)
            outb_l = sbC("outb_l", [1, D], F32)
            x2_sb = sbC("x2_sb", [128, NCH * TOK], F32)
            x1b_sb = sbC("x1cb_sb", [128, NCH * TOK], BF16)
            gtile = [sbC(f"gt{i}", [128, 512], F32) for i in range(2)]
            xrt = [sbC(f"xrt{i}", [128, 512], F32) for i in range(2)]

            def c2_ld(gp):
                GD.inc(gp.dma_start(
                    out=x1b_sb[:].rearrange("p (c t) -> p c t", c=NCH),
                    in_=bass.AP(self.xr1, 0, [[TOK, 128], [128 * TOK, NCH], [1, TOK]])), 16)

            def c2_ld2(sync):
                LD.inc(sync.dma_start(out=outg_l[:], in_=self.inp["outg_row"].ap()), 16)
                LD.inc(sync.dma_start(out=outb_l[:], in_=self.inp["outb_row"].ap()), 16)
            self.on("gpsimd")(c2_ld)
            self.on("sync")(c2_ld2)
            self.barrier()

            vs_c0 = VS.n
            xr_ = [self.DS[2], self.DS[3]]
            xrb = [xr_[0].n, xr_[1].n]

            def c2_sync(sync):
                k = 0
                for pi in range(4):
                    for oci in range(4):
                        for tb in range(2):
                            oc = pi * 4 + oci
                            if k >= 2:
                                sync.wait_ge(VS.h, vs_c0 + k - 1)
                            xr_[k % 2].inc(sync.dma_start(
                                out=xrt[k % 2][:],
                                in_=bass.AP(self.xr1, oc * 128 * TOK + tb * 512,
                                            [[TOK, 128], [1, 512]])), 16)
                            k += 1

            def g_epi_a(a, oc, tb, psl):
                return a.activation(gtile[(oc * 2 + tb) % 2][:], psl, ACTF.Sigmoid,
                                    bias=P["gb"][:, oc:oc + 1])

            def g_epi_v(v, oc, tb, i):
                v.wait_ge(xr_[i % 2].h, xrb[i % 2] + 16 * (i // 2 + 1))
                dst = x2_sb[:, oc * TOK + tb * 512: oc * TOK + tb * 512 + 512]
                return v.scalar_tensor_tensor(dst, gtile[i % 2][:],
                                              P["bcpp"][:, oc:oc + 1],
                                              xrt[i % 2][:], ALU.mult, ALU.add)

            self.on("sync")(c2_sync)
            self.proj_fm("g0", self.inp["gw_l0"], D, D, 16,
                         lambda dc, tb: x1b_sb[:, dc * TOK + tb * 512: dc * TOK + tb * 512 + 512],
                         2, 512, g_epi_a, g_epi_v)

            self.emit_ln(ph, x2_sb, TOK, outg_l, outb_l, eps=1e-6)
            self.emit_resid_add(ph, x2_sb, self.xr1)
            self.emit_ln(ph, x2_sb, TOK, None, None, spill=self.xr2)
            self.barrier()

    def thought_step(self, ph, pth, ptt, th, d1_sb, d2_sb, step):
        nc, P = self.nc, self.P
        TS, AS, VS, GD = self.TS, self.AS, self.VS, self.GD

        def t_v(v):
            VS.inc(v.reduce_sum(self.M["arld"][0:2, 0:1], th[:], mybir.AxisListType.X))
            v.wait_ge(VS.h, VS.n)
            VS.inc(v.tensor_scalar_mul(self.M["arld"][0:2, 0:1],
                                       self.M["arld"][0:2, 0:1], 1.0 / D))
        self.on("vector")(t_v)

        def t_a(a):
            a.wait_ge(VS.h, VS.n)
            AS.inc(a.activation(self.M["tht"][:], th[:], ACTF.Square))
        self.on("scalar")(t_a)

        def t_v2(v):
            v.wait_ge(AS.h, AS.n)
            VS.inc(v.reduce_sum(self.M["arld"][0:2, 1:2], self.M["tht"][:],
                                mybir.AxisListType.X))
            v.wait_ge(VS.h, VS.n)
            VS.inc(v.tensor_scalar_mul(self.M["arld"][0:2, 1:2],
                                       self.M["arld"][0:2, 1:2], 1.0 / D))
            v.wait_ge(VS.h, VS.n)
            VS.inc(v.tensor_tensor(self.M["arld"][0:2, 2:3], self.M["arld"][0:2, 0:1],
                                   self.M["arld"][0:2, 0:1], ALU.mult))
            v.wait_ge(VS.h, VS.n)
            VS.inc(v.tensor_tensor(self.M["arld"][0:2, 2:3], self.M["arld"][0:2, 1:2],
                                   self.M["arld"][0:2, 2:3], ALU.subtract))
        self.on("vector")(t_v2)

        def t_a2(a):
            a.wait_ge(VS.h, VS.n)
            AS.inc(a.activation(self.M["arld"][0:2, 2:3], self.M["arld"][0:2, 2:3],
                                ACTF.Sqrt, bias=P["epsc"][0:2, 1:2]))
        self.on("scalar")(t_a2)

        def t_v3(v):
            v.wait_ge(AS.h, AS.n)
            VS.inc(v.reciprocal(self.M["arld"][0:2, 3:4], self.M["arld"][0:2, 2:3]))
            v.wait_ge(VS.h, VS.n)
            v.tensor_scalar(self.M["tht"][:], th[:], self.M["arld"][0:2, 0:1],
                            self.M["arld"][0:2, 3:4], ALU.subtract, ALU.mult)
            v.tensor_tensor(self.M["tht"][:], self.M["tht"][:], self.c_prs["png2"][:], ALU.mult)
            v.tensor_tensor(self.M["tht"][:], self.M["tht"][:], self.c_prs["pnb2"][:], ALU.add)
            VS.inc(v.tensor_copy(self.M["thb"][:], self.M["tht"][:]))
        self.on("vector")(t_v3)
        self.barrier(["vector", "tensor", "scalar"])

        ts0, as0 = TS.n, AS.n

        def t_tr(t):
            for c in range(NCH):
                if c >= 2:
                    t.wait_ge(AS.h, as0 + c - 1)
                TS.inc(t.transpose(ptt[:, (c % 2) * 128:(c % 2) * 128 + 2],
                                   self.M["thb"][0:2, c * 128:(c + 1) * 128],
                                   P["ident"][0:2, 0:2]))

        def t_te(a):
            for c in range(NCH):
                a.wait_ge(TS.h, ts0 + c + 1)
                AS.inc(a.activation(self.M["thfm"][:, c * 2:(c + 1) * 2],
                                    ptt[:, (c % 2) * 128:(c % 2) * 128 + 2],
                                    ACTF.Copy))
        self.on("tensor")(t_tr)
        self.on("scalar")(t_te)
        TS.n = ts0 + NCH
        AS.n = as0 + NCH
        self.barrier(["tensor", "scalar", "vector"])

        ts1 = TS.n

        def d1_t(t):
            for nb in range(2):
                for dc in range(NCH):
                    t.matmul(pth[0:2, nb * 512:(nb + 1) * 512],
                             self.M["thfm"][:, dc * 2:(dc + 1) * 2],
                             d1_sb[:, dc * 1024 + nb * 512: dc * 1024 + nb * 512 + 512],
                             start=(dc == 0), stop=False)
                TS.inc(t.matmul(pth[0:2, nb * 512:(nb + 1) * 512],
                                P["onesfr"][0:1, 0:2],
                                self.M["d1b"][0:1, nb * 512:(nb + 1) * 512],
                                start=False, stop=True))
        self.on("tensor")(d1_t)
        TS.n = ts1 + 2

        def d1_a(a):
            a.wait_ge(TS.h, TS.n)
            AS.inc(a.activation(self.M["yth"][0:2, :], pth[0:2, 0:1024], ACTF.Gelu))
        self.on("scalar")(d1_a)

        def d1_v(v):
            v.wait_ge(AS.h, AS.n)
            VS.inc(v.tensor_copy(self.M["ythb"][:], self.M["yth"][:]))
        self.on("vector")(d1_v)
        self.barrier(["vector", "tensor", "scalar"])

        ts2, as2 = TS.n, AS.n

        def d2_tr(t):
            for c in range(8):
                if c >= 2:
                    t.wait_ge(AS.h, as2 + c - 1)
                TS.inc(t.transpose(ptt[:, (c % 2) * 128:(c % 2) * 128 + 2],
                                   self.M["ythb"][0:2, c * 128:(c + 1) * 128],
                                   P["ident"][0:2, 0:2]))

        def d2_te(a):
            for c in range(8):
                a.wait_ge(TS.h, ts2 + c + 1)
                AS.inc(a.activation(self.M["ythfm"][:, c * 2:(c + 1) * 2],
                                    ptt[:, (c % 2) * 128:(c % 2) * 128 + 2],
                                    ACTF.Copy))
        self.on("tensor")(d2_tr)
        self.on("scalar")(d2_te)
        TS.n = ts2 + 8
        AS.n = as2 + 8
        self.barrier(["tensor", "scalar"])

        ts3 = TS.n

        def d2_t(t):
            for nb in range(4):
                ins = None
                for hc in range(8):
                    ins = t.matmul(pth[0:2, nb * 512:(nb + 1) * 512],
                                   self.M["ythfm"][:, hc * 2:(hc + 1) * 2],
                                   d2_sb[:, hc * D + nb * 512: hc * D + nb * 512 + 512],
                                   start=(hc == 0), stop=(hc == 7))
                TS.inc(ins)
        self.on("tensor")(d2_t)
        TS.n = ts3 + 4

        def d2_v2(v):
            v.wait_ge(TS.h, TS.n)
            VS.inc(v.tensor_copy(self.M["arld"][:], pth[0:2, 0:2048]))
        self.on("vector")(d2_v2)
        self.barrier(["vector", "tensor", "scalar", "gpsimd"])

        def d2_g(gp):
            GD.inc(gp.dma_start(out=self.ar_th_in.ap(), in_=self.M["arld"][:]), 16)
            gp.wait_ge(GD.h, GD.n)
            self.CCs.inc(gp.collective_compute(
                "AllReduce", ALU.add, replica_groups=[self.core],
                ins=[self.ar_th_in.ap()], outs=[self.ar_th_out.ap()]))
            gp.wait_ge(self.CCs.h, self.CCs.n)
            GD.inc(gp.dma_start(out=self.M["arld"][:], in_=self.ar_th_out.ap()), 16)
        self.on("gpsimd")(d2_g)
        self.barrier()

        def d2_v3(v):
            v.tensor_tensor(th[:], th[:], self.M["arld"][:], ALU.add)
            VS.inc(v.tensor_tensor(th[:], th[:], self.c_prs["d2b2"][:], ALU.add))
        self.on("vector")(d2_v3)
        self.barrier(["vector", "tensor", "scalar", "gpsimd"])

    def rowpair_matmul(self, ph, pth, ptt, th, w_dram, out_row2, bias_rows,
                       bias_base):
        nc, P = self.nc, self.P
        TS, AS, VS, GD = self.TS, self.AS, self.VS, self.GD
        thb = self.M["thb"]

        def r_v(v):
            VS.inc(v.tensor_copy(thb[:], th[:]))
        self.on("vector")(r_v)
        self.barrier(["vector", "tensor", "scalar"])
        ts0, as0 = TS.n, AS.n

        def r_tr(t):
            for c in range(NCH):
                if c >= 2:
                    t.wait_ge(AS.h, as0 + c - 1)
                TS.inc(t.transpose(ptt[:, (c % 2) * 128:(c % 2) * 128 + 2],
                                   thb[0:2, c * 128:(c + 1) * 128],
                                   P["ident"][0:2, 0:2]))

        def r_te(a):
            for c in range(NCH):
                a.wait_ge(TS.h, ts0 + c + 1)
                AS.inc(a.activation(self.M["thfm"][:, c * 2:(c + 1) * 2],
                                    ptt[:, (c % 2) * 128:(c % 2) * 128 + 2],
                                    ACTF.Copy))
        self.on("tensor")(r_tr)
        self.on("scalar")(r_te)
        TS.n = ts0 + NCH
        AS.n = as0 + NCH
        self.barrier(["tensor", "scalar", "gpsimd"])

        wsl = [ph.enter_context(nc.sbuf_tensor(f"rw{i}_{bias_base}", [128, 512], BF16))
               for i in range(2)]
        ts1 = TS.n
        ws_ = [self.DS[0], self.DS[1]]
        wsb = [ws_[0].n, ws_[1].n]

        def r_dma(gp):
            i = 0
            for dc in range(NCH):
                for nb in range(4):
                    if i >= 2:
                        gp.wait_ge(TS.h, ts1 + i - 1)
                    src = bass.AP(w_dram, dc * 128 * D + nb * 512, [[D, 128], [1, 512]])
                    ws_[i % 2].inc(gp.dma_start(out=wsl[i % 2][:], in_=src), 16)
                    i += 1
        self.on("gpsimd")(r_dma)

        def r_t(t):
            i = 0
            for dc in range(NCH):
                for nb in range(4):
                    t.wait_ge(ws_[i % 2].h, wsb[i % 2] + 16 * (i // 2 + 1))
                    TS.inc(t.matmul(pth[0:2, nb * 512:(nb + 1) * 512],
                                    self.M["thfm"][:, dc * 2:(dc + 1) * 2],
                                    wsl[i % 2][:], start=(dc == 0),
                                    stop=(dc == NCH - 1)))
                    i += 1
        self.on("tensor")(r_t)
        TS.n = ts1 + 64

        def r_v2(v):
            v.wait_ge(TS.h, TS.n)
            v.tensor_copy(out_row2[:], pth[0:2, 0:2048])
            VS.inc(v.tensor_tensor(out_row2[:], out_row2[:],
                                   bias_rows[:], ALU.add))
        self.on("vector")(r_v2)
        self.barrier(["vector", "tensor", "scalar", "gpsimd"])

    def phase_d(self):
        nc, P = self.nc, self.P
        TS, AS, VS, GD, LD = self.TS, self.AS, self.VS, self.GD, self.LD
        with ExitStack() as ph:
            attnr = ph.enter_context(nc.sbuf_tensor("attnr", [128, NCH * TOK], BF16))
            with ExitStack() as ph1:
                sbD = lambda n, s, d: ph1.enter_context(nc.sbuf_tensor(n, s, d))
                qf = sbD("qf", [128, NCH * TOK], BF16)
                kf = sbD("kf", [128, NCH * TOK], BF16)
                vt = sbD("vt", [128, 8 * D], BF16)

                with ExitStack() as ph1a:
                    x2b = ph1a.enter_context(
                        nc.sbuf_tensor("x2b", [128, NCH * TOK], BF16))

                    def d_ld(gp):
                        GD.inc(gp.dma_start(
                            out=x2b[:].rearrange("p (c t) -> p c t", c=NCH),
                            in_=bass.AP(self.xr2, 0,
                                        [[TOK, 128], [128 * TOK, NCH], [1, TOK]])), 16)
                    self.on("gpsimd")(d_ld)
                    self.barrier()

                    def mk_epi(dst):
                        def epi(a, oc, tb, psl):
                            return a.activation(
                                dst[:, oc * TOK + tb * 512: oc * TOK + tb * 512 + 512],
                                psl, ACTF.Silu)
                        return epi

                    rhs2 = lambda dc, tb: x2b[:, dc * TOK + tb * 512: dc * TOK + tb * 512 + 512]
                    self.proj_fm("qw", self.inp["qwT"], D, D, 16, rhs2, 2, 512,
                                 mk_epi(qf))
                    self.proj_fm("kw", self.inp["kwT"], D, D, 16, rhs2, 2, 512,
                                 mk_epi(kf))

                    with nc.psum_tensor("pvd2", [128, 4 * 512], F32) as pvd2, \
                         nc.psum_tensor("pvd3", [128, 4 * 512], F32) as pvd3:
                        pvb = [pvd2, pvd3]
                        vsl2 = [ph1a.enter_context(
                            nc.sbuf_tensor(f"vsl2{i}", [128, D], BF16))
                            for i in range(2)]
                        ts0, as0 = TS.n, AS.n
                        vs_ = [self.DS[0], self.DS[1]]
                        vsb = [vs_[0].n, vs_[1].n]

                        def v2_dma(gp):
                            i = 0
                            for tp in range(4):
                                for dc in range(NCH):
                                    if i >= 2:
                                        gp.wait_ge(TS.h, ts0 + i - 1)
                                    src = bass.AP(self.inp["vwT"], dc * 128 * D,
                                                  [[D, 128], [1, D]])
                                    vs_[i % 2].inc(gp.dma_start(out=vsl2[i % 2][:], in_=src), 16)
                                    i += 1

                        def v2_t(t):
                            i = 0
                            for tp in range(4):
                                if tp >= 1:
                                    t.wait_ge(AS.h, as0 + 2 * tp)
                                for dc in range(NCH):
                                    t.wait_ge(vs_[i % 2].h, vsb[i % 2] + 16 * (i // 2 + 1))
                                    ins = None
                                    for half in range(2):
                                        tt = 2 * tp + half
                                        for nb in range(4):
                                            ins = t.matmul(pvb[half][:, nb * 512:(nb + 1) * 512],
                                                           x2b[:, dc * TOK + tt * 128: dc * TOK + tt * 128 + 128],
                                                           vsl2[i % 2][:, nb * 512:(nb + 1) * 512],
                                                           start=(dc == 0), stop=(dc == NCH - 1))
                                    TS.inc(ins)
                                    i += 1

                        def v2_a(a):
                            for tp in range(4):
                                a.wait_ge(TS.h, ts0 + (tp + 1) * NCH)
                                for half in range(2):
                                    tt = 2 * tp + half
                                    ins = None
                                    for nb in range(4):
                                        ins = a.activation(
                                            vt[:, tt * D + nb * 512: tt * D + nb * 512 + 512],
                                            pvb[half][:, nb * 512:(nb + 1) * 512], ACTF.Silu)
                                    AS.inc(ins)
                        self.on("gpsimd")(v2_dma)
                        self.on("tensor")(v2_t)
                        self.on("scalar")(v2_a)
                        TS.n = ts0 + 4 * NCH
                        AS.n = as0 + 8
                        self.barrier()
                    if _TRUNC == 31:
                        return

                with ExitStack() as ph1b:
                    sb2 = lambda n, s, d: ph1b.enter_context(nc.sbuf_tensor(n, s, d))
                    KVb = sb2("KVb", [128, 4 * 1024], BF16)
                    Scur = sb2("Scur", [128, 1024], F32)
                    Cld = sb2("Cld", [128, 1024], F32)
                    ktb = sb2("ktb", [128, 256], BF16)
                    kts = sb2("kts", [128, 256], BF16)
                    dds = [sb2(f"dds{i}", [128, 512], F32) for i in range(2)]
                    qtl = sb2("qtl", [128, TOK], BF16)
                    S4 = sb2("S4", [128, 4 * 64], F32)
                    S4z = sb2("S4z", [128, 2 * 4 * 64], BF16)
                    awbuf = sb2("awbuf", [128, 2 * 512], BF16)
                    kdrow = [sb2(f"kdrow{i}", [1, 512], F32) for i in range(2)]
                    qdrow = [sb2(f"qdrow{i}", [1, 512], F32) for i in range(2)]
                    cfrow = [sb2(f"cfrow{i}", [1, 1024], F32) for i in range(2)]

                    with nc.psum_tensor("pkd", [128, 256], F32) as pkd, \
                         nc.psum_tensor("pktr", [128, 256], F32) as pktr, \
                         nc.psum_tensor("pKV", [128, 64], F32) as pKV:
                        plan = []
                        tsn, asn, vsn = TS.n, AS.n, VS.n
                        kd_ = [self.DS[2], self.DS[3]]
                        kdb = [kd_[0].n, kd_[1].n]
                        for hp in range(16):
                            e = {"hp": hp}
                            e["ld_kd"] = kdb[hp % 2] + 16 * (hp // 2 + 1)
                            tsn += 1
                            e["ts_kd"] = tsn
                            e["vs_kt"] = []
                            e["ts_tr"] = []
                            e["as_ev"] = []
                            e["ts_kv"] = []
                            e["vs_kv"] = []
                            for bl in range(4):
                                vsn += 1
                                e["vs_kt"].append(vsn)
                                tsn += 4
                                e["ts_tr"].append(tsn)
                                asn += 1
                                e["as_ev"].append(asn)
                                tsn += 2
                                e["ts_kv"].append(tsn)
                                vsn += 2
                                e["vs_kv"].append(vsn)
                            plan.append(e)

                        def d1_s(sync):
                            for hi, e in enumerate(plan):
                                if hi >= 2:
                                    sync.wait_ge(TS.h, plan[hi - 2]["ts_kd"])
                                kd_[hi % 2].inc(sync.dma_start(
                                    out=kdrow[hi % 2][:],
                                    in_=bass.AP(self.inp["kd_rows"], e["hp"] * 512,
                                                [[512, 1], [1, 512]])), 16)

                        def d1_t(t):
                            for hi, e in enumerate(plan):
                                hp = e["hp"]
                                t.wait_ge(kd_[hi % 2].h, e["ld_kd"])
                                if hi >= 1:
                                    t.wait_ge(VS.h, plan[hi - 1]["vs_kt"][3])
                                t.matmul(pkd[0:64, :], P["onesfr"][0:1, 0:64],
                                         kdrow[hi % 2][0:1, 0:256],
                                         start=True, stop=True)
                                TS.inc(t.matmul(pkd[64:128, :], P["onesfr"][0:1, 0:64],
                                                kdrow[hi % 2][0:1, 256:512],
                                                start=True, stop=True,
                                                tile_position=(0, 64)))
                                for bl in range(4):
                                    t.wait_ge(VS.h, e["vs_kt"][bl])
                                    for par in range(2):
                                        pr = par * 64
                                        for j in range(2):
                                            TS.inc(t.matmul(
                                                pktr[:, par * 128 + j * 64: par * 128 + (j + 1) * 64],
                                                ktb[:, j * 128:(j + 1) * 128],
                                                P["ident"][:, pr:pr + 64],
                                                start=True, stop=True))
                                    t.wait_ge(AS.h, e["as_ev"][bl])
                                    if hi + bl >= 1:
                                        prevkv = e["vs_kv"][bl - 1] if bl >= 1 \
                                            else plan[hi - 1]["vs_kv"][3]
                                        t.wait_ge(VS.h, prevkv)
                                    for par in range(2):
                                        pr = par * 64
                                        h = 2 * hp + par
                                        ins = None
                                        for j in range(2):
                                            ins = t.matmul(
                                                pKV[pr:pr + 64, 0:64],
                                                kts[:, par * 128 + j * 64: par * 128 + (j + 1) * 64],
                                                vt[:, (bl * 2 + j) * D + h * 64: (bl * 2 + j) * D + h * 64 + 64],
                                                start=(j == 0), stop=(j == 1),
                                                tile_position=(0, pr))
                                        TS.inc(ins)

                        def d1_v(v):
                            for hi, e in enumerate(plan):
                                hp = e["hp"]
                                for bl in range(4):
                                    v.wait_ge(TS.h, e["ts_kd"])
                                    if hi + bl >= 1:
                                        prev = e["ts_tr"][bl - 1] if bl >= 1 \
                                            else plan[hi - 1]["ts_tr"][3]
                                        v.wait_ge(TS.h, prev)
                                    kf_sl = kf[:, hp * TOK + bl * 256: hp * TOK + bl * 256 + 256]
                                    VS.inc(v.tensor_tensor(ktb[:, :], kf_sl,
                                                           pkd[:, :], ALU.mult))
                                    v.wait_ge(TS.h, e["ts_kv"][bl])
                                    for par in range(2):
                                        pr = par * 64
                                        ins = v.tensor_copy(
                                            KVb[pr:pr + 64, bl * 1024 + hp * 64: bl * 1024 + hp * 64 + 64],
                                            pKV[pr:pr + 64, 0:64])
                                        VS.inc(ins)

                        def d1_a(a):
                            for hi, e in enumerate(plan):
                                for bl in range(4):
                                    a.wait_ge(TS.h, e["ts_tr"][bl])
                                    if hi + bl >= 1:
                                        prev = e["ts_kv"][bl - 1] if bl >= 1 \
                                            else plan[hi - 1]["ts_kv"][3]
                                        a.wait_ge(TS.h, prev)
                                    AS.inc(a.activation(kts[:, :], pktr[:, :],
                                                        ACTF.Copy))
                        self.on("sync")(d1_s)
                        self.on("tensor")(d1_t)
                        self.on("vector")(d1_v)
                        self.on("scalar")(d1_a)
                        TS.n, AS.n, VS.n = tsn, asn, vsn
                        self.barrier(["tensor", "scalar", "vector", "sync"])
                        if _TRUNC == 41:
                            return

                        def c_v(v):
                            for h in range(H1):
                                hp, pr = h // 2, (h % 2) * 64
                                sl = slice(hp * 64, hp * 64 + 64)
                                for bl in range(4):
                                    w = float(_BD[h] ** (3 - bl))
                                    srcv = KVb[pr:pr + 64, bl * 1024 + hp * 64: bl * 1024 + hp * 64 + 64]
                                    if bl > 0:
                                        v.wait_ge(VS.h, VS.n)
                                    if bl == 0:
                                        VS.inc(v.tensor_scalar_mul(
                                            Cld[pr:pr + 64, sl], srcv, w))
                                    else:
                                        VS.inc(v.scalar_tensor_tensor(
                                            Cld[pr:pr + 64, sl], srcv, w,
                                            Cld[pr:pr + 64, sl], ALU.mult, ALU.add))
                        self.on("vector")(c_v)
                        self.barrier(["vector", "gpsimd"])

                        def c_g(gp):
                            for par in range(2):
                                GD.inc(gp.dma_start(
                                    out=bass.AP(self.ag_c_in, par * 64,
                                                [[H1 * DH1, 64], [128, 16], [1, 64]]),
                                    in_=Cld[par * 64:par * 64 + 64, :].rearrange(
                                        "p (c e) -> p c e", c=16)), 16)
                            gp.wait_ge(GD.h, GD.n)
                            self.CCs.inc(gp.collective_compute(
                                "AllGather", ALU.bypass, replica_groups=[self.core],
                                ins=[self.ag_c_in.ap()], outs=[self.ag_c_out.ap()]))
                            gp.wait_ge(self.CCs.h, self.CCs.n)
                        self.on("gpsimd")(c_g)
                        self.barrier()
                        if _TRUNC == 42:
                            return

                        vs1 = VS.n
                        cl_ = self.DS[4]
                        clb = cl_.n
                        cf_ = [self.DS[2], self.DS[3]]
                        cfb = [cf_[0].n, cf_[1].n]

                        def s_g2(gp):
                            pid = nc.partition_id(engines=[EngT.Pool])
                            for j in range(3):
                                if j >= 1:
                                    gp.wait_ge(VS.h, vs1 + j)
                                base = ((pid // 4) * 4 + j) * 64 * H1 * DH1
                                for par in range(2):
                                    cl_.inc(gp.dma_start(
                                        out=Cld[par * 64:par * 64 + 64, :].rearrange(
                                            "p (c e) -> p c e", c=16),
                                        in_=bass.AP(self.ag_c_out, base + par * 64,
                                                    [[H1 * DH1, 64], [128, 16], [1, 64]])), 16)

                        with nc.psum_tensor("pcf", [128, 2 * 512], F32) as pcf:
                            ts_cf = [TS.n]
                            cf_marks = [TS.n + k + 1 for k in range(6)]

                            def s_s(sync):
                                k = 0
                                for j in range(3):
                                    for par in range(2):
                                        if k >= 2:
                                            sync.wait_ge(TS.h, cf_marks[(k - 2)])
                                        cf_[k % 2].inc(sync.dma_start(
                                            out=cfrow[k % 2][:],
                                            in_=bass.AP(self.inp["coeff_rows"],
                                                        j * D + par * 1024,
                                                        [[1024, 1], [1, 1024]])), 16)
                                        k += 1

                            def s_t(t):
                                k = 0
                                for j in range(3):
                                    t.wait_ge(cl_.h, clb + 32 * (j + 1))
                                    if j >= 1:
                                        t.wait_ge(VS.h, vs1 + j)
                                    for par in range(2):
                                        pr = par * 64
                                        t.wait_ge(cf_[k % 2].h, cfb[k % 2] + 16 * (k // 2 + 1))
                                        t.matmul(pcf[pr:pr + 64, 0:512],
                                                 P["onesfr"][0:1, 0:64],
                                                 cfrow[k % 2][0:1, 0:512],
                                                 start=True, stop=True,
                                                 tile_position=(0, pr))
                                        TS.inc(t.matmul(pcf[pr:pr + 64, 512:1024],
                                                        P["onesfr"][0:1, 0:64],
                                                        cfrow[k % 2][0:1, 512:1024],
                                                        start=True, stop=True,
                                                        tile_position=(0, pr)))
                                        k += 1

                            def s_v2(v):
                                v.memset(Scur[:, :], 0.0)
                                for j in range(3):
                                    v.wait_ge(TS.h, ts_cf[0] + (j + 1) * 2)
                                    v.tensor_tensor(Cld[:, :], Cld[:, :],
                                                    pcf[:, 0:1024], ALU.mult)
                                    VS.inc(v.tensor_tensor(Scur[:, :], Scur[:, :],
                                                           Cld[:, :], ALU.add))
                            self.on("gpsimd")(s_g2)
                            self.on("sync")(s_s)
                            self.on("tensor")(s_t)
                            self.on("vector")(s_v2)
                            TS.n = ts_cf[0] + 6
                            VS.n = vs1 + 3
                            self.barrier(["tensor", "scalar", "vector", "gpsimd", "sync"])

                    if _TRUNC == 32:
                        return
                    with nc.psum_tensor("pqd2", [128, 256], F32) as pqd2, \
                         nc.psum_tensor("paw0", [128, 512], F32) as paw0, \
                         nc.psum_tensor("paw1", [128, 512], F32) as paw1, \
                         nc.psum_tensor("pip", [128, 2 * 512], F32) as pip:
                        paw = [paw0, paw1]
                        plan = []
                        tsn, asn, vsn = TS.n, AS.n, VS.n
                        qd_ = [self.DS[2], self.DS[3]]
                        qdb = [qd_[0].n, qd_[1].n]
                        dd_ = self.DS[4]
                        ddb = dd_.n
                        for hp in range(16):
                            e = {"hp": hp}
                            e["gd_dd"] = ddb + 32 * (hp + 1)
                            e["ld_qd"] = qdb[hp % 2] + 16 * (hp // 2 + 1)
                            tsn += 1
                            e["ts_qd"] = tsn
                            e["aw"] = [0] * 8
                            e["ip"] = [0] * 8
                            for s in range(8):
                                tsn += 1
                                e["aw"][s] = tsn
                                if s >= 1:
                                    tsn += 1
                                    e["ip"][s - 1] = tsn
                            tsn += 1
                            e["ip"][7] = tsn
                            e["vs0"] = vsn
                            e["qt"] = [vsn + 1 + b for b in range(4)]
                            e["st"] = [vsn + 4 + 4 * (b + 1) for b in range(4)]
                            e["dd"] = [vsn + 20 + 1 + s for s in range(8)]
                            vsn += 28
                            e["ev"] = [asn + 1 + s for s in range(8)]
                            asn += 8
                            plan.append(e)

                        def d2_s(sync):
                            for hi, e in enumerate(plan):
                                if hi >= 2:
                                    sync.wait_ge(TS.h, plan[hi - 2]["ts_qd"])
                                qd_[hi % 2].inc(sync.dma_start(
                                    out=qdrow[hi % 2][:],
                                    in_=bass.AP(self.inp["qd_rows"], e["hp"] * 512,
                                                [[512, 1], [1, 512]])), 16)

                        def d2_g2(gp):
                            for hi, e in enumerate(plan):
                                hp = e["hp"]
                                if hi >= 1:
                                    gp.wait_ge(VS.h, plan[hi - 1]["dd"][7])
                                for par in range(2):
                                    src = bass.AP(self.inp["ddT"],
                                                  (2 * hp + par) * BS1 * BS1,
                                                  [[BS1, 128], [128 * BS1, 2], [1, 256]])
                                    dd_.inc(gp.dma_start(
                                        out=dds[par][:].rearrange("p (j t) -> p j t", j=2),
                                        in_=src), 16)

                        def d2_t2(t):
                            for hi, e in enumerate(plan):
                                hp = e["hp"]
                                t.wait_ge(qd_[hi % 2].h, e["ld_qd"])
                                if hi >= 1:
                                    t.wait_ge(VS.h, plan[hi - 1]["qt"][3])
                                t.matmul(pqd2[0:64, :], P["onesfr"][0:1, 0:64],
                                         qdrow[hi % 2][0:1, 0:256],
                                         start=True, stop=True)
                                TS.inc(t.matmul(pqd2[64:128, :], P["onesfr"][0:1, 0:64],
                                                qdrow[hi % 2][0:1, 256:512],
                                                start=True, stop=True,
                                                tile_position=(0, 64)))
                                def emit_aw(s):
                                    bl, par = s // 2, s % 2
                                    pr = par * 64
                                    if s >= 2:
                                        t.wait_ge(VS.h, e["dd"][s - 2])
                                    ins = None
                                    for j in range(2):
                                        kf_sl = kf[pr:pr + 64,
                                                   hp * TOK + bl * 256 + j * 128:
                                                   hp * TOK + bl * 256 + j * 128 + 128]
                                        ins = t.matmul(
                                            paw[s % 2][:, j * 256:(j + 1) * 256],
                                            kf_sl,
                                            qf[pr:pr + 64,
                                               hp * TOK + bl * 256: hp * TOK + (bl + 1) * 256],
                                            start=True, stop=True)
                                    TS.inc(ins)

                                def emit_ip(s):
                                    bl, par = s // 2, s % 2
                                    pr = par * 64
                                    h = 2 * hp + par
                                    t.wait_ge(VS.h, e["dd"][s])
                                    t.wait_ge(VS.h, e["qt"][bl])
                                    t.wait_ge(VS.h, e["st"][bl])
                                    if s >= 2:
                                        t.wait_ge(AS.h, e["ev"][s - 2])
                                    for j in range(2):
                                        t.matmul(
                                            pip[pr:pr + 64, (s % 2) * 512:(s % 2) * 512 + 256],
                                            vt[:, (bl * 2 + j) * D + h * 64: (bl * 2 + j) * D + h * 64 + 64],
                                            awbuf[:, (s % 2) * 512 + j * 256:(s % 2) * 512 + (j + 1) * 256],
                                            start=(j == 0), stop=False,
                                            tile_position=(0, pr))
                                    ins = t.matmul(
                                        pip[pr:pr + 64, (s % 2) * 512:(s % 2) * 512 + 256],
                                        S4z[:, par * 256 + bl * 64: par * 256 + (bl + 1) * 64],
                                        qtl[:, bl * 256:(bl + 1) * 256],
                                        start=False, stop=True,
                                        tile_position=(0, pr))
                                    TS.inc(ins)

                                for s in range(8):
                                    emit_aw(s)
                                    if s >= 1:
                                        emit_ip(s - 1)
                                emit_ip(7)

                        def d2_v(v):
                            v.memset(S4z[:], 0.0)
                            for hi, e in enumerate(plan):
                                hp = e["hp"]
                                v.wait_ge(TS.h, e["ts_qd"])
                                if hi >= 1:
                                    v.wait_ge(TS.h, plan[hi - 1]["ip"][7])
                                for bl in range(4):
                                    qsrc = qf[:, hp * TOK + bl * 256: hp * TOK + bl * 256 + 256]
                                    VS.inc(v.tensor_tensor(
                                        qtl[:, bl * 256:(bl + 1) * 256], qsrc,
                                        pqd2[:, :], ALU.mult))
                                stb = e["vs0"] + 4
                                sk = 0
                                for bl in range(4):
                                    for par in range(2):
                                        pr = par * 64
                                        h = 2 * hp + par
                                        if bl == 0:
                                            ins = v.tensor_copy(
                                                S4[pr:pr + 64, 0:64],
                                                Scur[pr:pr + 64, hp * 64:(hp + 1) * 64])
                                        else:
                                            ins = v.scalar_tensor_tensor(
                                                S4[pr:pr + 64, bl * 64:(bl + 1) * 64],
                                                S4[pr:pr + 64, (bl - 1) * 64: bl * 64],
                                                float(_BD[h]),
                                                KVb[pr:pr + 64, (bl - 1) * 1024 + hp * 64: (bl - 1) * 1024 + hp * 64 + 64],
                                                ALU.mult, ALU.add)
                                        VS.inc(ins)
                                        v.wait_ge(VS.h, stb + 2 * sk + 1)
                                        VS.inc(v.tensor_copy(
                                            S4z[pr:pr + 64, par * 256 + bl * 64: par * 256 + (bl + 1) * 64],
                                            S4[pr:pr + 64, bl * 64:(bl + 1) * 64]))
                                        sk += 1
                                for s in range(8):
                                    bl, par = s // 2, s % 2
                                    v.wait_ge(TS.h, e["aw"][s])
                                    v.wait_ge(dd_.h, e["gd_dd"])
                                    if s >= 2:
                                        v.wait_ge(TS.h, e["ip"][s - 2])
                                    VS.inc(v.tensor_tensor(
                                        awbuf[:, (s % 2) * 512:(s % 2) * 512 + 512],
                                        paw[s % 2][:, :], dds[par][:, :],
                                        ALU.mult))

                        def d2_a(a):
                            for e in plan:
                                hp = e["hp"]
                                for s in range(8):
                                    bl, par = s // 2, s % 2
                                    pr = par * 64
                                    a.wait_ge(TS.h, e["ip"][s])
                                    AS.inc(a.activation(
                                        attnr[pr:pr + 64,
                                              hp * TOK + bl * 256: hp * TOK + bl * 256 + 256],
                                        pip[pr:pr + 64, (s % 2) * 512:(s % 2) * 512 + 256],
                                        ACTF.Copy))
                        self.on("sync")(d2_s)
                        self.on("gpsimd")(d2_g2)
                        self.on("tensor")(d2_t2)
                        self.on("vector")(d2_v)
                        self.on("scalar")(d2_a)
                        TS.n, AS.n, VS.n = tsn, asn, vsn
                        self.barrier()
                    if _TRUNC == 33:
                        def spdbg3(sync):
                            self.LD.inc(sync.dma_start(out=self.dbg.ap(),
                                                       in_=attnr[:, :]), 16)
                        self.on("sync")(spdbg3)
                        self.barrier()
                        return

            with ExitStack() as ph3:
                sb3 = lambda n, s, d: ph3.enter_context(nc.sbuf_tensor(n, s, d))
                nwpp = sb3("nwpp", [128, 16], F32)
                x3_sb = sb3("x3_sb", [128, NCH * TOK], F32)
                ga_sb = sb3("ga_sb", [128, NCH * TOK], BF16)
                x2b2 = sb3("x2b2", [128, NCH * TOK], BF16)
                gt3 = [sb3(f"gt3{i}", [128, 512], F32) for i in range(2)]

                def d3_ld(gp):
                    GD.inc(gp.dma_start(
                        out=x2b2[:].rearrange("p (c t) -> p c t", c=NCH),
                        in_=bass.AP(self.xr2, 0,
                                    [[TOK, 128], [128 * TOK, NCH], [1, TOK]])), 16)

                def d3_ld2(sync):
                    LD.inc(sync.dma_start(out=nwpp[:], in_=self.inp["nw_pp"].ap()), 16)
                self.on("gpsimd")(d3_ld)
                self.on("sync")(d3_ld2)
                self.barrier()

                with nc.psum_tensor("prs", [128, 512], F32) as prs, \
                     nc.sbuf_tensor("sqd0", [128, 512], BF16) as sqd0, \
                     nc.sbuf_tensor("sqd1", [128, 512], BF16) as sqd1:
                    sqd = [sqd0, sqd1]
                    ts0, as0, vs0 = TS.n, AS.n, VS.n

                    def r_a(a):
                        for tb in range(2):
                            for dc in range(NCH):
                                i = tb * NCH + dc
                                if i >= 2:
                                    a.wait_ge(TS.h, ts0 + i - 1)
                                AS.inc(a.activation(
                                    sqd[i % 2][:],
                                    attnr[:, dc * TOK + tb * 512: dc * TOK + tb * 512 + 512],
                                    ACTF.Square))

                    def r_t(t):
                        for tb in range(2):
                            if tb >= 1:
                                t.wait_ge(VS.h, vs0 + tb)
                            for dc in range(NCH):
                                i = tb * NCH + dc
                                t.wait_ge(AS.h, as0 + i + 1)
                                TS.inc(t.matmul(prs[0:1, :], P["onesbfc"][:],
                                                sqd[i % 2][:], start=(dc == 0),
                                                stop=(dc == NCH - 1)))

                    def r_v(v):
                        for tb in range(2):
                            v.wait_ge(TS.h, ts0 + (tb + 1) * NCH)
                            sl = slice(tb * 512, tb * 512 + 512)
                            VS.inc(v.tensor_scalar_mul(P["r_sq"][0:1, sl],
                                                       prs[0:1, :], 1.0 / D))
                    self.on("scalar")(r_a)
                    self.on("tensor")(r_t)
                    self.on("vector")(r_v)
                    TS.n = ts0 + 2 * NCH
                    AS.n = as0 + 2 * NCH
                    VS.n = vs0 + 2
                    self.barrier(["tensor", "scalar", "vector"])

                def r_a2(a):
                    for tb in range(2):
                        sl = slice(tb * 512, tb * 512 + 512)
                        AS.inc(a.activation(P["r_t0"][0:1, sl], P["r_sq"][0:1, sl],
                                            ACTF.Sqrt, bias=P["epsc"][0:1, 0:1]))
                self.on("scalar")(r_a2)

                def r_v2(v):
                    v.wait_ge(AS.h, AS.n)
                    for tb in range(2):
                        sl = slice(tb * 512, tb * 512 + 512)
                        VS.inc(v.reciprocal(P["r_A"][0:1, sl], P["r_t0"][0:1, sl]))
                self.on("vector")(r_v2)
                self.barrier(["tensor", "scalar", "vector"])

                rbc_sb = ph3.enter_context(nc.sbuf_tensor("rbc_sb", [128, 1024], F32))
                with nc.psum_tensor("prb", [128, 2 * 512], F32) as prb:
                    def nb_t(t):
                        ins = None
                        for tb in range(2):
                            ins = t.matmul(prb[:, tb * 512:(tb + 1) * 512],
                                           P["onesfr"][:],
                                           P["r_A"][0:1, tb * 512:(tb + 1) * 512],
                                           start=True, stop=True)
                        TS.inc(ins)
                    self.on("tensor")(nb_t)

                    def nb_v(v):
                        v.wait_ge(TS.h, TS.n)
                        VS.inc(v.tensor_copy(rbc_sb[:], prb[:, :]))
                    self.on("vector")(nb_v)
                    self.barrier(["tensor", "scalar", "vector"])

                rhsga = lambda dc, tb: x2b2[:, dc * TOK + tb * 512: dc * TOK + tb * 512 + 512]

                def ga_epi_a2(a, oc, tb, psl):
                    return a.activation(gt3[(oc * 2 + tb) % 2][:], psl,
                                        ACTF.Sigmoid)

                def ga_epi_v2(v, oc, tb, i):
                    dst = ga_sb[:, oc * TOK + tb * 512: oc * TOK + tb * 512 + 512]
                    at = attnr[:, oc * TOK + tb * 512: oc * TOK + tb * 512 + 512]
                    v.tensor_tensor(dst, at, rbc_sb[:, tb * 512:(tb + 1) * 512],
                                    ALU.mult)
                    v.tensor_scalar_mul(dst, dst, nwpp[:, oc:oc + 1])
                    return v.tensor_tensor(dst, dst, gt3[i % 2][:], ALU.mult)

                self.proj_fm("g1", self.inp["gwT1"], D, D, 16, rhsga, 2, 512,
                             ga_epi_a2, ga_epi_v2)

                xres = [ph3.enter_context(
                    nc.sbuf_tensor(f"xres{i}", [128, 512], F32))
                    for i in range(2)]
                xo_ = [self.DS[2], self.DS[3]]
                xob = [xo_[0].n, xo_[1].n]
                vs_o0 = self.VS.n

                def o_sync(sync):
                    k = 0
                    for pi in range(4):
                        for oci in range(4):
                            for tb in range(2):
                                oc = pi * 4 + oci
                                if k >= 2:
                                    sync.wait_ge(VS.h, vs_o0 + k - 1)
                                xo_[k % 2].inc(sync.dma_start(
                                    out=xres[k % 2][:],
                                    in_=bass.AP(self.xr2,
                                                oc * 128 * TOK + tb * 512,
                                                [[TOK, 128], [1, 512]])), 16)
                                k += 1

                def o_epi_a(a, oc, tb, psl):
                    return a.activation(gt3[(oc * 2 + tb) % 2][:], psl, ACTF.Copy)

                def o_epi_v(v, oc, tb, i):
                    v.wait_ge(xo_[i % 2].h, xob[i % 2] + 16 * (i // 2 + 1))
                    dst = x3_sb[:, oc * TOK + tb * 512: oc * TOK + tb * 512 + 512]
                    return v.tensor_tensor(dst, gt3[i % 2][:], xres[i % 2][:],
                                           ALU.add)

                self.on("sync")(o_sync)
                self.proj_fm("o1", self.inp["owT1"], D, D, 16,
                             lambda dc, tb: ga_sb[:, dc * TOK + tb * 512: dc * TOK + tb * 512 + 512],
                             2, 512, o_epi_a, o_epi_v)

                self.emit_ln(ph3, x3_sb, TOK, None, None, spill=self.xr3)
                self.barrier()

    def phase_e(self):
        nc, P = self.nc, self.P
        TS, AS, VS, GD, LD = self.TS, self.AS, self.VS, self.GD, self.LD
        for hf in range(2):
            with ExitStack() as ph:
                sbE = lambda n, s, d: ph.enter_context(nc.sbuf_tensor(n, s, d))
                xb = sbE(f"xbe{hf}", [128, NCH * 512], BF16)
                gate = sbE(f"gate{hf}", [128, OCH1 * 512], BF16)
                up = sbE(f"up{hf}", [128, OCH1 * 512], BF16)
                x4 = sbE(f"x4_{hf}", [128, NCH * 512], F32)
                tmpE = [sbE(f"tmpE{hf}{i}", [128, 512], F32) for i in range(2)]

                def e_ld(gp):
                    GD.inc(gp.dma_start(
                        out=xb[:].rearrange("p (c t) -> p c t", c=NCH),
                        in_=bass.AP(self.xr3, hf * 512,
                                    [[TOK, 128], [128 * TOK, NCH], [1, 512]])), 16)
                self.on("gpsimd")(e_ld)
                self.barrier(["gpsimd", "tensor", "scalar", "vector", "sync"])

                rhsE = lambda dc, tb: xb[:, dc * 512:(dc + 1) * 512]

                with nc.psum_tensor(f"pxs{hf}", [128, 512], F32) as pxs:
                    def xs_t(t):
                        ins = None
                        for dc in range(NCH):
                            ins = t.matmul(pxs[0:1, :], P["onesbfc"][:], rhsE(dc, 0),
                                           start=(dc == 0), stop=(dc == NCH - 1))
                        TS.inc(ins)
                    self.on("tensor")(xs_t)

                    def xs_v(v):
                        v.wait_ge(TS.h, TS.n)
                        VS.inc(v.tensor_copy(P["xsum"][0:1, 0:512], pxs[0:1, :]))
                    self.on("vector")(xs_v)
                    self.barrier(["tensor", "scalar", "vector"])

                def ge_epi(a, oc, tb, psl):
                    return a.activation(gate[:, oc * 512:(oc + 1) * 512], psl,
                                        ACTF.Silu, scale=P["sg"][:, oc:oc + 1])

                def up_epi(a, oc, tb, psl):
                    return a.mul(up[:, oc * 512:(oc + 1) * 512], psl,
                                 P["su"][:, oc:oc + 1])

                corr = lambda tb: P["xsum"][0:1, 0:512]
                self.proj_fm(f"eg{hf}", self.inp["wg_u8"], D, HID1, OCH1,
                             rhsE, 1, 512, ge_epi, corr_row=corr)
                self.proj_fm(f"eu{hf}", self.inp["wu_u8"], D, HID1, OCH1,
                             rhsE, 1, 512, up_epi, corr_row=corr)

                vs0 = VS.n

                def gu_v(v):
                    for oc in range(OCH1):
                        VS.inc(v.tensor_tensor(gate[:, oc * 512:(oc + 1) * 512],
                                               gate[:, oc * 512:(oc + 1) * 512],
                                               up[:, oc * 512:(oc + 1) * 512],
                                               ALU.mult))
                self.on("vector")(gu_v)
                self.barrier(["vector", "tensor", "scalar"])

                with nc.psum_tensor(f"pgs{hf}", [128, 512], F32) as pgs:
                    def gs_t(t):
                        ins = None
                        for hc in range(OCH1):
                            ins = t.matmul(pgs[0:1, :], P["onesbfc"][:],
                                           gate[:, hc * 512:(hc + 1) * 512],
                                           start=(hc == 0), stop=(hc == OCH1 - 1))
                        TS.inc(ins)
                    self.on("tensor")(gs_t)

                    def gs_v(v):
                        v.wait_ge(TS.h, TS.n)
                        VS.inc(v.tensor_copy(P["xsum"][0:1, 512:1024], pgs[0:1, :]))
                    self.on("vector")(gs_v)
                    self.barrier(["tensor", "scalar", "vector"])

                xe_ = [self.DS[2], self.DS[3]]
                xeb = [xe_[0].n, xe_[1].n]
                xres_e = [sbE(f"xre{hf}{i}", [128, 512], F32) for i in range(2)]

                def dn_epi_a(a, oc, tb, psl):
                    return a.activation(tmpE[oc % 2][:], psl, ACTF.Copy)

                vs_e0 = self.VS.n

                def dn_epi_v(v, oc, tb, i):
                    v.wait_ge(xe_[i % 2].h, xeb[i % 2] + 16 * (i // 2 + 1))
                    dst = x4[:, oc * 512:(oc + 1) * 512]
                    return v.scalar_tensor_tensor(dst, tmpE[i % 2][:],
                                                  P["sd"][:, oc:oc + 1],
                                                  xres_e[i % 2][:],
                                                  ALU.mult, ALU.add)

                def d_sync2(sync):
                    for k in range(16):
                        if k >= 2:
                            sync.wait_ge(VS.h, vs_e0 + k - 1)
                        xe_[k % 2].inc(sync.dma_start(
                            out=xres_e[k % 2][:],
                            in_=bass.AP(self.xr3, k * 128 * TOK + hf * 512,
                                        [[TOK, 128], [1, 512]])), 16)
                self.on("sync")(d_sync2)
                corr2 = lambda tb: P["xsum"][0:1, 512:1024]
                self.proj_fm(f"ed{hf}", self.inp["wd_u8"], HID1, D, 16,
                             lambda dc, tb: gate[:, dc * 512:(dc + 1) * 512],
                             1, 512, dn_epi_a, dn_epi_v, corr_row=corr2)

                self.emit_ln(ph, x4, 512, None, None)
                def e_out(sync):
                    LD.inc(sync.dma_start(
                        out=bass.AP(self.y, hf * 512, [[TOK, 128], [128 * TOK, NCH], [1, 512]]),
                        in_=x4[:].rearrange("p (c t) -> p c t", c=NCH)), 16)
                self.on("sync")(e_out)
                self.barrier()

        def fin(sync):
            sync.wait_ge(self.LD.h, self.LD.n)
        self.on("sync")(fin)


_BUILT = {}


def _host_prep(inputs):
    x = np.asarray(inputs["L0_attn_x"], dtype=np.float32)
    asc = np.ascontiguousarray
    qd32 = _QD.astype(np.float32)
    kd32 = _KD.astype(np.float32)
    dd32 = _DD.astype(np.float32)

    mask = np.zeros((4, 128, 512), np.float32)
    for o in range(4):
        ki = 128 * o + np.arange(128)[:, None]
        qj = np.arange(512)[None, :]
        mask[o] = (ki <= qj).astype(np.float32)

    def pp(vec, n):
        return asc(np.asarray(vec, np.float32).reshape(n, 128).T)

    qkv_w = np.asarray(inputs["L1_attn_qkv_w"], np.float32).reshape(H1, 3, DH1, D)
    qwT = asc(qkv_w[:, 0].reshape(H1 * DH1, D).T)
    kwT = asc(qkv_w[:, 1].reshape(H1 * DH1, D).T)
    vwT = asc(qkv_w[:, 2].reshape(H1 * DH1, D).T)
    gwT1 = asc(np.asarray(inputs["L1_attn_gate_w"], np.float32).T)
    owT1 = asc(np.asarray(inputs["L1_attn_out_w"], np.float32).T)

    u8 = lambda a: asc(np.asarray(a).view(np.uint8)[:, ::4])
    wg8 = asc(u8(inputs["L1_ffn_wq_gate"]).T)
    wu8 = asc(u8(inputs["L1_ffn_wq_up"]).T)
    wd8 = asc(u8(inputs["L1_ffn_wq_down"]).T)

    pair = lambda k: np.ascontiguousarray(
        np.tile(np.asarray(inputs[k], np.float32).reshape(1, D), (2, 1)))
    png2 = pair("L0_ffn_pn_gamma")
    pnb2 = pair("L0_ffn_pn_beta")
    d2b2 = pair("L0_ffn_d2_b")
    aggb2 = pair("L0_ffn_agg_b")
    bcb2 = pair("L0_ffn_bc_b")

    x_fm = [asc(x[b].T) for b in range(B)]
    wq = np.asarray(inputs["L0_attn_wq"], np.float32)
    wk = np.asarray(inputs["L0_attn_wk"], np.float32)
    wv = np.asarray(inputs["L0_attn_wv"], np.float32)
    bq = np.asarray(inputs["L0_attn_bq"], np.float32)
    bk = np.asarray(inputs["L0_attn_bk"], np.float32)
    bv = np.asarray(inputs["L0_attn_bv"], np.float32)

    in_maps = []
    for c in range(NCC):
        b, p = c // 4, c % 4
        g = p
        coeff = np.zeros((3, D), np.float32)
        for j in range(3):
            if j < p:
                for h in range(H1):
                    coeff[j, h * 64:(h + 1) * 64] = (_BD[h] ** 4) ** (p - 1 - j)
        m = {
            "x_fm": x_fm[b],
            "wq_own": asc(wq[:, 512 * g:512 * g + 512]),
            "bq_pp": pp(bq[512 * g:512 * g + 512], 4),
            "wk_own": asc(wk[:, 128 * g:128 * g + 128]),
            "bk_pp": pp(bk[128 * g:128 * g + 128], 1),
            "wv_own": asc(wv[:, 128 * g:128 * g + 128]),
            "bv_row": asc(bv[128 * g:128 * g + 128].reshape(1, 128)),
            "wo_w": np.asarray(inputs["L0_attn_wo"], np.float32),
            "bo_pp": pp(inputs["L0_attn_bo"], 16),
            "masks": mask,
            "ing_pp": pp(inputs["L0_ffn_in_gamma"], 16),
            "inb_pp": pp(inputs["L0_ffn_in_beta"], 16),
            "agg_w": np.asarray(inputs["L0_ffn_agg_w"], np.float32),
            "png2": png2, "pnb2": pnb2, "d2b2": d2b2,
            "aggb2": aggb2, "bcb2": bcb2,
            "d1s": asc(np.asarray(inputs["L0_ffn_d1_w"], np.float32)[:, 1024 * c:1024 * c + 1024]),
            "d1b_own": asc(np.asarray(inputs["L0_ffn_d1_b"], np.float32)[1024 * c:1024 * c + 1024].reshape(1, 1024)),
            "d2s": asc(np.asarray(inputs["L0_ffn_d2_w"], np.float32)[1024 * c:1024 * c + 1024, :]),
            "bc_w": np.asarray(inputs["L0_ffn_bc_w"], np.float32),
            "gw_l0": np.asarray(inputs["L0_ffn_gate_w"], np.float32),
            "gb_pp": pp(inputs["L0_ffn_gate_b"], 16),
            "outg_row": np.asarray(inputs["L0_ffn_out_gamma"], np.float32).reshape(1, D),
            "outb_row": np.asarray(inputs["L0_ffn_out_beta"], np.float32).reshape(1, D),
            "own_mask": np.full((1, 1), 1.0 if p == 3 else 0.0, np.float32),
            "qwT": qwT, "kwT": kwT, "vwT": vwT, "gwT1": gwT1, "owT1": owT1,
            "nw_row": np.asarray(inputs["L1_attn_norm_w"], np.float32).reshape(1, D),
            "nw_pp": pp(inputs["L1_attn_norm_w"], 16),
            "qd_rows": qd32.reshape(1, -1),
            "kd_rows": kd32.reshape(1, -1),
            "ddT": np.ascontiguousarray(np.transpose(dd32, (0, 2, 1))),
            "coeff_rows": coeff.reshape(1, -1),
            "wg_u8": wg8, "wu_u8": wu8, "wd_u8": wd8,
            "sg_pp": pp(inputs["L1_ffn_s_gate"], OCH1),
            "su_pp": pp(inputs["L1_ffn_s_up"], OCH1),
            "sd_pp": pp(inputs["L1_ffn_s_down"], 16),
        }
        in_maps.append(m)
    return in_maps


def kernel(**inputs):
    in_maps = _host_prep(inputs)
    kb = KB()
    nc = kb.build()
    res = run_bass_kernel_spmd(nc, in_maps, core_ids=list(range(NCC)))
    out = np.zeros((B, D, S), np.float32)
    for c in range(NCC):
        b, p = c // 4, c % 4
        out[b][:, 1024 * p:1024 * p + 1024] = res.results[c]["y"]
    return np.ascontiguousarray(out.transpose(0, 2, 1))
